# version 20
# baseline (speedup 1.0000x reference)
import numpy as np
import concourse.bass as bass
import concourse.mybir as mybir
from concourse.bass_utils import run_bass_kernel_spmd

F32 = mybir.dt.float32
BF16 = mybir.dt.bfloat16
F16 = mybir.dt.float16
F8 = mybir.dt.float8e4
AT = mybir.ActivationFunctionType
ALU = mybir.AluOpType
AX = mybir.AxisListType

MAGIC = 12582912.0
NEGBIG = -1.0e30
EPS = 1e-5

BC, L, D, DCH, N, M, CH = 2, 2048, 1024, 8, 64, 16, 16
HC, HCH, WIN, NW, P = 4096, 32, 512, 4, 128
ST = 514

_CONST_CACHE = {}
DEBUG = False
SERIAL = False
DBGWIN = (0, 1)
DBGPROBE = None
BLEVEL = 0
SLO, SHI = 0, 10 ** 9
SELFWAIT = True
RACECHECK = False


def host_consts():
    if _CONST_CACHE:
        return _CONST_CACHE
    c = {}
    j = np.arange(P)
    c["cJ127"] = np.broadcast_to((127.0 - j)[:, None], (P, N)).astype(np.float32).copy()
    c["cT1"] = np.broadcast_to((j + 1.0)[None, :], (N, P)).astype(np.float32).copy()
    c["cK16"] = np.broadcast_to(np.arange(M, dtype=np.float64)[None, :], (N, M)).astype(np.float32).copy()
    J2 = np.zeros((4, P, P), np.float32)
    M2 = np.full((4, P, P), NEGBIG, np.float32)
    for g in range(4):
        for jj in range(P):
            q, jl = jj // M, jj % M
            for e in range(2):
                if q == 2 * g + e:
                    J2[g, jj, e * N:(e + 1) * N] = 15.0 - jl
                    M2[g, jj, e * N:(e + 1) * N] = 0.0
    c["cJ2"], c["cM2"] = J2.reshape(4 * P, P).copy(), M2.reshape(4 * P, P).copy()
    T2 = np.zeros((4, P, P), np.float32)
    M2W = np.full((4, P, P), NEGBIG, np.float32)
    for pp in range(4):
        for e in range(2):
            qp = 2 * pp + e
            lo = M * (qp + 1)
            for t in range(lo, P):
                T2[pp, e * N:(e + 1) * N, t] = t - lo + 1.0
                M2W[pp, e * N:(e + 1) * N, t] = 0.0
    c["cT2"], c["cM2W"] = T2.reshape(4 * P, P).copy(), M2W.reshape(4 * P, P).copy()
    c["cI"] = np.eye(P, dtype=np.float32)
    c["cONE"] = np.ones((P, P), np.float32)
    _CONST_CACHE.update(c)
    return c


class Sched:

    def __init__(self):
        self.events = []
        self.cnt = {}
        self.pending = {}
        self.marks = []

    def mark(self, name):
        self.marks.append((name, len(self.events), dict(self.cnt)))

    def barrier(self, engines=("sy", "v", "s", "t", "g")):
        snap = [(s, v) for s, v in self.cnt.items() if v > 0]
        for eng in engines:
            self.pending[eng] = list(snap)

    def now(self, sem):
        return self.cnt.get(sem, 0)

    def ev(self, eng, fn, waits=(), inc=None, n=1, lag=0):
        if SERIAL == 1:
            w = [(s, v) for s, v in self.cnt.items() if v > 0]
        elif SERIAL == 2 and eng in ("v", "s", "t"):
            w = [(s, v) for s, v in self.cnt.items()
                 if v > 0 and s in ("v", "s", "t")]
            w += [(s, v) for (s, v) in waits
                  if s is not None and v > 0 and s not in ("v", "s", "t")]
        elif SERIAL == 7 and eng == "v":
            vi = self.cnt.get("v", 0)
            if SLO <= vi < SHI:
                w = [(s, v) for s, v in self.cnt.items() if v > 0 and s == "s"]
                w += [(s, v) for (s, v) in waits if s is not None and v > 0]
            else:
                w = [(s, v) for (s, v) in waits if s is not None and v > 0]
        elif SERIAL == 8 and eng == "s":
            si = self.cnt.get("s", 0)
            if SLO <= si < SHI:
                w = [(s, v) for s, v in self.cnt.items() if v > 0 and s == "v"]
                w += [(s, v) for (s, v) in waits if s is not None and v > 0]
            else:
                w = [(s, v) for (s, v) in waits if s is not None and v > 0]
        elif SERIAL == 9:
            if eng == "v":
                w = [(s, v) for s, v in self.cnt.items() if v > 0 and s == "s"]
                w += [(s, v) for (s, v) in waits if s is not None and v > 0]
            elif eng == "s":
                si = self.cnt.get("s", 0)
                w = list(waits)
                if SLO <= si < SHI:
                    w = [(s, v) for s, v in self.cnt.items() if v > 0 and s == "v"] + w
                w = [(s, v) for (s, v) in w if s is not None and v > 0]
            else:
                w = [(s, v) for (s, v) in waits if s is not None and v > 0]
        elif SERIAL == 10:
            if eng == "s":
                w = [(s, v) for s, v in self.cnt.items() if v > 0 and s == "v"]
                w += [(s, v) for (s, v) in waits if s is not None and v > 0]
            elif eng == "v":
                vi = self.cnt.get("v", 0)
                w = list(waits)
                if SLO <= vi < SHI:
                    w = [(s, v) for s, v in self.cnt.items() if v > 0 and s == "s"] + w
                w = [(s, v) for (s, v) in w if s is not None and v > 0]
            else:
                w = [(s, v) for (s, v) in waits if s is not None and v > 0]
        elif SERIAL in (4, 5, 6):
            pair = {4: ("v", "s"), 5: ("v", "t"), 6: ("s", "t")}[SERIAL]
            if eng in pair:
                w = [(s, v) for s, v in self.cnt.items() if v > 0 and s in pair]
                w += [(s, v) for (s, v) in waits
                      if s is not None and v > 0 and s not in pair]
            else:
                w = [(s, v) for (s, v) in waits if s is not None and v > 0]
        elif SERIAL == 3:
            w = [(s, v) for s, v in self.cnt.items()
                 if v > 0 and s not in ("v", "s", "t")]
            w += [(s, v) for (s, v) in waits if s is not None and v > 0]
        else:
            w = [(s, v) for (s, v) in waits if s is not None and v > 0]
        if eng in self.pending:
            w = self.pending.pop(eng) + w
        self.events.append((eng, w, fn, inc, n, lag))
        if inc is not None:
            self.cnt[inc] = self.cnt.get(inc, 0) + n

    def emit(self, eng_name, handle, sems, delay=False):
        own = 0
        for eng, waits, fn, inc, n, lag in self.events:
            if eng != eng_name:
                continue
            if SELFWAIT and eng_name in ("v", "s") and own - lag > 0:
                handle.wait_ge(sems[eng_name], own - lag)
            for s, v in waits:
                if s == eng_name and v <= own - lag:
                    continue
                handle.wait_ge(sems[s], v)
            inst = fn(handle)
            if inc == eng_name:
                own += n
            if inc is not None:
                inst.then_inc(sems[inc], n)


def build_program():
    nc = bass.Bass("TRN2", target_bir_lowering=False,
                   detect_race_conditions=RACECHECK)
    hc = host_consts()

    u = nc.dram_tensor("u", [BC, L, D], F32, kind="ExternalInput").ap()
    out = nc.dram_tensor("out", [BC, L, D], F32, kind="ExternalOutput").ap()
    w1T = nc.dram_tensor("w1T", [D, HC], F32, kind="ExternalInput").ap()
    w2T = nc.dram_tensor("w2T", [HC, D], F32, kind="ExternalInput").ap()
    CTd = nc.dram_tensor("CT", [N, D], F32, kind="ExternalInput").ap()
    alog = nc.dram_tensor("alog", [N], F32, kind="ExternalInput").ap()
    deltad = nc.dram_tensor("delta", [D], F32, kind="ExternalInput").ap()
    Dd = nc.dram_tensor("Dvec", [D], F32, kind="ExternalInput").ap()
    fad = nc.dram_tensor("fa", [D], F32, kind="ExternalInput").ap()
    gammad = nc.dram_tensor("gamma", [D], F32, kind="ExternalInput").ap()
    betad = nc.dram_tensor("beta", [D], F32, kind="ExternalInput").ap()
    rewardd = nc.dram_tensor("reward", [D], F32, kind="ExternalInput").ap()
    convwd = nc.dram_tensor("convw", [HC, 3], F32, kind="ExternalInput").ap()
    convbd = nc.dram_tensor("convb", [HC], F32, kind="ExternalInput").ap()
    cst = {k: nc.dram_tensor(k, list(v.shape), F32, kind="ExternalInput").ap()
           for k, v in hc.items()}
    dbg = {}
    if DEBUG:
        dbg["xh"] = nc.dram_tensor("dbg_xh", [L, D], BF16, kind="ExternalOutput").ap()
        dbg["y2p"] = nc.dram_tensor("dbg_y2p", [L, D], BF16, kind="ExternalOutput").ap()
        dbg["intra"] = nc.dram_tensor("dbg_intra", [L, D], F16, kind="ExternalOutput").ap()
        dbg["x1"] = nc.dram_tensor("dbg_x1", [BC * L, D], F16, kind="ExternalOutput").ap()
        dbg["xq1"] = nc.dram_tensor("dbg_xq1", [L, D], F16, kind="ExternalOutput").ap()
        dbg["hsl"] = nc.dram_tensor("dbg_hsl", [P, HCH * ST], BF16, kind="ExternalOutput").ap()
        dbg["acc2"] = nc.dram_tensor("dbg_acc2", [P, HCH * WIN], F16, kind="ExternalOutput").ap()
        dbg["scol"] = nc.dram_tensor("dbg_scol", [P, 8], F32, kind="ExternalOutput").ap()
        dbg["ffw"] = nc.dram_tensor("dbg_ffw", [P, DCH * WIN], BF16, kind="ExternalOutput").ap()
        dbg["kt"] = nc.dram_tensor("dbg_kt", [P, DCH * M], F32, kind="ExternalOutput").ap()
        dbg["mr"] = nc.dram_tensor("dbg_mr", [P, 8], F32, kind="ExternalOutput").ap()
        dbg["probe"] = nc.dram_tensor("dbg_probe", [P, WIN], F16, kind="ExternalOutput").ap()
    w1q8 = nc.dram_tensor("w1q8", [HCH, DCH, P, P], F8, kind="Internal").ap()
    w2q8 = nc.dram_tensor("w2q8", [DCH, HCH, P, P], F8, kind="Internal").ap()

    sch = Sched()
    keep = []
    SSM_BYTES = 51200
    FFN_BYTES = 87296
    ARENA_BYTES = SSM_BYTES + FFN_BYTES
    _off = {"arena_ssm": [0], "arena_ffn": [SSM_BYTES], "arena_setup": [SSM_BYTES]}
    _REG_END = {"arena_ssm": SSM_BYTES, "arena_ffn": ARENA_BYTES,
                "arena_setup": ARENA_BYTES}
    _arena = [None]

    def _align(x, a=64):
        return (x + a - 1) // a * a

    class TV:

        def __init__(self, ap):
            self._ap = ap

        def ap(self):
            return self._ap

        def __getitem__(self, sl):
            return self._ap[sl]

    def sb(name, shape, dtype, region="perm"):
        nbytes = int(np.prod(shape[1:])) * mybir.dt.size(dtype)
        if region == "perm":
            h = nc.alloc_sbuf_tensor(name, shape, dtype)
            keep.append(h)
            return h
        o = _off[region][0]
        _off[region][0] = _align(o + nbytes)
        assert _off[region][0] <= _REG_END.get(region, ARENA_BYTES), (
            name, region, _off[region][0])
        es = mybir.dt.size(dtype)
        assert o % es == 0
        view = _arena[0].bitcast(dtype).ap()[:, o // es: o // es + int(np.prod(shape[1:]))]
        if len(shape) == 2 and shape[0] < 128:
            view = view[0:shape[0], :]
        return TV(view)

    def start_arena():
        _arena[0] = nc.alloc_sbuf_tensor("arena", [P, ARENA_BYTES], mybir.dt.uint8)
        keep.append(_arena[0])

    bI = sb("bI", [P, P], F32)
    bONE = sb("bONE", [P, P], F32)
    bAlog = sb("bAlog", [N, 1], F32)
    bd0 = sb("bd0", [1, 1], F32)
    rrew = sb("rrew", [1, D], BF16)
    cD = sb("cD", [P, DCH], F32)
    cfa = sb("cfa", [P, DCH], F32)
    cDfa = sb("cDfa", [P, DCH], F32)
    wcv = sb("wcv", [P, HCH * 4], F32)
    dA = sb("dA", [N, 1], F32)
    dA2 = sb("dA2", [P, 1], F32)
    rS = sb("rS", [N, 1], F32)
    V1 = sb("V1", [P, N], BF16)
    W1Ts = sb("W1Ts", [N, P], BF16)
    V2 = sb("V2", [P, 4 * P], BF16)
    W2s = sb("W2s", [P, 4 * P], BF16)
    P2m = sb("P2m", [N, M], BF16)
    CtT = sb("CtT", [N, D], BF16)
    Ct2 = sb("Ct2", [P, D], BF16)
    KtT = sb("KtT", [P, DCH * M], F32)
    gRep = sb("gRep", [P, D], BF16)
    bRep = sb("bRep", [P, D], BF16)
    mcols = sb("mcols", [P, 48], F32)
    msum = sb("msum", [P, 4], F32)
    mreps = sb("mreps", [P, 8], F32)
    stat6 = sb("stat6", [P, 12], F32)
    stat2 = sb("stat2", [P, 2], F32)
    colm = sb("colm", [P, 8], F32)
    ceps = sb("ceps", [P, 1], F32)
    c128 = sb("c128", [P, 1], F32)

    NX1, NQT = 12, 8
    x1 = sb("x1", [P, NX1 * D], F16)
    xq1T = sb("xq1T", [P, NQT * D], BF16)
    w1r = sb("w1r", [P, 24 * P], F8)
    w2r = sb("w2r", [P, 48 * P], F8)
    bIb = sb("bIb", [P, P], BF16)
    outt = sb("outt", [P, 2 * D], F32)
    start_arena()
    bJ127 = sb("bJ127", [P, N], F32, "arena_setup")
    bT1 = sb("bT1", [N, P], F32, "arena_setup")
    bK16 = sb("bK16", [N, M], F32, "arena_setup")
    bJ2 = sb("bJ2", [P, 4 * P], F32, "arena_setup")
    bM2 = sb("bM2", [P, 4 * P], F32, "arena_setup")
    bT2 = sb("bT2", [P, 4 * P], F32, "arena_setup")
    bM2W = sb("bM2W", [P, 4 * P], F32, "arena_setup")
    bCT = sb("bCT", [N, D], F32, "arena_setup")
    rfa = sb("rfa", [1, D], F32, "arena_setup")
    rgam = sb("rgam", [1, D], F32, "arena_setup")
    rbet = sb("rbet", [1, D], F32, "arena_setup")
    tmpc = sb("tmpc", [P, P], F32, "arena_setup")
    tmpd = sb("tmpd", [N, P], F32, "arena_setup")
    wbuf = sb("wbuf", [P, 2 * 2048], F32, "arena_setup")
    wqb = sb("wqb", [P, 2 * 2048], F8, "arena_setup")
    wdump = sb("wdump", [P, 2 * 2048], BF16, "arena_setup")
    h1 = sb("h1", [N, D], F32, "arena_ssm")
    hC1 = sb("hC1", [N, D], BF16, "arena_ssm")
    y2t = sb("y2t", [P, D], BF16, "arena_ssm")

    xq1t = sb("xq1t", [P, D], F16, "arena_ssm")
    xh = sb("xh", [P, 2 * D], BF16, "arena_ssm")
    intraT = sb("intraT", [P, D], F16, "arena_ssm")
    tmpa = sb("tmpa", [P, D], BF16, "arena_ssm")
    _tmpb_off = _off["arena_ssm"][0] - 2048
    tmpb = TV(_arena[0].bitcast(F16).ap()[:, _tmpb_off // 2:_tmpb_off // 2 + D])
    xhT = sb("xhT", [P, DCH * 512], BF16, "arena_ssm")
    accb = sb("accb", [P, DCH * 512], F16, "arena_ssm")
    y2p = sb("y2p", [P, 4 * D], BF16, "arena_ssm")
    G2C = sb("G2C", [P, 4 * D], BF16, "arena_ssm")
    hsl = sb("hsl", [P, HCH * ST], BF16, "arena_ffn")
    ubuf = sb("ubuf", [P, 2 * 512], BF16, "arena_ffn")
    vbuf = sb("vbuf", [P, 2 * 512], BF16, "arena_ffn")
    cbuf = sb("cbuf", [P, 2 * 512], BF16, "arena_ffn")
    acc2 = sb("acc2", [P, HCH * WIN], F16, "arena_ffn")
    maxt = sb("maxt", [P, WIN], F16, "arena_ffn")
    mint = sb("mint", [P, WIN], F16, "arena_ffn")
    amT = maxt
    s2col = sb("s2col", [P, 4], F32, "arena_ffn")
    s2cb = sb("s2cb", [P, 4], F32, "arena_ffn")
    s2rep = sb("s2rep", [P, WIN], BF16, "arena_ffn")
    ffw = sb("ffw", [P, 2 * WIN], BF16, "arena_ffn")
    srow = sb("srow", [1, WIN], F32, "arena_ffn")
    ffT = sb("ffT", [P, 4 * D], BF16, "arena_ffn")

    psA = nc.alloc_psum_tensor("psA", [P, 1024], F32)
    psB = nc.alloc_psum_tensor("psB", [P, 1024], F32)
    psM1a = nc.alloc_psum_tensor("psM1a", [P, 512], F32)
    psM1b = nc.alloc_psum_tensor("psM1b", [P, 512], F32)
    psM2a = nc.alloc_psum_tensor("psM2a", [P, 512], F32)
    psM2b = nc.alloc_psum_tensor("psM2b", [P, 512], F32)

    pg = {"A": ("v", 0), "B": ("v", 0), "M1a": ("s", 0), "M1b": ("s", 0),
          "M2a": ("s", 0), "M2b": ("s", 0)}
    xh_g = [[], []]
    intraT_g = [("v", 0), ("v", 0)]
    G2C_g = [("t", 0), ("t", 0)]
    y2p_g = [("v", 0)] * 8
    xq1t_g = ("sq", 0)
    amb_g = ("sq", 0)
    ffw_g = [("v", 0), ("v", 0)]
    ffT_g = ("v", 0)
    outt_g = [("ou", 0), ("ou", 0)]
    w1r_g = [("t", 0), ("t", 0), ("t", 0)]
    w2r_g = [("t", 0), ("t", 0), ("t", 0)]
    conv_g = [("v", 0)] * HCH
    acc2_g = ("t", 0)
    xstore_g = [("ou", 0)] * CH
    xstage_g = [("v", 0), ("v", 0)]
    tmpa_g = ("v", 0)
    tmpb_g = ("sq", 0)
    y2s = [0] * 8
    s2t_g = ("t", 0)
    ubuf_g = [("t", 0), ("t", 0)]
    pg_evac = [("s", 0)]

    def W(*pairs):
        return list(pairs)

    def vev(fn, waits=(), lag=0):
        sch.ev("v", fn, waits, inc="v", lag=lag)
        return sch.now("v")

    def sev(fn, waits=(), lag=0):
        sch.ev("s", fn, waits, inc="s", lag=lag)
        return sch.now("s")

    def tev(fn, waits=()):
        sch.ev("t", fn, waits, inc="t")
        return sch.now("t")

    def dma(dst, src, slow=False):
        if slow:
            return lambda e: e.dma_start(out=dst, in_=src,
                                         allow_slow_non_contiguous=True)
        return lambda e: e.dma_start(out=dst, in_=src)

    for dst, src in [
        (bJ127[:], cst["cJ127"]), (bT1[:], cst["cT1"]), (bK16[:], cst["cK16"]),
        (bI[:], cst["cI"]), (bONE[:], cst["cONE"]),
        (bAlog[:], alog.rearrange("(p o) -> p o", o=1)),
        (bd0[:], deltad[0:1].rearrange("(p o) -> p o", o=1)),
        (bCT[:], CTd),
        (rfa[:], fad.rearrange("(o d) -> o d", o=1)),
        (rgam[:], gammad.rearrange("(o d) -> o d", o=1)),
        (rbet[:], betad.rearrange("(o d) -> o d", o=1)),

        (cD[:], Dd.rearrange("(c p) -> p c", p=P)),
        (cfa[:], fad.rearrange("(c p) -> p c", p=P)),
        (wcv.ap().rearrange("p (c k) -> p c k", k=4)[:, :, 0:3],
         convwd.rearrange("(c p) k -> p c k", p=P)),
        (wcv.ap().rearrange("p (c k) -> p c k", k=4)[:, :, 3:4],
         convbd.rearrange("(c p) -> p c", p=P).rearrange("p c -> p c 1") if False else
         convbd.rearrange("(c p o) -> p c o", p=P, o=1)),
    ]:
        sch.ev("sy", dma(dst, src, slow=True), inc="s_ld", n=16)
    for g in range(4):
        for bdst, key in [(bJ2, "cJ2"), (bM2, "cM2"), (bT2, "cT2"), (bM2W, "cM2W")]:
            sch.ev("sy", dma(bdst[:, g * P:(g + 1) * P], cst[key][g * P:(g + 1) * P, :]),
                   inc="s_ld", n=16)
    sch.ev("g", dma(rrew[:], rewardd.rearrange("(o d) -> o d", o=1)), inc="s_rr", n=16)
    LD0 = sch.now("s_ld")

    vev(lambda e: e.tensor_copy(bIb[:], bI[:]), W(("s_ld", LD0)))
    vev(lambda e: e.memset(ceps[:], EPS))
    vev(lambda e: e.memset(c128[:], 128.0))
    sev(lambda e: e.activation(colm[0:N, 0:1], bAlog[:], AT.Exp), W(("s_ld", LD0)))
    tev(lambda e: e.matmul(psA[0:N, 0:1], bONE[0:1, 0:N], bd0[:], start=True, stop=True),
        W(("s_ld", LD0)))
    vev(lambda e: e.tensor_copy(colm[0:N, 1:2], psA[0:N, 0:1]), W(("t", sch.now("t"))))
    vev(lambda e: e.tensor_scalar(dA[:], colm[0:N, 0:1], colm[0:N, 1:2], -1.0,
                                  ALU.mult, ALU.mult), W(("s", sch.now("s"))))
    VDA = sch.now("v")
    pg["A"] = ("v", VDA)
    sev(lambda e: e.activation(rS[:], dA[:], AT.Exp, scale=c128[0:N, 0:1]), W(("v", VDA)))
    vev(lambda e: e.tensor_copy(dA2[0:N, :], dA[:]))
    vev(lambda e: e.tensor_copy(dA2[N:P, :], dA[:]))
    VDA2 = sch.now("v")
    tev(lambda e: e.transpose(psB[0:1, 0:N], dA[:], bI[0:N, 0:N]), W(("v", VDA)))
    vev(lambda e: e.tensor_copy(srow[0:1, 0:N], psB[0:1, 0:N]), W(("t", sch.now("t"))))
    vev(lambda e: e.tensor_copy(srow[0:1, N:2 * N], srow[0:1, 0:N]))
    VROW = sch.now("v")
    pg["B"] = ("v", VROW)
    tev(lambda e: e.matmul(psA[:, 0:N], bONE[0:1, :], srow[0:1, 0:N], start=True, stop=True),
        W(("v", VROW), pg["A"]))
    tev(lambda e: e.matmul(psB[:, 0:2 * N], bONE[0:1, :], srow[0:1, 0:2 * N],
                           start=True, stop=True), W(pg["B"]))
    TGRID = sch.now("t")
    vev(lambda e: e.tensor_mul(tmpc[:, 0:N], bJ127[:], psA[:, 0:N]), W(("t", TGRID)))
    sev(lambda e: e.activation(V1[:], tmpc[:, 0:N], AT.Exp), W(("v", sch.now("v"))))
    for g in range(4):
        vev(lambda e, g=g: e.tensor_mul(tmpc[:], bJ2[:, g * P:(g + 1) * P], psB[:, 0:P]),
            W(("t", TGRID), ("s", sch.now("s"))))
        vev(lambda e, g=g: e.tensor_add(tmpc[:], tmpc[:], bM2[:, g * P:(g + 1) * P]))
        sev(lambda e, g=g: e.activation(V2[:, g * P:(g + 1) * P], tmpc[:], AT.Exp),
            W(("v", sch.now("v"))))
    for g in range(4):
        vev(lambda e, g=g: e.tensor_scalar(tmpc[:], bT2[:, g * P:(g + 1) * P], dA2[:, 0:1],
                                           None, ALU.mult),
            W(("v", VDA2), ("s", sch.now("s"))))
        vev(lambda e, g=g: e.tensor_add(tmpc[:], tmpc[:], bM2W[:, g * P:(g + 1) * P]))
        sev(lambda e, g=g: e.activation(W2s[:, g * P:(g + 1) * P], tmpc[:], AT.Exp),
            W(("v", sch.now("v"))))
    vev(lambda e: e.tensor_scalar(tmpd[:], bT1[:], dA[:, 0:1], None, ALU.mult),
        W(("s", sch.now("s"))))
    sev(lambda e: e.activation(W1Ts[:], tmpd[:], AT.Exp), W(("v", sch.now("v"))))
    vev(lambda e: e.tensor_scalar(tmpd[:, 0:M], bK16[:], dA[:, 0:1], None, ALU.mult),
        W(("s", sch.now("s"))))
    sev(lambda e: e.activation(P2m[:], tmpd[0:N, 0:M], AT.Exp), W(("v", sch.now("v"))))
    for h in range(2):
        tev(lambda e, h=h: e.matmul(psA[0:N, h * 512:(h + 1) * 512], bONE[0:1, 0:N],
                                    rfa[:, h * 512:(h + 1) * 512], start=True, stop=True),
            W(("v", sch.now("v"))) if h == 0 else W())
    vev(lambda e: e.tensor_mul(CtT[:], bCT[:], psA[0:N, :]), W(("t", sch.now("t"))))
    vev(lambda e: e.tensor_copy(Ct2[0:N, :], CtT[:]))
    vev(lambda e: e.tensor_copy(Ct2[N:P, :], CtT[:]))
    VCT = sch.now("v")
    pg["A"] = ("v", VCT)
    for h in range(2):
        tev(lambda e, h=h: e.matmul(psB[:, h * 512:(h + 1) * 512], bONE[0:1, :],
                                    rgam[:, h * 512:(h + 1) * 512], start=True, stop=True),
            W(("v", VCT)) if h == 0 else W())
    vev(lambda e: e.tensor_copy(gRep[:], psB[:]), W(("t", sch.now("t"))))
    for h in range(2):
        tev(lambda e, h=h: e.matmul(psB[:, h * 512:(h + 1) * 512], bONE[0:1, :],
                                    rbet[:, h * 512:(h + 1) * 512], start=True, stop=True),
            W(("v", sch.now("v"))) if h == 0 else W())
    vev(lambda e: e.tensor_copy(bRep[:], psB[:]), W(("t", sch.now("t"))))
    pg["B"] = ("v", sch.now("v"))
    vev(lambda e: e.tensor_mul(cDfa[:], cD[:], cfa[:]), W(("s_ld", LD0)))
    for dc in range(DCH):
        tev(lambda e, dc=dc: e.matmul(psA[:, dc * M:(dc + 1) * M], CtT[:, dc * P:(dc + 1) * P],
                                      P2m[:], start=True, stop=True),
            W(("s", sch.now("s")), pg["A"]) if dc == 0 else W())
    vev(lambda e: e.tensor_copy(KtT[:], psA[:, 0:DCH * M]), W(("t", sch.now("t"))))
    vev(lambda e: e.tensor_tensor(
        KtT.ap().rearrange("p (c k) -> p c k", k=M)[:, :, 0:1],
        KtT.ap().rearrange("p (c k) -> p c k", k=M)[:, :, 0:1],
        cDfa.ap().rearrange("p (c o) -> p c o", o=1), ALU.add))
    pg["A"] = ("v", sch.now("v"))

    w1s = w1T.rearrange("(a p) h -> a p h", p=P)
    w2s = w2T.rearrange("(a p) h -> a p h", p=P)
    tiles = [(0, a, cb) for a in range(DCH) for cb in range(2)] + \
            [(1, a, 0) for a in range(HCH)]
    tw = {0: 2048, 1: 1024}
    svals = []
    for ti, (wi, a, cb) in enumerate(tiles):
        slot = ti % 2
        wd = tw[wi]
        src = w1s[a, :, cb * 2048:(cb + 1) * 2048] if wi == 0 else w2s[a, :, :]
        guard = svals[ti - 2] if ti >= 2 else 0
        sch.ev("sy", dma(wbuf[:, slot * 2048:slot * 2048 + wd], src),
               W(("s", guard)), inc="s_pl%d" % slot, n=16)
        mi = ti
        sdone = sev(lambda e, slot=slot, wd=wd, mi=mi: e.activation(
            wdump[:, 0:wd], wbuf[:, slot * 2048:slot * 2048 + wd], AT.Abs,
            accum_out=mcols[:, mi:mi + 1]), W(("s_pl%d" % slot, sch.now("s_pl%d" % slot))))
        svals.append(sdone)
    SP1 = sch.now("s")
    for wi, (c0, c1) in [(0, (0, 16)), (1, (16, 48))]:
        vev(lambda e, wi=wi, c0=c0, c1=c1: e.reduce_sum(
            msum[:, wi:wi + 1], mcols[:, c0:c1], AX.X), W(("s", SP1)))
        tev(lambda e, wi=wi: e.matmul(psA[0:1, wi:wi + 1], msum[:, wi:wi + 1],
                                      bONE[:, 0:1], start=True, stop=True),
            W(("v", sch.now("v")), pg["A"]) )
        vev(lambda e, wi=wi: e.tensor_scalar(colm[0:1, 4 + wi:5 + wi], psA[0:1, wi:wi + 1],
                                             1.0 / (HC * D), 1e-5, ALU.mult, ALU.max),
            W(("t", sch.now("t"))))
        tev(lambda e, wi=wi: e.matmul(psA[0:P, 8 + wi:9 + wi] if False else psA[:, 8 + wi:9 + wi],
                                      bONE[0:1, :], colm[0:1, 4 + wi:5 + wi],
                                      start=True, stop=True), W(("v", sch.now("v"))))
        vev(lambda e, wi=wi: e.tensor_copy(mreps[:, 4 + wi:5 + wi], psA[:, 8 + wi:9 + wi]),
            W(("t", sch.now("t"))))
        vev(lambda e, wi=wi: e.reciprocal(mreps[:, wi:wi + 1], mreps[:, 4 + wi:5 + wi]))
        vev(lambda e, wi=wi: e.tensor_scalar(mreps[:, 2 + wi:3 + wi], mreps[:, 4 + wi:5 + wi],
                                             1.0 / 127.0, None, ALU.mult))
    VP1 = sch.now("v")
    pg["A"] = ("v", VP1)

    vev(lambda e: e.memset(mreps[:, 6:7], 192.0), W(("v", VP1)))
    vev(lambda e: e.memset(mreps[:, 7:8], -192.0))
    VMS = sch.now("v")
    vq_last = [VMS, VMS]
    sact1_last = [0, 0]
    for ti, (wi, a, cb) in enumerate(tiles):
        slot = ti % 2
        wd = tw[wi]
        src = w1s[a, :, cb * 2048:(cb + 1) * 2048] if wi == 0 else w2s[a, :, :]
        guard_s = sact1_last[slot]
        sch.ev("sy", dma(wbuf[:, slot * 2048:slot * 2048 + wd], src),
               W(("v", VMS), ("s", guard_s)), inc="s_pl%d" % slot, n=16)
        SYD = sch.now("s_pl%d" % slot)
        sev(lambda e, slot=slot, wd=wd, wi=wi: e.activation(
            wdump[:, slot * 2048:slot * 2048 + wd],
            wbuf[:, slot * 2048:slot * 2048 + wd], AT.Identity,
            bias=mreps[:, 6:7], scale=mreps[:, wi:wi + 1]),
            W(("s_pl%d" % slot, SYD), ("v", vq_last[slot])), lag=1)
        sact1_last[slot] = sch.now("s")
        SACT2 = sev(lambda e, slot=slot, wd=wd: e.activation(
            wdump[:, slot * 2048:slot * 2048 + wd],
            wdump[:, slot * 2048:slot * 2048 + wd], AT.Identity,
            bias=mreps[:, 7:8]))
        vq = vev(lambda e, slot=slot, wd=wd: e.tensor_scalar(
            wqb[:, slot * 2048:slot * 2048 + wd],
            wdump[:, slot * 2048:slot * 2048 + wd],
            -1.0, 1.0, ALU.max, ALU.min),
            W(("s", SACT2), ("s_ps%d" % slot, sch.now("s_ps%d" % slot))))
        vq_last[slot] = vq
        if wi == 0:
            dst = w1q8[16 * cb:16 * cb + 16, a].rearrange("a p c -> p a c")
        else:
            dst = w2q8[:, a].rearrange("b p c -> p b c")
        srcq = wqb.ap()[:, slot * 2048:slot * 2048 + wd].rearrange(
            "p (a c) -> p a c", c=P)
        sch.ev("g", dma(dst, srcq), W(("v", vq)), inc="s_ps%d" % slot, n=16)
    GPREP = sch.now("s_ps0") + sch.now("s_ps1")
    VPREP = sch.now("v")
    if DEBUG and False:
        sch.ev("g", dma(dbg["kt"][:, :], KtT[:]), W(("v", VPREP)), inc="gq", n=16)
        sch.ev("g", dma(dbg["mr"][:, :], mreps[:]), W(("v", VPREP)), inc="gq", n=16)

    store_cnt = [0]

    def dbgev(key, dst_slice, src_ap, waits):
        if DEBUG:
            sch.ev("g", dma(dst_slice, src_ap), waits, inc="gq", n=16)
            return ("gq", sch.now("gq"))
        return ("gq", 0)

    NX1S = ["s_x%d" % i for i in range(NX1)]
    NQTS = ["s_tq%d" % i for i in range(NQT)]
    x1_g = [[("v", 0)]] * NX1
    xq1_g = [("t", 0)] * NQT
    xq1_done = {}
    xhT_g = [("v", 0)]
    h1_g = [("t", 0)]
    PREPW = [("s_ps0", sch.now("s_ps0")), ("s_ps1", sch.now("s_ps1"))]
    first_ring = [True]
    tails_cnt = [0]
    probe_hits = [0]

    def stage_A(s, ci):
        nonlocal tmpa_g
        t0 = ci * P
        slot = ci % 2
        qc = ci % 4
        gi = s * CH + ci
        xs = gi % NX1
        prev = gi - NX1
        xw = [("s_o%d" % (prev % 2), 16 * (prev // 2 + 1))] if prev >= 0 else []
        sch.ev("g", dma(x1[:, xs * D:(xs + 1) * D], u[s, t0:t0 + P, :]),
               W(*xw), inc=NX1S[xs], n=16)
        SYX = sch.now(NX1S[xs])
        vev(lambda e, xs=xs: e.bn_stats(stat6[:, 0:6], x1[:, xs * D:xs * D + 512]),
            W((NX1S[xs], SYX)))
        vev(lambda e, xs=xs: e.bn_stats(stat6[:, 6:12],
                                        x1[:, xs * D + 512:(xs + 1) * D]), lag=1)
        vev(lambda e: e.bn_aggr(stat2[:], stat6[:]))
        sev(lambda e: e.activation(colm[:, 1:2], stat2[:, 1:2], AT.Sqrt,
                                   bias=ceps[:, 0:1]), W(("v", sch.now("v"))), lag=1)
        vev(lambda e: e.reciprocal(colm[:, 1:2], colm[:, 1:2]), W(("s", sch.now("s"))))
        vxh = vev(lambda e, xs=xs, slot=slot: e.tensor_scalar(
            xh[:, slot * D:(slot + 1) * D], x1[:, xs * D:(xs + 1) * D],
            stat2[:, 0:1], colm[:, 1:2], ALU.subtract, ALU.mult), xh_g[slot])
        sch.ev("s", lambda e, slot=slot: e.dma_start_transpose(
            tmpa.ap().rearrange("p (c t) -> p c t", c=DCH),
            xh.ap()[:, slot * D:(slot + 1) * D].rearrange("p (c t) -> p c t", c=DCH)),
            W(("v", vxh), tmpa_g, tmpb_g), inc="s_txh", n=16)
        SQXH = sch.now("s_txh")
        vts = vev(lambda e, qc=qc: e.tensor_copy(
            xhT.ap().rearrange("p (dc t) -> p dc t", dc=DCH)[:, :, qc * P:(qc + 1) * P],
            tmpa.ap().rearrange("p (dc t) -> p dc t", dc=DCH)),
            W(("s_txh", SQXH), *xhT_g))
        tmpa_g = ("v", vts)
        for g in range(4):
            ps, nm = (psA, "A") if g % 2 == 0 else (psB, "B")
            tev(lambda e, g=g, ps=ps, slot=slot: e.matmul(
                ps[:, 0:512], V2[:, g * P:(g + 1) * P],
                xh[:, slot * D:slot * D + 512], start=True, stop=True),
                W(("v", vxh), pg[nm]))
            TG2 = tev(lambda e, g=g, ps=ps, slot=slot: e.matmul(
                ps[:, 512:1024], V2[:, g * P:(g + 1) * P],
                xh[:, slot * D + 512:(slot + 1) * D], start=True, stop=True))
            SG = sev(lambda e, g=g, ps=ps: e.copy(G2C[:, g * D:(g + 1) * D], ps[:]),
                     W(("t", TG2), G2C_g[0] if g == 0 else ("v", 0)), lag=3)
            pg[nm] = ("s", SG)
        SG2 = sch.now("s")
        for g in range(4):
            vev(lambda e, g=g: e.tensor_mul(
                G2C[:, g * D:(g + 1) * D], G2C[:, g * D:(g + 1) * D], Ct2[:]),
                W(("s", SG2)) if g == 0 else W(), lag=3)
        VG2C = sch.now("v")
        for hh in range(2):
            tev(lambda e, hh=hh, slot=slot: e.matmul(
                psA[0:N, hh * 512:(hh + 1) * 512], V1[:],
                xh[:, slot * D + hh * 512:slot * D + (hh + 1) * 512],
                start=True, stop=True), W(pg["A"]) if hh == 0 else W())
        TG1 = sch.now("t")
        xh_g[slot] = [("t", TG1), ("s_txh", SQXH)]
        if ci == 0:
            vev(lambda e: e.memset(h1[:], 0.0), W(h1_g[0]))
        vev(lambda e: e.tensor_mul(hC1[:], h1[:], CtT[:]), W(("t", TG1)))
        vev(lambda e: e.scalar_tensor_tensor(h1[:], h1[:], rS[:, 0:1], psA[0:N, :],
                                             ALU.mult, ALU.add))
        pg["A"] = ("v", sch.now("v"))
        VH = sch.now("v")
        for hh in range(2):
            tev(lambda e, hh=hh: e.matmul(
                psB[:, hh * 512:(hh + 1) * 512], W1Ts[:],
                hC1[:, hh * 512:(hh + 1) * 512], start=True, stop=False),
                W(("v", VH), pg["B"]) if hh == 0 else W())
        for g in range(4):
            for hh in range(2):
                tev(lambda e, g=g, hh=hh: e.matmul(
                    psB[:, hh * 512:(hh + 1) * 512], W2s[:, g * P:(g + 1) * P],
                    G2C[:, g * D + hh * 512:g * D + (hh + 1) * 512],
                    start=False, stop=(g == 3)),
                    W(("v", VG2C)) if (g == 0 and hh == 0) else W())
        TY = sch.now("t")
        G2C_g[0] = ("t", TY)
        h1_g[0] = ("t", TY)
        yg = y2p_g[qc] if isinstance(y2p_g[qc], list) else [y2p_g[qc]]
        SY2 = sev(lambda e, qc=qc: e.copy(y2p[:, qc * D:(qc + 1) * D], psB[:]),
                  W(("t", TY), *yg), lag=1)
        y2s[qc] = SY2
        pg["B"] = ("s", SY2)

    def taps(qtr_base_done):
        def tapap(buf, dc, k, hi):
            a = buf.ap()[:, dc * 512:(dc + 1) * 512].rearrange("p (g i) -> p g i", i=M)
            return a[:, :, k:hi]
        for dc in range(DCH):
            vev(lambda e, dc=dc: e.tensor_scalar(
                tapap(accb, dc, 0, M), tapap(xhT, dc, 0, M),
                KtT[:, dc * M:dc * M + 1], None, ALU.mult),
                W(intraT_g[0], intraT_g[1]) if dc == 0 else W(), lag=7)
        for k in range(1, M):
            for dc in range(DCH):
                vev(lambda e, k=k, dc=dc: e.scalar_tensor_tensor(
                    tapap(accb, dc, k, M), tapap(xhT, dc, 0, M - k),
                    KtT[:, dc * M + k:dc * M + k + 1],
                    tapap(accb, dc, k, M), ALU.mult, ALU.add), lag=7)
        VT = sch.now("v")
        xhT_g[0] = ("v", VT)
        return VT

    def stage_C(s, ci, VTAPS):
        nonlocal tmpb_g, xq1t_g
        qc = ci % 4
        isl = 0
        gi = s * CH + ci
        xs = gi % NX1
        qs = gi % NQT
        vgb = vev(lambda e, qc=qc: e.tensor_copy(
            tmpb.ap().rearrange("p (dc t) -> p dc t", dc=DCH),
            accb.ap().rearrange("p (dc t) -> p dc t", dc=DCH)[:, :, qc * P:(qc + 1) * P]),
            W(("v", VTAPS), tmpb_g, tmpa_g))
        itg = [intraT_g[isl]]
        sch.ev("s", lambda e, isl=isl: e.dma_start_transpose(
            intraT.ap().rearrange("p (c t) -> p c t", c=DCH),
            tmpb.ap().rearrange("p (c t) -> p c t", c=DCH)),
            W(("v", vgb), *itg), inc="s_ti%d" % isl, n=16)
        SQI = sch.now("s_ti%d" % isl)
        tmpb_g = ("s_ti%d" % isl, SQI)
        vy = vev(lambda e, qc=qc: e.tensor_add(
            y2t[:], y2p[:, qc * D:(qc + 1) * D], intraT[:]),
            W(("s_ti%d" % isl, SQI), ("s", y2s[qc]),
              *(xq1t_g if isinstance(xq1t_g, list) else [xq1t_g])))
        y2p_g[qc] = ("v", vy)
        intraT_g[isl] = ("v", vy)
        if ci == 0:
            vev(lambda e: e.tensor_add(y2t[0:1, :], y2t[0:1, :], rrew[:]),
                W(("s_rr", 16)))
        vev(lambda e: e.bn_stats(stat6[:, 0:6], y2t[:, 0:512]))
        vev(lambda e: e.bn_stats(stat6[:, 6:12], y2t[:, 512:1024]), lag=1)
        vev(lambda e: e.bn_aggr(stat2[:], stat6[:]))
        sev(lambda e: e.activation(colm[:, 1:2], stat2[:, 1:2], AT.Sqrt,
                                   bias=ceps[:, 0:1]), W(("v", sch.now("v"))), lag=1)
        vev(lambda e: e.reciprocal(colm[:, 1:2], colm[:, 1:2]), W(("s", sch.now("s"))))
        vev(lambda e: e.tensor_scalar(y2t[:], y2t[:], stat2[:, 0:1], colm[:, 1:2],
                                      ALU.subtract, ALU.mult))
        vev(lambda e: e.tensor_mul(y2t[:], y2t[:], gRep[:]))
        vev(lambda e: e.tensor_add(y2t[:], y2t[:], bRep[:]))
        vev(lambda e, xs=xs: e.tensor_add(x1[:, xs * D:(xs + 1) * D],
                                          x1[:, xs * D:(xs + 1) * D], y2t[:]))
        if DEBUG:
            sch.ev("g", dma(dbg["x1"][gi * P:(gi + 1) * P, :],
                            x1[:, xs * D:(xs + 1) * D]),
                   W(("v", sch.now("v"))), inc="gq", n=16)
        vev(lambda e, xs=xs: e.bn_stats(stat6[:, 0:6], x1[:, xs * D:xs * D + 512]))
        vev(lambda e, xs=xs: e.bn_stats(stat6[:, 6:12],
                                        x1[:, xs * D + 512:(xs + 1) * D]), lag=1)
        vev(lambda e: e.bn_aggr(stat2[:], stat6[:]))
        sev(lambda e: e.activation(colm[:, 1:2], stat2[:, 1:2], AT.Sqrt,
                                   bias=ceps[:, 0:1]), W(("v", sch.now("v"))), lag=1)
        vev(lambda e: e.reciprocal(colm[:, 1:2], colm[:, 1:2]), W(("s", sch.now("s"))))
        vev(lambda e, xs=xs: e.tensor_scalar(y2t[:], x1[:, xs * D:(xs + 1) * D],
                                             stat2[:, 0:1], colm[:, 1:2],
                                             ALU.subtract, ALU.mult))
        vev(lambda e: e.reduce_max(colm[:, 2:3], y2t[:], AX.X,
                                   apply_absolute_value=True))
        vev(lambda e: e.tensor_scalar(colm[:, 2:3], colm[:, 2:3], 1e-5, None, ALU.max))
        vev(lambda e: e.reciprocal(colm[:, 3:4], colm[:, 2:3]))
        vev(lambda e: e.tensor_scalar(colm[:, 3:4], colm[:, 3:4], 127.0, None, ALU.mult))
        vev(lambda e: e.tensor_scalar(colm[:, 4:5], colm[:, 2:3], mreps[:, 2:3],
                                      None, ALU.mult))
        xqg = xq1t_g if isinstance(xq1t_g, list) else [xq1t_g]
        vev(lambda e: e.tensor_scalar(xq1t[:], y2t[:], colm[:, 3:4], 1536.0,
                                      ALU.mult, ALU.add), W(*xqg))
        vev(lambda e: e.tensor_scalar(xq1t[:], xq1t[:], 1536.0, None, ALU.subtract))
        vxq = vev(lambda e: e.tensor_scalar(y2t[:], xq1t[:], colm[:, 4:5], None,
                                            ALU.mult))
        prevw = (gi - NQT) // 4
        qwg = [("s_mmw", prevw + 1)] if gi >= NQT else []
        sch.ev("sy", lambda e, qs=qs: e.dma_start_transpose(
            xq1T.ap()[:, qs * D:(qs + 1) * D].rearrange("p (c t) -> p c t", c=DCH),
            y2t.ap().rearrange("p (c t) -> p c t", c=DCH)),
            W(("v", vxq), *qwg), inc=NQTS[qs], n=16)
        xq1_done[gi] = (NQTS[qs], sch.now(NQTS[qs]))
        xq1t_g = [(NQTS[qs], sch.now(NQTS[qs]))]

    def ssm_half(s, h):
        for q in range(2):
            for c4 in range(4):
                gi = s * CH + h * 8 + q * 4 + c4
                if gi >= NX1:
                    yield ("tails", gi - NX1 + 1)
                stage_A(s, h * 8 + q * 4 + c4)
                yield
            VT = taps(None)
            yield
            for c4 in range(4):
                stage_C(s, h * 8 + q * 4 + c4, VT)
                yield
            yield ("qdone", s * 4 + h * 2 + q + 1)

    def ffn_seq(s):
        nonlocal acc2_g, amb_g, ffT_g, s2t_g

        def conv_products(cch, SPS):
            b0 = cch * ST
            sl = cch % 2
            w_c = W(("s", SPS), acc2_g if cch == 0 else ("v", 0), ubuf_g[sl])
            vev(lambda e, b0=b0, sl=sl, cch=cch: e.tensor_scalar(
                ubuf[:, sl * 512:sl * 512 + 512], hsl[:, b0:b0 + 512],
                wcv[:, cch * 4 + 0:cch * 4 + 1], None, ALU.mult), w_c)
            vev(lambda e, b0=b0, sl=sl, cch=cch: e.tensor_scalar(
                vbuf[:, sl * 512:sl * 512 + 512], hsl[:, b0 + 2:b0 + 514],
                wcv[:, cch * 4 + 2:cch * 4 + 3], None, ALU.mult), lag=1)
            cg = vev(lambda e, b0=b0, sl=sl, cch=cch: e.tensor_scalar(
                cbuf[:, sl * 512:sl * 512 + 512], hsl[:, b0 + 1:b0 + 513],
                wcv[:, cch * 4 + 1:cch * 4 + 2], wcv[:, cch * 4 + 3:cch * 4 + 4],
                ALU.mult, ALU.add), lag=2)
            conv_g[cch] = ("v", cg)
            return cg

        def conv_mms(cch, ps, evS, cg):
            a0 = cch * WIN
            sl = cch % 2
            tev(lambda e, ps=ps, sl=sl: e.matmul(
                ps[:], bIb[:], cbuf[:, sl * 512:sl * 512 + 512],
                start=True, stop=False), W(("v", cg), ("s", evS)))
            tev(lambda e, ps=ps, sl=sl: e.matmul(
                ps[:], bIb[:], ubuf[:, sl * 512:sl * 512 + 512],
                start=False, stop=False))
            TCV = tev(lambda e, ps=ps, sl=sl: e.matmul(
                ps[:], bIb[:], vbuf[:, sl * 512:sl * 512 + 512],
                start=False, stop=True))
            ubuf_g[sl] = ("t", TCV)
            SSI = sev(lambda e, ps=ps, a0=a0: e.activation(
                acc2[:, a0:a0 + WIN], ps[:], AT.Silu), W(("t", TCV)), lag=1)
            sch.mark("SSI s=%d cch=%d" % (s, cch))
            if DEBUG and DBGPROBE is not None and (s, cch) == DBGPROBE[:2]:
                if DBGPROBE[2] == probe_hits[0]:
                    sch.ev("sy", dma(dbg["probe"][:, :], acc2[:, a0:a0 + WIN]),
                           W(("s", SSI)), inc="dbq", n=16)
                probe_hits[0] += 1
            return SSI

        def mm1_cch(win, cch):
            sch.mark("MM1 s=%d win=%d cch=%d" % (s, win, cch))
            rh = cch % 3
            extra = PREPW if first_ring[0] else []
            first_ring[0] = False
            sch.ev("sy", dma(
                w1r.ap()[:, rh * 8 * P:(rh + 1) * 8 * P].rearrange(
                    "p (b c) -> p b c", b=DCH),
                w1q8[cch].rearrange("b p c -> p b c")),
                W(w1r_g[rh], *extra), inc="s_w1%d" % rh, n=16)
            SYW = sch.now("s_w1%d" % rh)
            ps, nm = (psM1a, "M1a") if cch % 2 == 0 else (psM1b, "M1b")
            base = ((s * CH + win * 4) % NQT)
            qw = [xq1_done[s * CH + win * 4 + j] for j in range(4)]
            for dc in range(DCH):
                mvap = xq1T.ap().rearrange("p (qs dc t) -> p qs dc t", qs=NQT,
                                           dc=DCH)[:, base:base + 4, dc, :]
                tev(lambda e, ps=ps, rh=rh, dc=dc, mvap=mvap: e.matmul(
                    ps[:], w1r[:, (rh * 8 + dc) * P:(rh * 8 + dc + 1) * P],
                    mvap, start=(dc == 0), stop=(dc == DCH - 1)),
                    W(("s_w1%d" % rh, SYW), *qw, pg[nm]) if dc == 0 else W())
            TMM = sch.now("t")
            w1r_g[rh] = ("t", TMM)
            if cch == HCH - 1:
                sch.ev("t", lambda e: e.nop(), W(), inc="s_mmw", n=1)
            b0 = cch * ST
            if win > 0:
                SPS = sev(lambda e, ps=ps, b0=b0: e.copy(hsl[:, b0 + 513:b0 + 514],
                                                         ps[:, 0:1]), W(("t", TMM)),
                          lag=1)
                cg = conv_products(cch, SPS)
                sev(lambda e, b0=b0: e.copy(hsl[:, b0:b0 + 1],
                                            hsl[:, b0 + 512:b0 + 513]),
                    W(("v", cg)))
                SE = sev(lambda e, ps=ps, b0=b0: e.copy(hsl[:, b0 + 1:b0 + 513],
                                                        ps[:]), W(("v", cg)))
                pg[nm] = ("s", conv_mms(cch, ps, SE, cg))
            else:
                cgl = conv_g[cch] if isinstance(conv_g[cch], list) else [conv_g[cch]]
                SE = sev(lambda e, ps=ps, b0=b0: e.copy(hsl[:, b0 + 1:b0 + 513],
                                                        ps[:]), W(("t", TMM), *cgl))
                pg[nm] = ("s", SE)

        def finish_window(win):
            nonlocal amb_g, ffT_g, acc2_g, s2t_g
            t0 = win * WIN
            SS = sch.now("s")
            sch.mark("FW s=%d win=%d SS=%d" % (s, win, SS))
            if DEBUG and (s, win) == DBGWIN:
                sch.ev("sy", dma(dbg["acc2"][:, :], acc2[:]), W(("s", SS)),
                       inc="dbq", n=16)
                sch.ev("sy", dma(dbg["hsl"][:, :], hsl[:]), W(("s", SS)),
                       inc="dbq", n=16)
            vev(lambda e: e.tensor_tensor(maxt[:], acc2[:, 0:WIN], acc2[:, WIN:2 * WIN],
                                          ALU.max), W(("s", SS), amb_g))
            vev(lambda e: e.tensor_tensor(mint[:], acc2[:, 0:WIN], acc2[:, WIN:2 * WIN],
                                          ALU.min), lag=1)
            for cch in range(2, HCH):
                vev(lambda e, cch=cch: e.tensor_tensor(
                    maxt[:], maxt[:], acc2[:, cch * WIN:(cch + 1) * WIN], ALU.max),
                    lag=1)
                vev(lambda e, cch=cch: e.tensor_tensor(
                    mint[:], mint[:], acc2[:, cch * WIN:(cch + 1) * WIN], ALU.min),
                    lag=1)
            yield
            vev(lambda e: e.tensor_scalar(mint[:], mint[:], -1.0, None, ALU.mult))
            vam = vev(lambda e: e.tensor_tensor(mint[:], maxt[:], mint[:], ALU.max),
                      W(*(amb_g if isinstance(amb_g, list) else [amb_g])))
            sch.ev("s", lambda e: e.dma_start_transpose(
                amT.ap().rearrange("p (b c) -> p b c", b=4),
                mint.ap().rearrange("p (b c) -> p b c", b=4)),
                W(("v", vam)), inc="s_tam", n=16)
            SQAM = sch.now("s_tam")
            amb_g = ("s_tam", SQAM)
            for tt in range(4):
                vev(lambda e, tt=tt: e.reduce_max(s2col[:, tt:tt + 1],
                                                  amT[:, tt * P:(tt + 1) * P], AX.X),
                    W(("s_tam", SQAM), s2t_g) if tt == 0 else W())
                vev(lambda e, tt=tt: e.tensor_scalar(s2col[:, tt:tt + 1],
                                                     s2col[:, tt:tt + 1], 1e-5, None,
                                                     ALU.max))
                vev(lambda e, tt=tt: e.tensor_scalar(s2cb[:, tt:tt + 1],
                                                     s2col[:, tt:tt + 1], mreps[:, 3:4],
                                                     None, ALU.mult))
                vev(lambda e, tt=tt: e.reciprocal(s2col[:, tt:tt + 1],
                                                  s2col[:, tt:tt + 1]))
                vev(lambda e, tt=tt: e.tensor_scalar(s2col[:, tt:tt + 1],
                                                     s2col[:, tt:tt + 1], 127.0, None,
                                                     ALU.mult))
            VS2 = sch.now("v")
            if DEBUG and (s, win) == DBGWIN:
                sch.ev("sy", dma(dbg["scol"][:, 0:4], s2col[:]), W(("v", VS2)),
                       inc="dbq", n=16)
                sch.ev("sy", dma(dbg["scol"][:, 4:8], s2cb[:]), W(("v", VS2)),
                       inc="dbq", n=16)
            for tt in range(4):
                tev(lambda e, tt=tt: e.transpose(psM2a[0:1, tt * P:(tt + 1) * P],
                                                 s2col[:, tt:tt + 1], bI[:]),
                    W(("v", VS2), pg["M2a"]) if tt == 0 else W())
            s2t_g = ("t", sch.now("t"))
            vev(lambda e: e.tensor_copy(srow[:], psM2a[0:1, 0:WIN]),
                W(("t", sch.now("t"))))
            pg["M2a"] = ("v", sch.now("v"))
            tev(lambda e: e.matmul(psM2b[:], bONE[0:1, :], srow[:], start=True,
                                   stop=True), W(("v", sch.now("v")), pg["M2b"]))
            vev(lambda e: e.tensor_copy(s2rep[:], psM2b[:]), W(("t", sch.now("t"))))
            pg["M2b"] = ("v", sch.now("v"))
            VREP = sch.now("v")
            bcast = bass.AP(tensor=s2rep.ap().tensor, offset=s2rep.ap().offset,
                            ap=[list(s2rep.ap().ap[0]), [0, HCH], [1, WIN]])
            vev(lambda e, bcast=bcast: e.tensor_tensor(
                acc2.ap().rearrange("p (c t) -> p c t", c=HCH),
                acc2.ap().rearrange("p (c t) -> p c t", c=HCH), bcast, ALU.mult),
                W(("v", VREP)))
            VQ2 = vev(lambda e: e.tensor_scalar(acc2[:], acc2[:], MAGIC, MAGIC,
                                                ALU.add, ALU.subtract))
            yield
            def w2load(hh):
                rh = hh % 3
                sch.ev("sy", dma(
                    w2r.ap()[:, rh * 16 * P:(rh + 1) * 16 * P].rearrange(
                        "p (a c) -> p a c", a=16),
                    w2q8[hh // 2, (hh % 2) * 16:(hh % 2) * 16 + 16].rearrange(
                        "a p c -> p a c")),
                    W(w2r_g[rh], *PREPW), inc="s_w2%d" % rh, n=16)
            w2load(0)
            w2load(1)
            w2load(2)
            for dc in range(DCH):
                rhA, rhB = (2 * dc) % 3, (2 * dc + 1) % 3
                GQA = sch.now("s_w2%d" % rhA)
                GQB = sch.now("s_w2%d" % rhB)
                ps, nm = (psM2a, "M2a") if dc % 2 == 0 else (psM2b, "M2b")
                for cch in range(HCH):
                    rh = rhA if cch < 16 else rhB
                    ti = cch % 16
                    tev(lambda e, ps=ps, rh=rh, ti=ti, cch=cch: e.matmul(
                        ps[:], w2r[:, (rh * 16 + ti) * P:(rh * 16 + ti + 1) * P],
                        acc2[:, cch * WIN:(cch + 1) * WIN],
                        start=(cch == 0), stop=(cch == HCH - 1)),
                        W(("s_w2%d" % rhA, GQA), ("s_w2%d" % rhB, GQB),
                          ("v", VQ2), pg[nm]) if cch == 0 else W())
                TM2 = sch.now("t")
                w2r_g[rhA] = ("t", TM2)
                w2r_g[rhB] = ("t", TM2)
                if 2 * dc + 3 < 2 * DCH:
                    w2load(2 * dc + 3)
                if 2 * dc + 4 < 2 * DCH:
                    w2load(2 * dc + 4)
                fsl = dc % 2
                SF = sev(lambda e, ps=ps, fsl=fsl: e.copy(
                    ffw[:, fsl * WIN:(fsl + 1) * WIN], ps[:]),
                    W(("t", TM2), ffw_g[fsl]))
                pg[nm] = ("s", SF)
                sch.ev("s", lambda e, dc=dc, fsl=fsl: e.dma_start_transpose(
                    ffT.ap()[:, dc * WIN:(dc + 1) * WIN].rearrange("p (b c) -> p b c",
                                                                   b=4),
                    ffw.ap()[:, fsl * WIN:(fsl + 1) * WIN].rearrange("p (b c) -> p b c",
                                                                     b=4)),
                    W(("s", SF), ffT_g if dc == 0 else ("v", 0)),
                    inc="s_tf%d" % fsl, n=16)
                ffw_g[fsl] = ("s_tf%d" % fsl, sch.now("s_tf%d" % fsl))
                if dc % 4 == 3:
                    yield
            SQF = None
            QFW = [("s_tf0", sch.now("s_tf0")), ("s_tf1", sch.now("s_tf1"))]
            if DEBUG and (s, win) == DBGWIN:
                sch.ev("sy", dma(dbg["ffw"][:, :], ffT[:]), W(*QFW), inc="dbq", n=16)
            for tt in range(4):
                gi = s * CH + win * 4 + tt
                xs = gi % NX1
                osl = (win * 4 + tt) % 2
                ftap = ffT.ap().rearrange("p (dc tt t) -> p dc tt t", dc=DCH, tt=4)[
                    :, :, tt, :]
                vo = vev(lambda e, tt=tt, xs=xs, osl=osl, ftap=ftap: e.scalar_tensor_tensor(
                    outt[:, osl * D:(osl + 1) * D], ftap,
                    s2cb[:, tt:tt + 1], x1[:, xs * D:(xs + 1) * D], ALU.mult, ALU.add),
                    W(*QFW, outt_g[osl]))
                sch.ev("s", lambda e, s=s, t0=t0, tt=tt, osl=osl: e.dma_start(
                    out=out[s, t0 + tt * P:t0 + (tt + 1) * P, :],
                    in_=outt[:, osl * D:(osl + 1) * D]), W(("v", vo)),
                    inc="s_o%d" % osl, n=16)
                outt_g[osl] = ("s_o%d" % osl, sch.now("s_o%d" % osl))
                x1_g[xs] = [("s_o%d" % osl, sch.now("s_o%d" % osl))]
                tails_cnt[0] += 1
            ffT_g = ("v", sch.now("v"))
            acc2_g = ("t", sch.now("t"))

        for w in range(NW):
            yield ("q", s * 4 + w)
            if w == 0 and s == 0:
                vev(lambda e: e.memset(
                    hsl.ap().rearrange("p (c k) -> p c k", k=ST)[:, :, 0:1], 0.0),
                    W(*PREPW))
            elif w == 0:
                vev(lambda e: e.memset(
                    hsl.ap().rearrange("p (c k) -> p c k", k=ST)[:, :, 0:1], 0.0),
                    W(*(conv_g[HCH - 1] if isinstance(conv_g[HCH - 1], list)
                        else [conv_g[HCH - 1]])))
            for cch in range(HCH):
                mm1_cch(w, cch)
                if cch % 4 == 3:
                    yield
            if w >= 1:
                for _ in finish_window(w - 1):
                    yield
        vev(lambda e: e.memset(
            hsl.ap().rearrange("p (c k) -> p c k", k=ST)[:, :, 513:514], 0.0),
            W(("s", sch.now("s"))))
        for cch in range(HCH):
            ps, nm = (psM1a, "M1a") if cch % 2 == 0 else (psM1b, "M1b")
            cg = conv_products(cch, 0)
            pg[nm] = ("s", conv_mms(cch, ps, pg[nm][1] if pg[nm][0] == "s" else 0, cg))
            if cch % 8 == 7:
                yield
        for _ in finish_window(NW - 1):
            yield

    ssm_gens = [(s * 2 + h, ssm_half(s, h)) for s in range(BC) for h in range(2)]
    fgens = [ffn_seq(s) for s in range(BC)]
    fneed = []
    for g in fgens:
        r = next(g)
        fneed.append(r[1] if isinstance(r, tuple) else None)
    quarters_done = 0
    fcur = 0
    sidx = 0
    sreq = [None]

    def ffn_tick():
        nonlocal fcur
        if fcur >= BC:
            return False
        if fneed[fcur] is not None and fneed[fcur] >= quarters_done:
            return False
        try:
            r = next(fgens[fcur])
            fneed[fcur] = r[1] if isinstance(r, tuple) else None
        except StopIteration:
            fcur += 1
        return True

    def ssm_tick():
        nonlocal sidx, quarters_done
        if sidx >= len(ssm_gens):
            return False
        if sreq[0] is not None and tails_cnt[0] < sreq[0]:
            return False
        hid, g = ssm_gens[sidx]
        try:
            r = next(g)
            if isinstance(r, tuple):
                if r[0] == "qdone":
                    quarters_done = max(quarters_done, r[1])
                    sreq[0] = None
                else:
                    sreq[0] = r[1]
            else:
                sreq[0] = None
        except StopIteration:
            sidx += 1
            sreq[0] = None
        return True

    while sidx < len(ssm_gens) or fcur < BC:
        p1 = ssm_tick()
        p2 = ffn_tick()
        p4 = ssm_tick()
        p3 = ffn_tick()
        if not (p1 or p2 or p3 or p4):
            raise RuntimeError("pipeline deadlock: sidx=%d sreq=%s fcur=%d fneed=%s "
                               "quarters=%d tails=%d" % (sidx, sreq[0], fcur,
                                                         fneed, quarters_done,
                                                         tails_cnt[0]))

    sch.ev("g", lambda e: e.engine_nop(),
           W(("s_o0", sch.now("s_o0")), ("s_o1", sch.now("s_o1"))))

    sem_names = sorted(set(list(sch.cnt.keys()) +
                           ["v", "s", "t"] +
                           [w[0] for (_e, ws, _f, _i, _n, _l) in sch.events for w in ws]))
    import contextlib
    with contextlib.ExitStack() as stack:
        sems = {nm: stack.enter_context(nc.semaphore(nm)) for nm in sem_names}
        blk = stack.enter_context(nc.Block())

        @blk.sync
        def _(e):
            sch.emit("sy", e, sems)

        @blk.vector
        def _(e):
            sch.emit("v", e, sems, delay=(SERIAL == 11))

        @blk.scalar
        def _(e):
            sch.emit("s", e, sems, delay=(SERIAL == 11))

        @blk.tensor
        def _(e):
            sch.emit("t", e, sems)

        @blk.gpsimd
        def _(e):
            sch.emit("g", e, sems)

    globals()['_last_sch'] = sch
    return nc


def kernel(L0_attn_u, L0_attn_A_log, L0_attn_B, L0_attn_C, L0_attn_D,
           L0_attn_delta, L0_attn_gamma, L0_attn_beta, L0_attn_freq_align,
           L0_attn_reward, L0_ffn_w1, L0_ffn_conv_w, L0_ffn_conv_b, L0_ffn_w2):
    hc = host_consts()
    nc = build_program()
    f32 = lambda a: np.ascontiguousarray(np.asarray(a, dtype=np.float32))
    base = {
        "w1T": f32(np.asarray(L0_ffn_w1).T),
        "w2T": f32(np.asarray(L0_ffn_w2).T),
        "CT": f32(np.asarray(L0_attn_C).T),
        "alog": f32(L0_attn_A_log), "delta": f32(L0_attn_delta),
        "Dvec": f32(L0_attn_D), "fa": f32(L0_attn_freq_align),
        "gamma": f32(L0_attn_gamma), "beta": f32(L0_attn_beta),
        "reward": f32(L0_attn_reward),
        "convw": f32(np.asarray(L0_ffn_conv_w).reshape(HC, 3)),
        "convb": f32(L0_ffn_conv_b),
    }
    base.update(hc)
    uu = f32(L0_attn_u)
    in_maps = []
    for c in range(8):
        m = dict(base)
        m["u"] = np.ascontiguousarray(uu[2 * c:2 * c + 2])
        in_maps.append(m)
    res = run_bass_kernel_spmd(nc, in_maps, core_ids=list(range(8)))
    return np.concatenate([r["out"] for r in res.results], axis=0)


# version 21
# speedup vs baseline: 1.0487x; 1.0487x over previous
import numpy as np
import concourse.bass as bass
import concourse.mybir as mybir
from concourse.bass_utils import run_bass_kernel_spmd

F32 = mybir.dt.float32
BF16 = mybir.dt.bfloat16
F16 = mybir.dt.float16
F8 = mybir.dt.float8e4
AT = mybir.ActivationFunctionType
ALU = mybir.AluOpType
AX = mybir.AxisListType

MAGIC = 12582912.0
NEGBIG = -1.0e30
EPS = 1e-5

BC, L, D, DCH, N, M, CH = 2, 2048, 1024, 8, 64, 16, 16
HC, HCH, WIN, NW, P = 4096, 32, 512, 4, 128
ST = 514

_CONST_CACHE = {}
DEBUG = False
SERIAL = False
DBGWIN = (0, 1)
DBGPROBE = None
BLEVEL = 0
SLO, SHI = 0, 10 ** 9
SELFWAIT = True
RACECHECK = False


def host_consts():
    if _CONST_CACHE:
        return _CONST_CACHE
    c = {}
    j = np.arange(P)
    c["cJ127"] = np.broadcast_to((127.0 - j)[:, None], (P, N)).astype(np.float32).copy()
    c["cT1"] = np.broadcast_to((j + 1.0)[None, :], (N, P)).astype(np.float32).copy()
    c["cK16"] = np.broadcast_to(np.arange(M, dtype=np.float64)[None, :], (N, M)).astype(np.float32).copy()
    J2 = np.zeros((4, P, P), np.float32)
    M2 = np.full((4, P, P), NEGBIG, np.float32)
    for g in range(4):
        for jj in range(P):
            q, jl = jj // M, jj % M
            for e in range(2):
                if q == 2 * g + e:
                    J2[g, jj, e * N:(e + 1) * N] = 15.0 - jl
                    M2[g, jj, e * N:(e + 1) * N] = 0.0
    c["cJ2"], c["cM2"] = J2.reshape(4 * P, P).copy(), M2.reshape(4 * P, P).copy()
    T2 = np.zeros((4, P, P), np.float32)
    M2W = np.full((4, P, P), NEGBIG, np.float32)
    for pp in range(4):
        for e in range(2):
            qp = 2 * pp + e
            lo = M * (qp + 1)
            for t in range(lo, P):
                T2[pp, e * N:(e + 1) * N, t] = t - lo + 1.0
                M2W[pp, e * N:(e + 1) * N, t] = 0.0
    c["cT2"], c["cM2W"] = T2.reshape(4 * P, P).copy(), M2W.reshape(4 * P, P).copy()
    c["cI"] = np.eye(P, dtype=np.float32)
    c["cONE"] = np.ones((P, P), np.float32)
    _CONST_CACHE.update(c)
    return c


class Sched:

    def __init__(self):
        self.events = []
        self.cnt = {}
        self.pending = {}
        self.marks = []

    def mark(self, name):
        self.marks.append((name, len(self.events), dict(self.cnt)))

    def barrier(self, engines=("sy", "v", "s", "t", "g")):
        snap = [(s, v) for s, v in self.cnt.items() if v > 0]
        for eng in engines:
            self.pending[eng] = list(snap)

    def now(self, sem):
        return self.cnt.get(sem, 0)

    def ev(self, eng, fn, waits=(), inc=None, n=1, lag=0):
        if SERIAL == 1:
            w = [(s, v) for s, v in self.cnt.items() if v > 0]
        elif SERIAL == 2 and eng in ("v", "s", "t"):
            w = [(s, v) for s, v in self.cnt.items()
                 if v > 0 and s in ("v", "s", "t")]
            w += [(s, v) for (s, v) in waits
                  if s is not None and v > 0 and s not in ("v", "s", "t")]
        elif SERIAL == 7 and eng == "v":
            vi = self.cnt.get("v", 0)
            if SLO <= vi < SHI:
                w = [(s, v) for s, v in self.cnt.items() if v > 0 and s == "s"]
                w += [(s, v) for (s, v) in waits if s is not None and v > 0]
            else:
                w = [(s, v) for (s, v) in waits if s is not None and v > 0]
        elif SERIAL == 8 and eng == "s":
            si = self.cnt.get("s", 0)
            if SLO <= si < SHI:
                w = [(s, v) for s, v in self.cnt.items() if v > 0 and s == "v"]
                w += [(s, v) for (s, v) in waits if s is not None and v > 0]
            else:
                w = [(s, v) for (s, v) in waits if s is not None and v > 0]
        elif SERIAL == 9:
            if eng == "v":
                w = [(s, v) for s, v in self.cnt.items() if v > 0 and s == "s"]
                w += [(s, v) for (s, v) in waits if s is not None and v > 0]
            elif eng == "s":
                si = self.cnt.get("s", 0)
                w = list(waits)
                if SLO <= si < SHI:
                    w = [(s, v) for s, v in self.cnt.items() if v > 0 and s == "v"] + w
                w = [(s, v) for (s, v) in w if s is not None and v > 0]
            else:
                w = [(s, v) for (s, v) in waits if s is not None and v > 0]
        elif SERIAL == 10:
            if eng == "s":
                w = [(s, v) for s, v in self.cnt.items() if v > 0 and s == "v"]
                w += [(s, v) for (s, v) in waits if s is not None and v > 0]
            elif eng == "v":
                vi = self.cnt.get("v", 0)
                w = list(waits)
                if SLO <= vi < SHI:
                    w = [(s, v) for s, v in self.cnt.items() if v > 0 and s == "s"] + w
                w = [(s, v) for (s, v) in w if s is not None and v > 0]
            else:
                w = [(s, v) for (s, v) in waits if s is not None and v > 0]
        elif SERIAL in (4, 5, 6):
            pair = {4: ("v", "s"), 5: ("v", "t"), 6: ("s", "t")}[SERIAL]
            if eng in pair:
                w = [(s, v) for s, v in self.cnt.items() if v > 0 and s in pair]
                w += [(s, v) for (s, v) in waits
                      if s is not None and v > 0 and s not in pair]
            else:
                w = [(s, v) for (s, v) in waits if s is not None and v > 0]
        elif SERIAL == 3:
            w = [(s, v) for s, v in self.cnt.items()
                 if v > 0 and s not in ("v", "s", "t")]
            w += [(s, v) for (s, v) in waits if s is not None and v > 0]
        else:
            w = [(s, v) for (s, v) in waits if s is not None and v > 0]
        if eng in self.pending:
            w = self.pending.pop(eng) + w
        self.events.append((eng, w, fn, inc, n, lag))
        if inc is not None:
            self.cnt[inc] = self.cnt.get(inc, 0) + n

    def emit(self, eng_name, handle, sems, delay=False):
        own = 0
        for eng, waits, fn, inc, n, lag in self.events:
            if eng != eng_name:
                continue
            if SELFWAIT and eng_name in ("v", "s") and own - lag > 0:
                handle.wait_ge(sems[eng_name], own - lag)
            for s, v in waits:
                if s == eng_name and v <= own - lag:
                    continue
                handle.wait_ge(sems[s], v)
            inst = fn(handle)
            if inc == eng_name:
                own += n
            if inc is not None:
                inst.then_inc(sems[inc], n)


def build_program():
    nc = bass.Bass("TRN2", target_bir_lowering=False,
                   detect_race_conditions=RACECHECK)
    hc = host_consts()

    u = nc.dram_tensor("u", [BC, L, D], F32, kind="ExternalInput").ap()
    out = nc.dram_tensor("out", [BC, L, D], F32, kind="ExternalOutput").ap()
    w1T = nc.dram_tensor("w1T", [D, HC], F32, kind="ExternalInput").ap()
    w2T = nc.dram_tensor("w2T", [HC, D], F32, kind="ExternalInput").ap()
    CTd = nc.dram_tensor("CT", [N, D], F32, kind="ExternalInput").ap()
    alog = nc.dram_tensor("alog", [N], F32, kind="ExternalInput").ap()
    deltad = nc.dram_tensor("delta", [D], F32, kind="ExternalInput").ap()
    Dd = nc.dram_tensor("Dvec", [D], F32, kind="ExternalInput").ap()
    fad = nc.dram_tensor("fa", [D], F32, kind="ExternalInput").ap()
    gammad = nc.dram_tensor("gamma", [D], F32, kind="ExternalInput").ap()
    betad = nc.dram_tensor("beta", [D], F32, kind="ExternalInput").ap()
    rewardd = nc.dram_tensor("reward", [D], F32, kind="ExternalInput").ap()
    convwd = nc.dram_tensor("convw", [HC, 3], F32, kind="ExternalInput").ap()
    convbd = nc.dram_tensor("convb", [HC], F32, kind="ExternalInput").ap()
    cst = {k: nc.dram_tensor(k, list(v.shape), F32, kind="ExternalInput").ap()
           for k, v in hc.items()}
    dbg = {}
    if DEBUG:
        dbg["xh"] = nc.dram_tensor("dbg_xh", [L, D], BF16, kind="ExternalOutput").ap()
        dbg["y2p"] = nc.dram_tensor("dbg_y2p", [L, D], BF16, kind="ExternalOutput").ap()
        dbg["intra"] = nc.dram_tensor("dbg_intra", [L, D], F16, kind="ExternalOutput").ap()
        dbg["x1"] = nc.dram_tensor("dbg_x1", [BC * L, D], F16, kind="ExternalOutput").ap()
        dbg["xq1"] = nc.dram_tensor("dbg_xq1", [L, D], F16, kind="ExternalOutput").ap()
        dbg["hsl"] = nc.dram_tensor("dbg_hsl", [P, HCH * ST], BF16, kind="ExternalOutput").ap()
        dbg["acc2"] = nc.dram_tensor("dbg_acc2", [P, HCH * WIN], F16, kind="ExternalOutput").ap()
        dbg["scol"] = nc.dram_tensor("dbg_scol", [P, 8], F32, kind="ExternalOutput").ap()
        dbg["ffw"] = nc.dram_tensor("dbg_ffw", [P, DCH * WIN], BF16, kind="ExternalOutput").ap()
        dbg["kt"] = nc.dram_tensor("dbg_kt", [P, DCH * M], F32, kind="ExternalOutput").ap()
        dbg["mr"] = nc.dram_tensor("dbg_mr", [P, 8], F32, kind="ExternalOutput").ap()
        dbg["probe"] = nc.dram_tensor("dbg_probe", [P, WIN], F16, kind="ExternalOutput").ap()
    w1q8 = nc.dram_tensor("w1q8", [HCH, DCH, P, P], F8, kind="Internal").ap()
    w2q8 = nc.dram_tensor("w2q8", [DCH, HCH, P, P], F8, kind="Internal").ap()

    sch = Sched()
    keep = []
    SSM_BYTES = 51200
    FFN_BYTES = 87296
    ARENA_BYTES = SSM_BYTES + FFN_BYTES
    _off = {"arena_ssm": [0], "arena_ffn": [SSM_BYTES], "arena_setup": [SSM_BYTES]}
    _REG_END = {"arena_ssm": SSM_BYTES, "arena_ffn": ARENA_BYTES,
                "arena_setup": ARENA_BYTES}
    _arena = [None]

    def _align(x, a=64):
        return (x + a - 1) // a * a

    class TV:

        def __init__(self, ap):
            self._ap = ap

        def ap(self):
            return self._ap

        def __getitem__(self, sl):
            return self._ap[sl]

    def sb(name, shape, dtype, region="perm"):
        nbytes = int(np.prod(shape[1:])) * mybir.dt.size(dtype)
        if region == "perm":
            h = nc.alloc_sbuf_tensor(name, shape, dtype)
            keep.append(h)
            return h
        o = _off[region][0]
        _off[region][0] = _align(o + nbytes)
        assert _off[region][0] <= _REG_END.get(region, ARENA_BYTES), (
            name, region, _off[region][0])
        es = mybir.dt.size(dtype)
        assert o % es == 0
        view = _arena[0].bitcast(dtype).ap()[:, o // es: o // es + int(np.prod(shape[1:]))]
        if len(shape) == 2 and shape[0] < 128:
            view = view[0:shape[0], :]
        return TV(view)

    def start_arena():
        _arena[0] = nc.alloc_sbuf_tensor("arena", [P, ARENA_BYTES], mybir.dt.uint8)
        keep.append(_arena[0])

    bI = sb("bI", [P, P], F32)
    bONE = sb("bONE", [P, P], F32)
    bAlog = sb("bAlog", [N, 1], F32)
    bd0 = sb("bd0", [1, 1], F32)
    rrew = sb("rrew", [1, D], BF16)
    cD = sb("cD", [P, DCH], F32)
    cfa = sb("cfa", [P, DCH], F32)
    cDfa = sb("cDfa", [P, DCH], F32)
    wcv = sb("wcv", [P, HCH * 4], F32)
    dA = sb("dA", [N, 1], F32)
    dA2 = sb("dA2", [P, 1], F32)
    rS = sb("rS", [N, 1], F32)
    V1 = sb("V1", [P, N], BF16)
    W1Ts = sb("W1Ts", [N, P], BF16)
    V2 = sb("V2", [P, 4 * P], BF16)
    W2s = sb("W2s", [P, 4 * P], BF16)
    P2m = sb("P2m", [N, M], BF16)
    CtT = sb("CtT", [N, D], BF16)
    Ct2 = sb("Ct2", [P, D], BF16)
    KtT = sb("KtT", [P, DCH * M], F32)
    gRep = sb("gRep", [P, D], BF16)
    bRep = sb("bRep", [P, D], BF16)
    mcols = sb("mcols", [P, 48], F32)
    msum = sb("msum", [P, 4], F32)
    mreps = sb("mreps", [P, 8], F32)
    stat6 = sb("stat6", [P, 12], F32)
    stat2 = sb("stat2", [P, 2], F32)
    colm = sb("colm", [P, 8], F32)
    ceps = sb("ceps", [P, 1], F32)
    c128 = sb("c128", [P, 1], F32)

    NX1, NQT = 12, 8
    x1 = sb("x1", [P, NX1 * D], F16)
    xq1T = sb("xq1T", [P, NQT * D], BF16)
    w1r = sb("w1r", [P, 24 * P], F8)
    w2r = sb("w2r", [P, 48 * P], F8)
    bIb = sb("bIb", [P, P], BF16)
    outt = sb("outt", [P, 2 * D], F32)
    start_arena()
    bJ127 = sb("bJ127", [P, N], F32, "arena_setup")
    bT1 = sb("bT1", [N, P], F32, "arena_setup")
    bK16 = sb("bK16", [N, M], F32, "arena_setup")
    bJ2 = sb("bJ2", [P, 4 * P], F32, "arena_setup")
    bM2 = sb("bM2", [P, 4 * P], F32, "arena_setup")
    bT2 = sb("bT2", [P, 4 * P], F32, "arena_setup")
    bM2W = sb("bM2W", [P, 4 * P], F32, "arena_setup")
    bCT = sb("bCT", [N, D], F32, "arena_setup")
    rfa = sb("rfa", [1, D], F32, "arena_setup")
    rgam = sb("rgam", [1, D], F32, "arena_setup")
    rbet = sb("rbet", [1, D], F32, "arena_setup")
    tmpc = sb("tmpc", [P, P], F32, "arena_setup")
    tmpd = sb("tmpd", [N, P], F32, "arena_setup")
    wbuf = sb("wbuf", [P, 2 * 2048], F32, "arena_setup")
    wqb = sb("wqb", [P, 2 * 2048], F8, "arena_setup")
    wdump = sb("wdump", [P, 2 * 2048], BF16, "arena_setup")
    h1 = sb("h1", [N, D], F32, "arena_ssm")
    hC1 = sb("hC1", [N, D], BF16, "arena_ssm")
    y2t = sb("y2t", [P, D], BF16, "arena_ssm")

    xq1t = sb("xq1t", [P, D], F16, "arena_ssm")
    xh = sb("xh", [P, 2 * D], BF16, "arena_ssm")
    intraT = sb("intraT", [P, D], F16, "arena_ssm")
    tmpa = sb("tmpa", [P, D], BF16, "arena_ssm")
    _tmpb_off = _off["arena_ssm"][0] - 2048
    tmpb = TV(_arena[0].bitcast(F16).ap()[:, _tmpb_off // 2:_tmpb_off // 2 + D])
    xhT = sb("xhT", [P, DCH * 512], BF16, "arena_ssm")
    accb = sb("accb", [P, DCH * 512], F16, "arena_ssm")
    y2p = sb("y2p", [P, 4 * D], BF16, "arena_ssm")
    G2C = sb("G2C", [P, 4 * D], BF16, "arena_ssm")
    hsl = sb("hsl", [P, HCH * ST], BF16, "arena_ffn")
    ubuf = sb("ubuf", [P, 2 * 512], BF16, "arena_ffn")
    vbuf = sb("vbuf", [P, 2 * 512], BF16, "arena_ffn")
    cbuf = sb("cbuf", [P, 2 * 512], BF16, "arena_ffn")
    acc2 = sb("acc2", [P, HCH * WIN], F16, "arena_ffn")
    maxt = sb("maxt", [P, WIN], F16, "arena_ffn")
    mint = sb("mint", [P, WIN], F16, "arena_ffn")
    amT = maxt
    s2col = sb("s2col", [P, 4], F32, "arena_ffn")
    s2cb = sb("s2cb", [P, 4], F32, "arena_ffn")
    s2rep = sb("s2rep", [P, WIN], BF16, "arena_ffn")
    ffw = sb("ffw", [P, 2 * WIN], BF16, "arena_ffn")
    srow = sb("srow", [1, WIN], F32, "arena_ffn")
    ffT = sb("ffT", [P, 4 * D], BF16, "arena_ffn")

    psA = nc.alloc_psum_tensor("psA", [P, 1024], F32)
    psB = nc.alloc_psum_tensor("psB", [P, 1024], F32)
    psM1a = nc.alloc_psum_tensor("psM1a", [P, 512], F32)
    psM1b = nc.alloc_psum_tensor("psM1b", [P, 512], F32)
    psM2a = nc.alloc_psum_tensor("psM2a", [P, 512], F32)
    psM2b = nc.alloc_psum_tensor("psM2b", [P, 512], F32)

    pg = {"A": ("v", 0), "B": ("v", 0), "M1a": ("s", 0), "M1b": ("s", 0),
          "M2a": ("s", 0), "M2b": ("s", 0)}
    xh_g = [[], []]
    intraT_g = [("v", 0), ("v", 0)]
    G2C_g = [("t", 0), ("t", 0)]
    y2p_g = [("v", 0)] * 8
    xq1t_g = ("sq", 0)
    amb_g = ("sq", 0)
    ffw_g = [("v", 0), ("v", 0)]
    ffT_g = ("v", 0)
    outt_g = [("ou", 0), ("ou", 0)]
    w1r_g = [("t", 0), ("t", 0), ("t", 0)]
    w2r_g = [("t", 0), ("t", 0), ("t", 0)]
    conv_g = [("v", 0)] * HCH
    acc2_g = ("t", 0)
    xstore_g = [("ou", 0)] * CH
    xstage_g = [("v", 0), ("v", 0)]
    tmpa_g = ("v", 0)
    tmpb_g = ("sq", 0)
    y2s = [0] * 8
    s2t_g = ("t", 0)
    ubuf_g = [("t", 0), ("t", 0)]
    pg_evac = [("s", 0)]

    def W(*pairs):
        return list(pairs)

    def vev(fn, waits=(), lag=0):
        sch.ev("v", fn, waits, inc="v", lag=lag)
        return sch.now("v")

    def sev(fn, waits=(), lag=0):
        sch.ev("s", fn, waits, inc="s", lag=lag)
        return sch.now("s")

    def tev(fn, waits=()):
        sch.ev("t", fn, waits, inc="t")
        return sch.now("t")

    def dma(dst, src, slow=False):
        if slow:
            return lambda e: e.dma_start(out=dst, in_=src,
                                         allow_slow_non_contiguous=True)
        return lambda e: e.dma_start(out=dst, in_=src)

    for dst, src in [
        (bJ127[:], cst["cJ127"]), (bT1[:], cst["cT1"]), (bK16[:], cst["cK16"]),
        (bI[:], cst["cI"]), (bONE[:], cst["cONE"]),
        (bAlog[:], alog.rearrange("(p o) -> p o", o=1)),
        (bd0[:], deltad[0:1].rearrange("(p o) -> p o", o=1)),
        (bCT[:], CTd),
        (rfa[:], fad.rearrange("(o d) -> o d", o=1)),
        (rgam[:], gammad.rearrange("(o d) -> o d", o=1)),
        (rbet[:], betad.rearrange("(o d) -> o d", o=1)),

        (cD[:], Dd.rearrange("(c p) -> p c", p=P)),
        (cfa[:], fad.rearrange("(c p) -> p c", p=P)),
        (wcv.ap().rearrange("p (c k) -> p c k", k=4)[:, :, 0:3],
         convwd.rearrange("(c p) k -> p c k", p=P)),
        (wcv.ap().rearrange("p (c k) -> p c k", k=4)[:, :, 3:4],
         convbd.rearrange("(c p) -> p c", p=P).rearrange("p c -> p c 1") if False else
         convbd.rearrange("(c p o) -> p c o", p=P, o=1)),
    ]:
        sch.ev("sy", dma(dst, src, slow=True), inc="s_ld", n=16)
    for g in range(4):
        for bdst, key in [(bJ2, "cJ2"), (bM2, "cM2"), (bT2, "cT2"), (bM2W, "cM2W")]:
            sch.ev("sy", dma(bdst[:, g * P:(g + 1) * P], cst[key][g * P:(g + 1) * P, :]),
                   inc="s_ld", n=16)
    sch.ev("g", dma(rrew[:], rewardd.rearrange("(o d) -> o d", o=1)), inc="s_rr", n=16)
    LD0 = sch.now("s_ld")

    vev(lambda e: e.tensor_copy(bIb[:], bI[:]), W(("s_ld", LD0)))
    vev(lambda e: e.memset(ceps[:], EPS))
    vev(lambda e: e.memset(c128[:], 128.0))
    sev(lambda e: e.activation(colm[0:N, 0:1], bAlog[:], AT.Exp), W(("s_ld", LD0)))
    tev(lambda e: e.matmul(psA[0:N, 0:1], bONE[0:1, 0:N], bd0[:], start=True, stop=True),
        W(("s_ld", LD0)))
    vev(lambda e: e.tensor_copy(colm[0:N, 1:2], psA[0:N, 0:1]), W(("t", sch.now("t"))))
    vev(lambda e: e.tensor_scalar(dA[:], colm[0:N, 0:1], colm[0:N, 1:2], -1.0,
                                  ALU.mult, ALU.mult), W(("s", sch.now("s"))))
    VDA = sch.now("v")
    pg["A"] = ("v", VDA)
    sev(lambda e: e.activation(rS[:], dA[:], AT.Exp, scale=c128[0:N, 0:1]), W(("v", VDA)))
    vev(lambda e: e.tensor_copy(dA2[0:N, :], dA[:]))
    vev(lambda e: e.tensor_copy(dA2[N:P, :], dA[:]))
    VDA2 = sch.now("v")
    tev(lambda e: e.transpose(psB[0:1, 0:N], dA[:], bI[0:N, 0:N]), W(("v", VDA)))
    vev(lambda e: e.tensor_copy(srow[0:1, 0:N], psB[0:1, 0:N]), W(("t", sch.now("t"))))
    vev(lambda e: e.tensor_copy(srow[0:1, N:2 * N], srow[0:1, 0:N]))
    VROW = sch.now("v")
    pg["B"] = ("v", VROW)
    tev(lambda e: e.matmul(psA[:, 0:N], bONE[0:1, :], srow[0:1, 0:N], start=True, stop=True),
        W(("v", VROW), pg["A"]))
    tev(lambda e: e.matmul(psB[:, 0:2 * N], bONE[0:1, :], srow[0:1, 0:2 * N],
                           start=True, stop=True), W(pg["B"]))
    TGRID = sch.now("t")
    vev(lambda e: e.tensor_mul(tmpc[:, 0:N], bJ127[:], psA[:, 0:N]), W(("t", TGRID)))
    sev(lambda e: e.activation(V1[:], tmpc[:, 0:N], AT.Exp), W(("v", sch.now("v"))))
    for g in range(4):
        vev(lambda e, g=g: e.tensor_mul(tmpc[:], bJ2[:, g * P:(g + 1) * P], psB[:, 0:P]),
            W(("t", TGRID), ("s", sch.now("s"))))
        vev(lambda e, g=g: e.tensor_add(tmpc[:], tmpc[:], bM2[:, g * P:(g + 1) * P]))
        sev(lambda e, g=g: e.activation(V2[:, g * P:(g + 1) * P], tmpc[:], AT.Exp),
            W(("v", sch.now("v"))))
    for g in range(4):
        vev(lambda e, g=g: e.tensor_scalar(tmpc[:], bT2[:, g * P:(g + 1) * P], dA2[:, 0:1],
                                           None, ALU.mult),
            W(("v", VDA2), ("s", sch.now("s"))))
        vev(lambda e, g=g: e.tensor_add(tmpc[:], tmpc[:], bM2W[:, g * P:(g + 1) * P]))
        sev(lambda e, g=g: e.activation(W2s[:, g * P:(g + 1) * P], tmpc[:], AT.Exp),
            W(("v", sch.now("v"))))
    vev(lambda e: e.tensor_scalar(tmpd[:], bT1[:], dA[:, 0:1], None, ALU.mult),
        W(("s", sch.now("s"))))
    sev(lambda e: e.activation(W1Ts[:], tmpd[:], AT.Exp), W(("v", sch.now("v"))))
    vev(lambda e: e.tensor_scalar(tmpd[:, 0:M], bK16[:], dA[:, 0:1], None, ALU.mult),
        W(("s", sch.now("s"))))
    sev(lambda e: e.activation(P2m[:], tmpd[0:N, 0:M], AT.Exp), W(("v", sch.now("v"))))
    for h in range(2):
        tev(lambda e, h=h: e.matmul(psA[0:N, h * 512:(h + 1) * 512], bONE[0:1, 0:N],
                                    rfa[:, h * 512:(h + 1) * 512], start=True, stop=True),
            W(("v", sch.now("v"))) if h == 0 else W())
    vev(lambda e: e.tensor_mul(CtT[:], bCT[:], psA[0:N, :]), W(("t", sch.now("t"))))
    vev(lambda e: e.tensor_copy(Ct2[0:N, :], CtT[:]))
    vev(lambda e: e.tensor_copy(Ct2[N:P, :], CtT[:]))
    VCT = sch.now("v")
    pg["A"] = ("v", VCT)
    for h in range(2):
        tev(lambda e, h=h: e.matmul(psB[:, h * 512:(h + 1) * 512], bONE[0:1, :],
                                    rgam[:, h * 512:(h + 1) * 512], start=True, stop=True),
            W(("v", VCT)) if h == 0 else W())
    vev(lambda e: e.tensor_copy(gRep[:], psB[:]), W(("t", sch.now("t"))))
    for h in range(2):
        tev(lambda e, h=h: e.matmul(psB[:, h * 512:(h + 1) * 512], bONE[0:1, :],
                                    rbet[:, h * 512:(h + 1) * 512], start=True, stop=True),
            W(("v", sch.now("v"))) if h == 0 else W())
    vev(lambda e: e.tensor_copy(bRep[:], psB[:]), W(("t", sch.now("t"))))
    pg["B"] = ("v", sch.now("v"))
    vev(lambda e: e.tensor_mul(cDfa[:], cD[:], cfa[:]), W(("s_ld", LD0)))
    for dc in range(DCH):
        tev(lambda e, dc=dc: e.matmul(psA[:, dc * M:(dc + 1) * M], CtT[:, dc * P:(dc + 1) * P],
                                      P2m[:], start=True, stop=True),
            W(("s", sch.now("s")), pg["A"]) if dc == 0 else W())
    vev(lambda e: e.tensor_copy(KtT[:], psA[:, 0:DCH * M]), W(("t", sch.now("t"))))
    vev(lambda e: e.tensor_tensor(
        KtT.ap().rearrange("p (c k) -> p c k", k=M)[:, :, 0:1],
        KtT.ap().rearrange("p (c k) -> p c k", k=M)[:, :, 0:1],
        cDfa.ap().rearrange("p (c o) -> p c o", o=1), ALU.add))
    pg["A"] = ("v", sch.now("v"))

    w1s = w1T.rearrange("(a p) h -> a p h", p=P)
    w2s = w2T.rearrange("(a p) h -> a p h", p=P)
    tiles = [(0, a, cb) for a in range(DCH) for cb in range(2)] + \
            [(1, a, 0) for a in range(HCH)]
    tw = {0: 2048, 1: 1024}
    svals = []
    for ti, (wi, a, cb) in enumerate(tiles):
        slot = ti % 2
        wd = tw[wi]
        src = w1s[a, :, cb * 2048:(cb + 1) * 2048] if wi == 0 else w2s[a, :, :]
        guard = svals[ti - 2] if ti >= 2 else 0
        sch.ev("sy", dma(wbuf[:, slot * 2048:slot * 2048 + wd], src),
               W(("s", guard)), inc="s_pl%d" % slot, n=16)
        mi = ti
        sdone = sev(lambda e, slot=slot, wd=wd, mi=mi: e.activation(
            wdump[:, 0:wd], wbuf[:, slot * 2048:slot * 2048 + wd], AT.Abs,
            accum_out=mcols[:, mi:mi + 1]), W(("s_pl%d" % slot, sch.now("s_pl%d" % slot))))
        svals.append(sdone)
    SP1 = sch.now("s")
    for wi, (c0, c1) in [(0, (0, 16)), (1, (16, 48))]:
        vev(lambda e, wi=wi, c0=c0, c1=c1: e.reduce_sum(
            msum[:, wi:wi + 1], mcols[:, c0:c1], AX.X), W(("s", SP1)))
        tev(lambda e, wi=wi: e.matmul(psA[0:1, wi:wi + 1], msum[:, wi:wi + 1],
                                      bONE[:, 0:1], start=True, stop=True),
            W(("v", sch.now("v")), pg["A"]) )
        vev(lambda e, wi=wi: e.tensor_scalar(colm[0:1, 4 + wi:5 + wi], psA[0:1, wi:wi + 1],
                                             1.0 / (HC * D), 1e-5, ALU.mult, ALU.max),
            W(("t", sch.now("t"))))
        tev(lambda e, wi=wi: e.matmul(psA[0:P, 8 + wi:9 + wi] if False else psA[:, 8 + wi:9 + wi],
                                      bONE[0:1, :], colm[0:1, 4 + wi:5 + wi],
                                      start=True, stop=True), W(("v", sch.now("v"))))
        vev(lambda e, wi=wi: e.tensor_copy(mreps[:, 4 + wi:5 + wi], psA[:, 8 + wi:9 + wi]),
            W(("t", sch.now("t"))))
        vev(lambda e, wi=wi: e.reciprocal(mreps[:, wi:wi + 1], mreps[:, 4 + wi:5 + wi]))
        vev(lambda e, wi=wi: e.tensor_scalar(mreps[:, 2 + wi:3 + wi], mreps[:, 4 + wi:5 + wi],
                                             1.0 / 127.0, None, ALU.mult))
    VP1 = sch.now("v")
    pg["A"] = ("v", VP1)

    vev(lambda e: e.memset(mreps[:, 6:7], 192.0), W(("v", VP1)))
    vev(lambda e: e.memset(mreps[:, 7:8], -192.0))
    VMS = sch.now("v")
    vq_last = [VMS, VMS]
    sact1_last = [0, 0]
    for ti, (wi, a, cb) in enumerate(tiles):
        slot = ti % 2
        wd = tw[wi]
        src = w1s[a, :, cb * 2048:(cb + 1) * 2048] if wi == 0 else w2s[a, :, :]
        guard_s = sact1_last[slot]
        sch.ev("sy", dma(wbuf[:, slot * 2048:slot * 2048 + wd], src),
               W(("v", VMS), ("s", guard_s)), inc="s_pl%d" % slot, n=16)
        SYD = sch.now("s_pl%d" % slot)
        sev(lambda e, slot=slot, wd=wd, wi=wi: e.activation(
            wdump[:, slot * 2048:slot * 2048 + wd],
            wbuf[:, slot * 2048:slot * 2048 + wd], AT.Identity,
            bias=mreps[:, 6:7], scale=mreps[:, wi:wi + 1]),
            W(("s_pl%d" % slot, SYD), ("v", vq_last[slot])), lag=1)
        sact1_last[slot] = sch.now("s")
        SACT2 = sev(lambda e, slot=slot, wd=wd: e.activation(
            wdump[:, slot * 2048:slot * 2048 + wd],
            wdump[:, slot * 2048:slot * 2048 + wd], AT.Identity,
            bias=mreps[:, 7:8]))
        vq = vev(lambda e, slot=slot, wd=wd: e.tensor_scalar(
            wqb[:, slot * 2048:slot * 2048 + wd],
            wdump[:, slot * 2048:slot * 2048 + wd],
            -1.0, 1.0, ALU.max, ALU.min),
            W(("s", SACT2), ("s_ps%d" % slot, sch.now("s_ps%d" % slot))))
        vq_last[slot] = vq
        if wi == 0:
            dst = w1q8[16 * cb:16 * cb + 16, a].rearrange("a p c -> p a c")
        else:
            dst = w2q8[:, a].rearrange("b p c -> p b c")
        srcq = wqb.ap()[:, slot * 2048:slot * 2048 + wd].rearrange(
            "p (a c) -> p a c", c=P)
        sch.ev("g", dma(dst, srcq), W(("v", vq)), inc="s_ps%d" % slot, n=16)
    GPREP = sch.now("s_ps0") + sch.now("s_ps1")
    VPREP = sch.now("v")
    if DEBUG and False:
        sch.ev("g", dma(dbg["kt"][:, :], KtT[:]), W(("v", VPREP)), inc="gq", n=16)
        sch.ev("g", dma(dbg["mr"][:, :], mreps[:]), W(("v", VPREP)), inc="gq", n=16)

    store_cnt = [0]

    def dbgev(key, dst_slice, src_ap, waits):
        if DEBUG:
            sch.ev("g", dma(dst_slice, src_ap), waits, inc="gq", n=16)
            return ("gq", sch.now("gq"))
        return ("gq", 0)

    NX1S = ["s_x%d" % i for i in range(NX1)]
    NQTS = ["s_tq%d" % i for i in range(NQT)]
    x1_g = [[("v", 0)]] * NX1
    xq1_g = [("t", 0)] * NQT
    xq1_done = {}
    xhT_g = [("v", 0)]
    h1_g = [("t", 0)]
    PREPW = [("s_ps0", sch.now("s_ps0")), ("s_ps1", sch.now("s_ps1"))]
    first_ring = [True]
    tails_cnt = [0]
    probe_hits = [0]

    def stage_A(s, ci):
        nonlocal tmpa_g
        t0 = ci * P
        slot = ci % 2
        qc = ci % 4
        gi = s * CH + ci
        xs = gi % NX1
        prev = gi - NX1
        xw = [("s_o%d" % (prev % 2), 16 * (prev // 2 + 1))] if prev >= 0 else []
        sch.ev("g", dma(x1[:, xs * D:(xs + 1) * D], u[s, t0:t0 + P, :]),
               W(*xw), inc=NX1S[xs], n=16)
        SYX = sch.now(NX1S[xs])
        vev(lambda e, xs=xs: e.bn_stats(stat6[:, 0:6], x1[:, xs * D:xs * D + 512]),
            W((NX1S[xs], SYX)))
        vev(lambda e, xs=xs: e.bn_stats(stat6[:, 6:12],
                                        x1[:, xs * D + 512:(xs + 1) * D]), lag=1)
        vev(lambda e: e.bn_aggr(stat2[:], stat6[:]))
        sev(lambda e: e.activation(colm[:, 1:2], stat2[:, 1:2], AT.Sqrt,
                                   bias=ceps[:, 0:1]), W(("v", sch.now("v"))), lag=1)
        vev(lambda e: e.reciprocal(colm[:, 1:2], colm[:, 1:2]), W(("s", sch.now("s"))))
        vxh = vev(lambda e, xs=xs, slot=slot: e.tensor_scalar(
            xh[:, slot * D:(slot + 1) * D], x1[:, xs * D:(xs + 1) * D],
            stat2[:, 0:1], colm[:, 1:2], ALU.subtract, ALU.mult), xh_g[slot])
        sch.ev("s", lambda e, slot=slot: e.dma_start_transpose(
            tmpa.ap().rearrange("p (c t) -> p c t", c=DCH),
            xh.ap()[:, slot * D:(slot + 1) * D].rearrange("p (c t) -> p c t", c=DCH)),
            W(("v", vxh), tmpa_g, tmpb_g), inc="s_txh", n=16)
        SQXH = sch.now("s_txh")
        vts = vev(lambda e, qc=qc: e.tensor_copy(
            xhT.ap().rearrange("p (dc t) -> p dc t", dc=DCH)[:, :, qc * P:(qc + 1) * P],
            tmpa.ap().rearrange("p (dc t) -> p dc t", dc=DCH)),
            W(("s_txh", SQXH), *xhT_g))
        tmpa_g = ("v", vts)
        for g in range(4):
            ps, nm = (psA, "A") if g % 2 == 0 else (psB, "B")
            tev(lambda e, g=g, ps=ps, slot=slot: e.matmul(
                ps[:, 0:512], V2[:, g * P:(g + 1) * P],
                xh[:, slot * D:slot * D + 512], start=True, stop=True),
                W(("v", vxh), pg[nm]))
            TG2 = tev(lambda e, g=g, ps=ps, slot=slot: e.matmul(
                ps[:, 512:1024], V2[:, g * P:(g + 1) * P],
                xh[:, slot * D + 512:(slot + 1) * D], start=True, stop=True))
            SG = sev(lambda e, g=g, ps=ps: e.copy(G2C[:, g * D:(g + 1) * D], ps[:]),
                     W(("t", TG2), G2C_g[0] if g == 0 else ("v", 0)), lag=3)
            pg[nm] = ("s", SG)
        SG2 = sch.now("s")
        for g in range(4):
            vev(lambda e, g=g: e.tensor_mul(
                G2C[:, g * D:(g + 1) * D], G2C[:, g * D:(g + 1) * D], Ct2[:]),
                W(("s", SG2)) if g == 0 else W(), lag=3)
        VG2C = sch.now("v")
        for hh in range(2):
            tev(lambda e, hh=hh, slot=slot: e.matmul(
                psA[0:N, hh * 512:(hh + 1) * 512], V1[:],
                xh[:, slot * D + hh * 512:slot * D + (hh + 1) * 512],
                start=True, stop=True), W(pg["A"]) if hh == 0 else W())
        TG1 = sch.now("t")
        xh_g[slot] = [("t", TG1), ("s_txh", SQXH)]
        if ci == 0:
            vev(lambda e: e.memset(h1[:], 0.0), W(h1_g[0]))
        vev(lambda e: e.tensor_mul(hC1[:], h1[:], CtT[:]), W(("t", TG1)))
        vev(lambda e: e.scalar_tensor_tensor(h1[:], h1[:], rS[:, 0:1], psA[0:N, :],
                                             ALU.mult, ALU.add))
        pg["A"] = ("v", sch.now("v"))
        VH = sch.now("v")
        for hh in range(2):
            tev(lambda e, hh=hh: e.matmul(
                psB[:, hh * 512:(hh + 1) * 512], W1Ts[:],
                hC1[:, hh * 512:(hh + 1) * 512], start=True, stop=False),
                W(("v", VH), pg["B"]) if hh == 0 else W())
        for g in range(4):
            for hh in range(2):
                tev(lambda e, g=g, hh=hh: e.matmul(
                    psB[:, hh * 512:(hh + 1) * 512], W2s[:, g * P:(g + 1) * P],
                    G2C[:, g * D + hh * 512:g * D + (hh + 1) * 512],
                    start=False, stop=(g == 3)),
                    W(("v", VG2C)) if (g == 0 and hh == 0) else W())
        TY = sch.now("t")
        G2C_g[0] = ("t", TY)
        h1_g[0] = ("t", TY)
        yg = y2p_g[qc] if isinstance(y2p_g[qc], list) else [y2p_g[qc]]
        SY2 = sev(lambda e, qc=qc: e.copy(y2p[:, qc * D:(qc + 1) * D], psB[:]),
                  W(("t", TY), *yg), lag=1)
        y2s[qc] = SY2
        pg["B"] = ("s", SY2)

    def taps(qtr_base_done):
        def tapap(buf, dc, k, hi):
            a = buf.ap()[:, dc * 512:(dc + 1) * 512].rearrange("p (g i) -> p g i", i=M)
            return a[:, :, k:hi]
        for dc in range(DCH):
            vev(lambda e, dc=dc: e.tensor_scalar(
                tapap(accb, dc, 0, M), tapap(xhT, dc, 0, M),
                KtT[:, dc * M:dc * M + 1], None, ALU.mult),
                W(intraT_g[0], intraT_g[1]) if dc == 0 else W(), lag=7)
        for k in range(1, M):
            for dc in range(DCH):
                vev(lambda e, k=k, dc=dc: e.scalar_tensor_tensor(
                    tapap(accb, dc, k, M), tapap(xhT, dc, 0, M - k),
                    KtT[:, dc * M + k:dc * M + k + 1],
                    tapap(accb, dc, k, M), ALU.mult, ALU.add), lag=7)
        VT = sch.now("v")
        xhT_g[0] = ("v", VT)
        return VT

    def stage_C(s, ci, VTAPS):
        nonlocal tmpb_g, xq1t_g
        qc = ci % 4
        isl = 0
        gi = s * CH + ci
        xs = gi % NX1
        qs = gi % NQT
        vgb = vev(lambda e, qc=qc: e.tensor_copy(
            tmpb.ap().rearrange("p (dc t) -> p dc t", dc=DCH),
            accb.ap().rearrange("p (dc t) -> p dc t", dc=DCH)[:, :, qc * P:(qc + 1) * P]),
            W(("v", VTAPS), tmpb_g, tmpa_g))
        itg = [intraT_g[isl]]
        sch.ev("s", lambda e, isl=isl: e.dma_start_transpose(
            intraT.ap().rearrange("p (c t) -> p c t", c=DCH),
            tmpb.ap().rearrange("p (c t) -> p c t", c=DCH)),
            W(("v", vgb), *itg), inc="s_ti%d" % isl, n=16)
        SQI = sch.now("s_ti%d" % isl)
        tmpb_g = ("s_ti%d" % isl, SQI)
        vy = vev(lambda e, qc=qc: e.tensor_add(
            y2t[:], y2p[:, qc * D:(qc + 1) * D], intraT[:]),
            W(("s_ti%d" % isl, SQI), ("s", y2s[qc]),
              *(xq1t_g if isinstance(xq1t_g, list) else [xq1t_g])))
        y2p_g[qc] = ("v", vy)
        intraT_g[isl] = ("v", vy)
        if ci == 0:
            vev(lambda e: e.tensor_add(y2t[0:1, :], y2t[0:1, :], rrew[:]),
                W(("s_rr", 16)))
        vev(lambda e: e.bn_stats(stat6[:, 0:6], y2t[:, 0:512]))
        vev(lambda e: e.bn_stats(stat6[:, 6:12], y2t[:, 512:1024]), lag=1)
        vev(lambda e: e.bn_aggr(stat2[:], stat6[:]))
        sev(lambda e: e.activation(colm[:, 1:2], stat2[:, 1:2], AT.Sqrt,
                                   bias=ceps[:, 0:1]), W(("v", sch.now("v"))), lag=1)
        vev(lambda e: e.reciprocal(colm[:, 1:2], colm[:, 1:2]), W(("s", sch.now("s"))))
        vev(lambda e: e.tensor_scalar(y2t[:], y2t[:], stat2[:, 0:1], colm[:, 1:2],
                                      ALU.subtract, ALU.mult))
        vev(lambda e: e.tensor_mul(y2t[:], y2t[:], gRep[:]))
        vev(lambda e: e.tensor_add(y2t[:], y2t[:], bRep[:]))
        vev(lambda e, xs=xs: e.tensor_add(x1[:, xs * D:(xs + 1) * D],
                                          x1[:, xs * D:(xs + 1) * D], y2t[:]))
        if DEBUG:
            sch.ev("g", dma(dbg["x1"][gi * P:(gi + 1) * P, :],
                            x1[:, xs * D:(xs + 1) * D]),
                   W(("v", sch.now("v"))), inc="gq", n=16)
        vev(lambda e, xs=xs: e.bn_stats(stat6[:, 0:6], x1[:, xs * D:xs * D + 512]))
        vev(lambda e, xs=xs: e.bn_stats(stat6[:, 6:12],
                                        x1[:, xs * D + 512:(xs + 1) * D]), lag=1)
        vev(lambda e: e.bn_aggr(stat2[:], stat6[:]))
        sev(lambda e: e.activation(colm[:, 1:2], stat2[:, 1:2], AT.Sqrt,
                                   bias=ceps[:, 0:1]), W(("v", sch.now("v"))), lag=1)
        vev(lambda e: e.reciprocal(colm[:, 1:2], colm[:, 1:2]), W(("s", sch.now("s"))))
        vev(lambda e, xs=xs: e.tensor_scalar(y2t[:], x1[:, xs * D:(xs + 1) * D],
                                             stat2[:, 0:1], colm[:, 1:2],
                                             ALU.subtract, ALU.mult))
        vev(lambda e: e.reduce_max(colm[:, 2:3], y2t[:], AX.X,
                                   apply_absolute_value=True))
        vev(lambda e: e.tensor_scalar(colm[:, 2:3], colm[:, 2:3], 1e-5, None, ALU.max))
        vev(lambda e: e.reciprocal(colm[:, 3:4], colm[:, 2:3]))
        vev(lambda e: e.tensor_scalar(colm[:, 3:4], colm[:, 3:4], 127.0, None, ALU.mult))
        vev(lambda e: e.tensor_scalar(colm[:, 4:5], colm[:, 2:3], mreps[:, 2:3],
                                      None, ALU.mult))
        xqg = xq1t_g if isinstance(xq1t_g, list) else [xq1t_g]
        vev(lambda e: e.tensor_scalar(xq1t[:], y2t[:], colm[:, 3:4], 1536.0,
                                      ALU.mult, ALU.add), W(*xqg))
        vev(lambda e: e.tensor_scalar(xq1t[:], xq1t[:], 1536.0, None, ALU.subtract))
        vxq = vev(lambda e: e.tensor_scalar(y2t[:], xq1t[:], colm[:, 4:5], None,
                                            ALU.mult))
        prevw = (gi - NQT) // 4
        qwg = [("s_mmw", prevw + 1)] if gi >= NQT else []
        sch.ev("s", lambda e, qs=qs: e.dma_start_transpose(
            xq1T.ap()[:, qs * D:(qs + 1) * D].rearrange("p (c t) -> p c t", c=DCH),
            y2t.ap().rearrange("p (c t) -> p c t", c=DCH)),
            W(("v", vxq), *qwg), inc=NQTS[qs], n=16)
        xq1_done[gi] = (NQTS[qs], sch.now(NQTS[qs]))
        xq1t_g = [(NQTS[qs], sch.now(NQTS[qs]))]

    def ssm_half(s, h):
        for q in range(2):
            for c4 in range(4):
                gi = s * CH + h * 8 + q * 4 + c4
                if gi >= NX1:
                    yield ("tails", gi - NX1 + 1)
                stage_A(s, h * 8 + q * 4 + c4)
                yield
            VT = taps(None)
            yield
            for c4 in range(4):
                stage_C(s, h * 8 + q * 4 + c4, VT)
                yield
            yield ("qdone", s * 4 + h * 2 + q + 1)

    def ffn_seq(s):
        nonlocal acc2_g, amb_g, ffT_g, s2t_g

        def conv_products(cch, SPS):
            b0 = cch * ST
            sl = cch % 2
            w_c = W(("s", SPS), acc2_g if cch == 0 else ("v", 0), ubuf_g[sl])
            vev(lambda e, b0=b0, sl=sl, cch=cch: e.tensor_scalar(
                ubuf[:, sl * 512:sl * 512 + 512], hsl[:, b0:b0 + 512],
                wcv[:, cch * 4 + 0:cch * 4 + 1], None, ALU.mult), w_c)
            vev(lambda e, b0=b0, sl=sl, cch=cch: e.tensor_scalar(
                vbuf[:, sl * 512:sl * 512 + 512], hsl[:, b0 + 2:b0 + 514],
                wcv[:, cch * 4 + 2:cch * 4 + 3], None, ALU.mult), lag=1)
            cg = vev(lambda e, b0=b0, sl=sl, cch=cch: e.tensor_scalar(
                cbuf[:, sl * 512:sl * 512 + 512], hsl[:, b0 + 1:b0 + 513],
                wcv[:, cch * 4 + 1:cch * 4 + 2], wcv[:, cch * 4 + 3:cch * 4 + 4],
                ALU.mult, ALU.add), lag=2)
            conv_g[cch] = ("v", cg)
            return cg

        def conv_mms(cch, ps, evS, cg):
            a0 = cch * WIN
            sl = cch % 2
            tev(lambda e, ps=ps, sl=sl: e.matmul(
                ps[:], bIb[:], cbuf[:, sl * 512:sl * 512 + 512],
                start=True, stop=False), W(("v", cg), ("s", evS)))
            tev(lambda e, ps=ps, sl=sl: e.matmul(
                ps[:], bIb[:], ubuf[:, sl * 512:sl * 512 + 512],
                start=False, stop=False))
            TCV = tev(lambda e, ps=ps, sl=sl: e.matmul(
                ps[:], bIb[:], vbuf[:, sl * 512:sl * 512 + 512],
                start=False, stop=True))
            ubuf_g[sl] = ("t", TCV)
            SSI = sev(lambda e, ps=ps, a0=a0: e.activation(
                acc2[:, a0:a0 + WIN], ps[:], AT.Silu), W(("t", TCV)), lag=1)
            sch.mark("SSI s=%d cch=%d" % (s, cch))
            if DEBUG and DBGPROBE is not None and (s, cch) == DBGPROBE[:2]:
                if DBGPROBE[2] == probe_hits[0]:
                    sch.ev("sy", dma(dbg["probe"][:, :], acc2[:, a0:a0 + WIN]),
                           W(("s", SSI)), inc="dbq", n=16)
                probe_hits[0] += 1
            return SSI

        def mm1_cch(win, cch):
            sch.mark("MM1 s=%d win=%d cch=%d" % (s, win, cch))
            rh = cch % 3
            extra = PREPW if first_ring[0] else []
            first_ring[0] = False
            sch.ev("sy", dma(
                w1r.ap()[:, rh * 8 * P:(rh + 1) * 8 * P].rearrange(
                    "p (b c) -> p b c", b=DCH),
                w1q8[cch].rearrange("b p c -> p b c")),
                W(w1r_g[rh], *extra), inc="s_w1%d" % rh, n=16)
            SYW = sch.now("s_w1%d" % rh)
            ps, nm = (psM1a, "M1a") if cch % 2 == 0 else (psM1b, "M1b")
            base = ((s * CH + win * 4) % NQT)
            qw = [xq1_done[s * CH + win * 4 + j] for j in range(4)]
            for dc in range(DCH):
                mvap = xq1T.ap().rearrange("p (qs dc t) -> p qs dc t", qs=NQT,
                                           dc=DCH)[:, base:base + 4, dc, :]
                tev(lambda e, ps=ps, rh=rh, dc=dc, mvap=mvap: e.matmul(
                    ps[:], w1r[:, (rh * 8 + dc) * P:(rh * 8 + dc + 1) * P],
                    mvap, start=(dc == 0), stop=(dc == DCH - 1)),
                    W(("s_w1%d" % rh, SYW), *qw, pg[nm]) if dc == 0 else W())
            TMM = sch.now("t")
            w1r_g[rh] = ("t", TMM)
            if cch == HCH - 1:
                sch.ev("t", lambda e: e.nop(), W(), inc="s_mmw", n=1)
            b0 = cch * ST
            if win > 0:
                SPS = sev(lambda e, ps=ps, b0=b0: e.copy(hsl[:, b0 + 513:b0 + 514],
                                                         ps[:, 0:1]), W(("t", TMM)),
                          lag=1)
                cg = conv_products(cch, SPS)
                sev(lambda e, b0=b0: e.copy(hsl[:, b0:b0 + 1],
                                            hsl[:, b0 + 512:b0 + 513]),
                    W(("v", cg)))
                SE = sev(lambda e, ps=ps, b0=b0: e.copy(hsl[:, b0 + 1:b0 + 513],
                                                        ps[:]), W(("v", cg)))
                pg[nm] = ("s", conv_mms(cch, ps, SE, cg))
            else:
                cgl = conv_g[cch] if isinstance(conv_g[cch], list) else [conv_g[cch]]
                SE = sev(lambda e, ps=ps, b0=b0: e.copy(hsl[:, b0 + 1:b0 + 513],
                                                        ps[:]), W(("t", TMM), *cgl))
                pg[nm] = ("s", SE)

        def finish_window(win):
            nonlocal amb_g, ffT_g, acc2_g, s2t_g
            t0 = win * WIN
            SS = sch.now("s")
            sch.mark("FW s=%d win=%d SS=%d" % (s, win, SS))
            if DEBUG and (s, win) == DBGWIN:
                sch.ev("sy", dma(dbg["acc2"][:, :], acc2[:]), W(("s", SS)),
                       inc="dbq", n=16)
                sch.ev("sy", dma(dbg["hsl"][:, :], hsl[:]), W(("s", SS)),
                       inc="dbq", n=16)
            vev(lambda e: e.tensor_tensor(maxt[:], acc2[:, 0:WIN], acc2[:, WIN:2 * WIN],
                                          ALU.max), W(("s", SS), amb_g))
            vev(lambda e: e.tensor_tensor(mint[:], acc2[:, 0:WIN], acc2[:, WIN:2 * WIN],
                                          ALU.min), lag=1)
            for cch in range(2, HCH):
                vev(lambda e, cch=cch: e.tensor_tensor(
                    maxt[:], maxt[:], acc2[:, cch * WIN:(cch + 1) * WIN], ALU.max),
                    lag=1)
                vev(lambda e, cch=cch: e.tensor_tensor(
                    mint[:], mint[:], acc2[:, cch * WIN:(cch + 1) * WIN], ALU.min),
                    lag=1)
            yield
            vev(lambda e: e.tensor_scalar(mint[:], mint[:], -1.0, None, ALU.mult))
            vam = vev(lambda e: e.tensor_tensor(mint[:], maxt[:], mint[:], ALU.max),
                      W(*(amb_g if isinstance(amb_g, list) else [amb_g])))
            sch.ev("s", lambda e: e.dma_start_transpose(
                amT.ap().rearrange("p (b c) -> p b c", b=4),
                mint.ap().rearrange("p (b c) -> p b c", b=4)),
                W(("v", vam)), inc="s_tam", n=16)
            SQAM = sch.now("s_tam")
            amb_g = ("s_tam", SQAM)
            for tt in range(4):
                vev(lambda e, tt=tt: e.reduce_max(s2col[:, tt:tt + 1],
                                                  amT[:, tt * P:(tt + 1) * P], AX.X),
                    W(("s_tam", SQAM), s2t_g) if tt == 0 else W(),
                    lag=0 if tt == 0 else 1)
                vev(lambda e, tt=tt: e.tensor_scalar(s2col[:, tt:tt + 1],
                                                     s2col[:, tt:tt + 1], 1e-5, None,
                                                     ALU.max))
                vev(lambda e, tt=tt: e.tensor_scalar(s2cb[:, tt:tt + 1],
                                                     s2col[:, tt:tt + 1], mreps[:, 3:4],
                                                     None, ALU.mult))
                vev(lambda e, tt=tt: e.reciprocal(s2col[:, tt:tt + 1],
                                                  s2col[:, tt:tt + 1]))
                vev(lambda e, tt=tt: e.tensor_scalar(s2col[:, tt:tt + 1],
                                                     s2col[:, tt:tt + 1], 127.0, None,
                                                     ALU.mult))
            VS2 = sch.now("v")
            if DEBUG and (s, win) == DBGWIN:
                sch.ev("sy", dma(dbg["scol"][:, 0:4], s2col[:]), W(("v", VS2)),
                       inc="dbq", n=16)
                sch.ev("sy", dma(dbg["scol"][:, 4:8], s2cb[:]), W(("v", VS2)),
                       inc="dbq", n=16)
            for tt in range(4):
                tev(lambda e, tt=tt: e.transpose(psM2a[0:1, tt * P:(tt + 1) * P],
                                                 s2col[:, tt:tt + 1], bI[:]),
                    W(("v", VS2), pg["M2a"]) if tt == 0 else W())
            s2t_g = ("t", sch.now("t"))
            vev(lambda e: e.tensor_copy(srow[:], psM2a[0:1, 0:WIN]),
                W(("t", sch.now("t"))))
            pg["M2a"] = ("v", sch.now("v"))
            tev(lambda e: e.matmul(psM2b[:], bONE[0:1, :], srow[:], start=True,
                                   stop=True), W(("v", sch.now("v")), pg["M2b"]))
            vev(lambda e: e.tensor_copy(s2rep[:], psM2b[:]), W(("t", sch.now("t"))))
            pg["M2b"] = ("v", sch.now("v"))
            VREP = sch.now("v")
            bcast = bass.AP(tensor=s2rep.ap().tensor, offset=s2rep.ap().offset,
                            ap=[list(s2rep.ap().ap[0]), [0, HCH], [1, WIN]])
            vev(lambda e, bcast=bcast: e.tensor_tensor(
                acc2.ap().rearrange("p (c t) -> p c t", c=HCH),
                acc2.ap().rearrange("p (c t) -> p c t", c=HCH), bcast, ALU.mult),
                W(("v", VREP)))
            VQ2 = vev(lambda e: e.tensor_scalar(acc2[:], acc2[:], MAGIC, MAGIC,
                                                ALU.add, ALU.subtract))
            yield
            def w2load(hh):
                rh = hh % 3
                sch.ev("sy", dma(
                    w2r.ap()[:, rh * 16 * P:(rh + 1) * 16 * P].rearrange(
                        "p (a c) -> p a c", a=16),
                    w2q8[hh // 2, (hh % 2) * 16:(hh % 2) * 16 + 16].rearrange(
                        "a p c -> p a c")),
                    W(w2r_g[rh], *PREPW), inc="s_w2%d" % rh, n=16)
            w2load(0)
            w2load(1)
            w2load(2)
            for dc in range(DCH):
                rhA, rhB = (2 * dc) % 3, (2 * dc + 1) % 3
                GQA = sch.now("s_w2%d" % rhA)
                GQB = sch.now("s_w2%d" % rhB)
                ps, nm = (psM2a, "M2a") if dc % 2 == 0 else (psM2b, "M2b")
                for cch in range(HCH):
                    rh = rhA if cch < 16 else rhB
                    ti = cch % 16
                    tev(lambda e, ps=ps, rh=rh, ti=ti, cch=cch: e.matmul(
                        ps[:], w2r[:, (rh * 16 + ti) * P:(rh * 16 + ti + 1) * P],
                        acc2[:, cch * WIN:(cch + 1) * WIN],
                        start=(cch == 0), stop=(cch == HCH - 1)),
                        W(("s_w2%d" % rhA, GQA), ("s_w2%d" % rhB, GQB),
                          ("v", VQ2), pg[nm]) if cch == 0 else W())
                TM2 = sch.now("t")
                w2r_g[rhA] = ("t", TM2)
                w2r_g[rhB] = ("t", TM2)
                if 2 * dc + 3 < 2 * DCH:
                    w2load(2 * dc + 3)
                if 2 * dc + 4 < 2 * DCH:
                    w2load(2 * dc + 4)
                fsl = dc % 2
                SF = sev(lambda e, ps=ps, fsl=fsl: e.copy(
                    ffw[:, fsl * WIN:(fsl + 1) * WIN], ps[:]),
                    W(("t", TM2), ffw_g[fsl]))
                pg[nm] = ("s", SF)
                sch.ev("s", lambda e, dc=dc, fsl=fsl: e.dma_start_transpose(
                    ffT.ap()[:, dc * WIN:(dc + 1) * WIN].rearrange("p (b c) -> p b c",
                                                                   b=4),
                    ffw.ap()[:, fsl * WIN:(fsl + 1) * WIN].rearrange("p (b c) -> p b c",
                                                                     b=4)),
                    W(("s", SF), ffT_g if dc == 0 else ("v", 0)),
                    inc="s_tf%d" % fsl, n=16)
                ffw_g[fsl] = ("s_tf%d" % fsl, sch.now("s_tf%d" % fsl))
                if dc % 4 == 3:
                    yield
            SQF = None
            QFW = [("s_tf0", sch.now("s_tf0")), ("s_tf1", sch.now("s_tf1"))]
            if DEBUG and (s, win) == DBGWIN:
                sch.ev("sy", dma(dbg["ffw"][:, :], ffT[:]), W(*QFW), inc="dbq", n=16)
            for tt in range(4):
                gi = s * CH + win * 4 + tt
                xs = gi % NX1
                osl = (win * 4 + tt) % 2
                ftap = ffT.ap().rearrange("p (dc tt t) -> p dc tt t", dc=DCH, tt=4)[
                    :, :, tt, :]
                vo = vev(lambda e, tt=tt, xs=xs, osl=osl, ftap=ftap: e.scalar_tensor_tensor(
                    outt[:, osl * D:(osl + 1) * D], ftap,
                    s2cb[:, tt:tt + 1], x1[:, xs * D:(xs + 1) * D], ALU.mult, ALU.add),
                    W(*QFW, outt_g[osl]), lag=1)
                sch.ev("s", lambda e, s=s, t0=t0, tt=tt, osl=osl: e.dma_start(
                    out=out[s, t0 + tt * P:t0 + (tt + 1) * P, :],
                    in_=outt[:, osl * D:(osl + 1) * D]), W(("v", vo)),
                    inc="s_o%d" % osl, n=16)
                outt_g[osl] = ("s_o%d" % osl, sch.now("s_o%d" % osl))
                x1_g[xs] = [("s_o%d" % osl, sch.now("s_o%d" % osl))]
                tails_cnt[0] += 1
            ffT_g = ("v", sch.now("v"))
            acc2_g = ("t", sch.now("t"))

        for w in range(NW):
            yield ("q", s * 4 + w)
            if w == 0 and s == 0:
                vev(lambda e: e.memset(
                    hsl.ap().rearrange("p (c k) -> p c k", k=ST)[:, :, 0:1], 0.0),
                    W(*PREPW))
            elif w == 0:
                vev(lambda e: e.memset(
                    hsl.ap().rearrange("p (c k) -> p c k", k=ST)[:, :, 0:1], 0.0),
                    W(*(conv_g[HCH - 1] if isinstance(conv_g[HCH - 1], list)
                        else [conv_g[HCH - 1]])))
            for cch in range(HCH):
                mm1_cch(w, cch)
                if cch % 4 == 3:
                    yield
            if w >= 1:
                for _ in finish_window(w - 1):
                    yield
        vev(lambda e: e.memset(
            hsl.ap().rearrange("p (c k) -> p c k", k=ST)[:, :, 513:514], 0.0),
            W(("s", sch.now("s"))))
        for cch in range(HCH):
            ps, nm = (psM1a, "M1a") if cch % 2 == 0 else (psM1b, "M1b")
            cg = conv_products(cch, 0)
            pg[nm] = ("s", conv_mms(cch, ps, pg[nm][1] if pg[nm][0] == "s" else 0, cg))
            if cch % 8 == 7:
                yield
        for _ in finish_window(NW - 1):
            yield

    ssm_gens = [(s * 2 + h, ssm_half(s, h)) for s in range(BC) for h in range(2)]
    fgens = [ffn_seq(s) for s in range(BC)]
    fneed = []
    for g in fgens:
        r = next(g)
        fneed.append(r[1] if isinstance(r, tuple) else None)
    quarters_done = 0
    fcur = 0
    sidx = 0
    sreq = [None]

    def ffn_tick():
        nonlocal fcur
        if fcur >= BC:
            return False
        if fneed[fcur] is not None and fneed[fcur] >= quarters_done:
            return False
        try:
            r = next(fgens[fcur])
            fneed[fcur] = r[1] if isinstance(r, tuple) else None
        except StopIteration:
            fcur += 1
        return True

    def ssm_tick():
        nonlocal sidx, quarters_done
        if sidx >= len(ssm_gens):
            return False
        if sreq[0] is not None and tails_cnt[0] < sreq[0]:
            return False
        hid, g = ssm_gens[sidx]
        try:
            r = next(g)
            if isinstance(r, tuple):
                if r[0] == "qdone":
                    quarters_done = max(quarters_done, r[1])
                    sreq[0] = None
                else:
                    sreq[0] = r[1]
            else:
                sreq[0] = None
        except StopIteration:
            sidx += 1
            sreq[0] = None
        return True

    while sidx < len(ssm_gens) or fcur < BC:
        p1 = ssm_tick()
        p2 = ffn_tick()
        p4 = ssm_tick()
        p3 = ffn_tick()
        if not (p1 or p2 or p3 or p4):
            raise RuntimeError("pipeline deadlock: sidx=%d sreq=%s fcur=%d fneed=%s "
                               "quarters=%d tails=%d" % (sidx, sreq[0], fcur,
                                                         fneed, quarters_done,
                                                         tails_cnt[0]))

    sch.ev("g", lambda e: e.engine_nop(),
           W(("s_o0", sch.now("s_o0")), ("s_o1", sch.now("s_o1"))))

    sem_names = sorted(set(list(sch.cnt.keys()) +
                           ["v", "s", "t"] +
                           [w[0] for (_e, ws, _f, _i, _n, _l) in sch.events for w in ws]))
    import contextlib
    with contextlib.ExitStack() as stack:
        sems = {nm: stack.enter_context(nc.semaphore(nm)) for nm in sem_names}
        blk = stack.enter_context(nc.Block())

        @blk.sync
        def _(e):
            sch.emit("sy", e, sems)

        @blk.vector
        def _(e):
            sch.emit("v", e, sems, delay=(SERIAL == 11))

        @blk.scalar
        def _(e):
            sch.emit("s", e, sems, delay=(SERIAL == 11))

        @blk.tensor
        def _(e):
            sch.emit("t", e, sems)

        @blk.gpsimd
        def _(e):
            sch.emit("g", e, sems)

    globals()['_last_sch'] = sch
    return nc


def kernel(L0_attn_u, L0_attn_A_log, L0_attn_B, L0_attn_C, L0_attn_D,
           L0_attn_delta, L0_attn_gamma, L0_attn_beta, L0_attn_freq_align,
           L0_attn_reward, L0_ffn_w1, L0_ffn_conv_w, L0_ffn_conv_b, L0_ffn_w2):
    hc = host_consts()
    nc = build_program()
    f32 = lambda a: np.ascontiguousarray(np.asarray(a, dtype=np.float32))
    base = {
        "w1T": f32(np.asarray(L0_ffn_w1).T),
        "w2T": f32(np.asarray(L0_ffn_w2).T),
        "CT": f32(np.asarray(L0_attn_C).T),
        "alog": f32(L0_attn_A_log), "delta": f32(L0_attn_delta),
        "Dvec": f32(L0_attn_D), "fa": f32(L0_attn_freq_align),
        "gamma": f32(L0_attn_gamma), "beta": f32(L0_attn_beta),
        "reward": f32(L0_attn_reward),
        "convw": f32(np.asarray(L0_ffn_conv_w).reshape(HC, 3)),
        "convb": f32(L0_ffn_conv_b),
    }
    base.update(hc)
    uu = f32(L0_attn_u)
    in_maps = []
    for c in range(8):
        m = dict(base)
        m["u"] = np.ascontiguousarray(uu[2 * c:2 * c + 2])
        in_maps.append(m)
    res = run_bass_kernel_spmd(nc, in_maps, core_ids=list(range(8)))
    return np.concatenate([r["out"] for r in res.results], axis=0)


# version 22 (final)
# speedup vs baseline: 1.0595x; 1.0103x over previous
import numpy as np
import concourse.bass as bass
import concourse.mybir as mybir
from concourse.bass_utils import run_bass_kernel_spmd

F32 = mybir.dt.float32
BF16 = mybir.dt.bfloat16
F16 = mybir.dt.float16
F8 = mybir.dt.float8e4
AT = mybir.ActivationFunctionType
ALU = mybir.AluOpType
AX = mybir.AxisListType

MAGIC = 12582912.0
NEGBIG = -1.0e30
EPS = 1e-5

BC, L, D, DCH, N, M, CH = 2, 2048, 1024, 8, 64, 16, 16
HC, HCH, WIN, NW, P = 4096, 32, 512, 4, 128
ST = 514

_CONST_CACHE = {}
DEBUG = False
SERIAL = False
DBGWIN = (0, 1)
DBGPROBE = None
BLEVEL = 0
SLO, SHI = 0, 10 ** 9
SELFWAIT = True
RACECHECK = False


def host_consts():
    if _CONST_CACHE:
        return _CONST_CACHE
    c = {}
    j = np.arange(P)
    c["cJ127"] = np.broadcast_to((127.0 - j)[:, None], (P, N)).astype(np.float32).copy()
    c["cT1"] = np.broadcast_to((j + 1.0)[None, :], (N, P)).astype(np.float32).copy()
    c["cK16"] = np.broadcast_to(np.arange(M, dtype=np.float64)[None, :], (N, M)).astype(np.float32).copy()
    J2 = np.zeros((4, P, P), np.float32)
    M2 = np.full((4, P, P), NEGBIG, np.float32)
    for g in range(4):
        for jj in range(P):
            q, jl = jj // M, jj % M
            for e in range(2):
                if q == 2 * g + e:
                    J2[g, jj, e * N:(e + 1) * N] = 15.0 - jl
                    M2[g, jj, e * N:(e + 1) * N] = 0.0
    c["cJ2"], c["cM2"] = J2.reshape(4 * P, P).copy(), M2.reshape(4 * P, P).copy()
    T2 = np.zeros((4, P, P), np.float32)
    M2W = np.full((4, P, P), NEGBIG, np.float32)
    for pp in range(4):
        for e in range(2):
            qp = 2 * pp + e
            lo = M * (qp + 1)
            for t in range(lo, P):
                T2[pp, e * N:(e + 1) * N, t] = t - lo + 1.0
                M2W[pp, e * N:(e + 1) * N, t] = 0.0
    c["cT2"], c["cM2W"] = T2.reshape(4 * P, P).copy(), M2W.reshape(4 * P, P).copy()
    c["cI"] = np.eye(P, dtype=np.float32)
    c["cONE"] = np.ones((P, P), np.float32)
    _CONST_CACHE.update(c)
    return c


class Sched:

    def __init__(self):
        self.events = []
        self.cnt = {}
        self.pending = {}
        self.marks = []

    def mark(self, name):
        self.marks.append((name, len(self.events), dict(self.cnt)))

    def barrier(self, engines=("sy", "v", "s", "t", "g")):
        snap = [(s, v) for s, v in self.cnt.items() if v > 0]
        for eng in engines:
            self.pending[eng] = list(snap)

    def now(self, sem):
        return self.cnt.get(sem, 0)

    def ev(self, eng, fn, waits=(), inc=None, n=1, lag=0):
        if SERIAL == 1:
            w = [(s, v) for s, v in self.cnt.items() if v > 0]
        elif SERIAL == 2 and eng in ("v", "s", "t"):
            w = [(s, v) for s, v in self.cnt.items()
                 if v > 0 and s in ("v", "s", "t")]
            w += [(s, v) for (s, v) in waits
                  if s is not None and v > 0 and s not in ("v", "s", "t")]
        elif SERIAL == 7 and eng == "v":
            vi = self.cnt.get("v", 0)
            if SLO <= vi < SHI:
                w = [(s, v) for s, v in self.cnt.items() if v > 0 and s == "s"]
                w += [(s, v) for (s, v) in waits if s is not None and v > 0]
            else:
                w = [(s, v) for (s, v) in waits if s is not None and v > 0]
        elif SERIAL == 8 and eng == "s":
            si = self.cnt.get("s", 0)
            if SLO <= si < SHI:
                w = [(s, v) for s, v in self.cnt.items() if v > 0 and s == "v"]
                w += [(s, v) for (s, v) in waits if s is not None and v > 0]
            else:
                w = [(s, v) for (s, v) in waits if s is not None and v > 0]
        elif SERIAL == 9:
            if eng == "v":
                w = [(s, v) for s, v in self.cnt.items() if v > 0 and s == "s"]
                w += [(s, v) for (s, v) in waits if s is not None and v > 0]
            elif eng == "s":
                si = self.cnt.get("s", 0)
                w = list(waits)
                if SLO <= si < SHI:
                    w = [(s, v) for s, v in self.cnt.items() if v > 0 and s == "v"] + w
                w = [(s, v) for (s, v) in w if s is not None and v > 0]
            else:
                w = [(s, v) for (s, v) in waits if s is not None and v > 0]
        elif SERIAL == 10:
            if eng == "s":
                w = [(s, v) for s, v in self.cnt.items() if v > 0 and s == "v"]
                w += [(s, v) for (s, v) in waits if s is not None and v > 0]
            elif eng == "v":
                vi = self.cnt.get("v", 0)
                w = list(waits)
                if SLO <= vi < SHI:
                    w = [(s, v) for s, v in self.cnt.items() if v > 0 and s == "s"] + w
                w = [(s, v) for (s, v) in w if s is not None and v > 0]
            else:
                w = [(s, v) for (s, v) in waits if s is not None and v > 0]
        elif SERIAL in (4, 5, 6):
            pair = {4: ("v", "s"), 5: ("v", "t"), 6: ("s", "t")}[SERIAL]
            if eng in pair:
                w = [(s, v) for s, v in self.cnt.items() if v > 0 and s in pair]
                w += [(s, v) for (s, v) in waits
                      if s is not None and v > 0 and s not in pair]
            else:
                w = [(s, v) for (s, v) in waits if s is not None and v > 0]
        elif SERIAL == 3:
            w = [(s, v) for s, v in self.cnt.items()
                 if v > 0 and s not in ("v", "s", "t")]
            w += [(s, v) for (s, v) in waits if s is not None and v > 0]
        else:
            w = [(s, v) for (s, v) in waits if s is not None and v > 0]
        if eng in self.pending:
            w = self.pending.pop(eng) + w
        self.events.append((eng, w, fn, inc, n, lag))
        if inc is not None:
            self.cnt[inc] = self.cnt.get(inc, 0) + n

    def emit(self, eng_name, handle, sems, delay=False):
        own = 0
        for eng, waits, fn, inc, n, lag in self.events:
            if eng != eng_name:
                continue
            if SELFWAIT and eng_name in ("v", "s") and own - lag > 0:
                handle.wait_ge(sems[eng_name], own - lag)
            for s, v in waits:
                if s == eng_name and v <= own - lag:
                    continue
                handle.wait_ge(sems[s], v)
            inst = fn(handle)
            if inc == eng_name:
                own += n
            if inc is not None:
                inst.then_inc(sems[inc], n)


def build_program():
    nc = bass.Bass("TRN2", target_bir_lowering=False,
                   detect_race_conditions=RACECHECK)
    hc = host_consts()

    u = nc.dram_tensor("u", [BC, L, D], F32, kind="ExternalInput").ap()
    out = nc.dram_tensor("out", [BC, L, D], F32, kind="ExternalOutput").ap()
    w1T = nc.dram_tensor("w1T", [D, HC], F32, kind="ExternalInput").ap()
    w2T = nc.dram_tensor("w2T", [HC, D], F32, kind="ExternalInput").ap()
    CTd = nc.dram_tensor("CT", [N, D], F32, kind="ExternalInput").ap()
    alog = nc.dram_tensor("alog", [N], F32, kind="ExternalInput").ap()
    deltad = nc.dram_tensor("delta", [D], F32, kind="ExternalInput").ap()
    Dd = nc.dram_tensor("Dvec", [D], F32, kind="ExternalInput").ap()
    fad = nc.dram_tensor("fa", [D], F32, kind="ExternalInput").ap()
    gammad = nc.dram_tensor("gamma", [D], F32, kind="ExternalInput").ap()
    betad = nc.dram_tensor("beta", [D], F32, kind="ExternalInput").ap()
    rewardd = nc.dram_tensor("reward", [D], F32, kind="ExternalInput").ap()
    convwd = nc.dram_tensor("convw", [HC, 3], F32, kind="ExternalInput").ap()
    convbd = nc.dram_tensor("convb", [HC], F32, kind="ExternalInput").ap()
    cst = {k: nc.dram_tensor(k, list(v.shape), F32, kind="ExternalInput").ap()
           for k, v in hc.items()}
    dbg = {}
    if DEBUG:
        dbg["xh"] = nc.dram_tensor("dbg_xh", [L, D], BF16, kind="ExternalOutput").ap()
        dbg["y2p"] = nc.dram_tensor("dbg_y2p", [L, D], BF16, kind="ExternalOutput").ap()
        dbg["intra"] = nc.dram_tensor("dbg_intra", [L, D], F16, kind="ExternalOutput").ap()
        dbg["x1"] = nc.dram_tensor("dbg_x1", [BC * L, D], F16, kind="ExternalOutput").ap()
        dbg["xq1"] = nc.dram_tensor("dbg_xq1", [L, D], F16, kind="ExternalOutput").ap()
        dbg["hsl"] = nc.dram_tensor("dbg_hsl", [P, HCH * ST], BF16, kind="ExternalOutput").ap()
        dbg["acc2"] = nc.dram_tensor("dbg_acc2", [P, HCH * WIN], F16, kind="ExternalOutput").ap()
        dbg["scol"] = nc.dram_tensor("dbg_scol", [P, 8], F32, kind="ExternalOutput").ap()
        dbg["ffw"] = nc.dram_tensor("dbg_ffw", [P, DCH * WIN], BF16, kind="ExternalOutput").ap()
        dbg["kt"] = nc.dram_tensor("dbg_kt", [P, DCH * M], F32, kind="ExternalOutput").ap()
        dbg["mr"] = nc.dram_tensor("dbg_mr", [P, 8], F32, kind="ExternalOutput").ap()
        dbg["probe"] = nc.dram_tensor("dbg_probe", [P, WIN], F16, kind="ExternalOutput").ap()
    w1q8 = nc.dram_tensor("w1q8", [HCH, DCH, P, P], F8, kind="Internal").ap()
    w2q8 = nc.dram_tensor("w2q8", [DCH, HCH, P, P], F8, kind="Internal").ap()

    sch = Sched()
    keep = []
    SSM_BYTES = 51200
    FFN_BYTES = 87296
    ARENA_BYTES = SSM_BYTES + FFN_BYTES
    _off = {"arena_ssm": [0], "arena_ffn": [SSM_BYTES], "arena_setup": [SSM_BYTES]}
    _REG_END = {"arena_ssm": SSM_BYTES, "arena_ffn": ARENA_BYTES,
                "arena_setup": ARENA_BYTES}
    _arena = [None]

    def _align(x, a=64):
        return (x + a - 1) // a * a

    class TV:

        def __init__(self, ap):
            self._ap = ap

        def ap(self):
            return self._ap

        def __getitem__(self, sl):
            return self._ap[sl]

    def sb(name, shape, dtype, region="perm"):
        nbytes = int(np.prod(shape[1:])) * mybir.dt.size(dtype)
        if region == "perm":
            h = nc.alloc_sbuf_tensor(name, shape, dtype)
            keep.append(h)
            return h
        o = _off[region][0]
        _off[region][0] = _align(o + nbytes)
        assert _off[region][0] <= _REG_END.get(region, ARENA_BYTES), (
            name, region, _off[region][0])
        es = mybir.dt.size(dtype)
        assert o % es == 0
        view = _arena[0].bitcast(dtype).ap()[:, o // es: o // es + int(np.prod(shape[1:]))]
        if len(shape) == 2 and shape[0] < 128:
            view = view[0:shape[0], :]
        return TV(view)

    def start_arena():
        _arena[0] = nc.alloc_sbuf_tensor("arena", [P, ARENA_BYTES], mybir.dt.uint8)
        keep.append(_arena[0])

    bI = sb("bI", [P, P], F32)
    bONE = sb("bONE", [P, P], F32)
    bAlog = sb("bAlog", [N, 1], F32)
    bd0 = sb("bd0", [1, 1], F32)
    rrew = sb("rrew", [1, D], BF16)
    cD = sb("cD", [P, DCH], F32)
    cfa = sb("cfa", [P, DCH], F32)
    cDfa = sb("cDfa", [P, DCH], F32)
    wcv = sb("wcv", [P, HCH * 4], F32)
    dA = sb("dA", [N, 1], F32)
    dA2 = sb("dA2", [P, 1], F32)
    rS = sb("rS", [N, 1], F32)
    V1 = sb("V1", [P, N], BF16)
    W1Ts = sb("W1Ts", [N, P], BF16)
    V2 = sb("V2", [P, 4 * P], BF16)
    W2s = sb("W2s", [P, 4 * P], BF16)
    P2m = sb("P2m", [N, M], BF16)
    CtT = sb("CtT", [N, D], BF16)
    Ct2 = sb("Ct2", [P, D], BF16)
    KtT = sb("KtT", [P, DCH * M], F32)
    gRep = sb("gRep", [P, D], BF16)
    bRep = sb("bRep", [P, D], BF16)
    mcols = sb("mcols", [P, 48], F32)
    msum = sb("msum", [P, 4], F32)
    mreps = sb("mreps", [P, 8], F32)
    stat6 = sb("stat6", [P, 12], F32)
    stat2 = sb("stat2", [P, 2], F32)
    colm = sb("colm", [P, 8], F32)
    ceps = sb("ceps", [P, 1], F32)
    c128 = sb("c128", [P, 1], F32)

    NX1, NQT = 12, 8
    x1 = sb("x1", [P, NX1 * D], F16)
    xq1T = sb("xq1T", [P, NQT * D], BF16)
    w1r = sb("w1r", [P, 24 * P], F8)
    w2r = sb("w2r", [P, 48 * P], F8)
    bIb = sb("bIb", [P, P], BF16)
    outt = sb("outt", [P, 2 * D], F32)
    start_arena()
    bJ127 = sb("bJ127", [P, N], F32, "arena_setup")
    bT1 = sb("bT1", [N, P], F32, "arena_setup")
    bK16 = sb("bK16", [N, M], F32, "arena_setup")
    bJ2 = sb("bJ2", [P, 4 * P], F32, "arena_setup")
    bM2 = sb("bM2", [P, 4 * P], F32, "arena_setup")
    bT2 = sb("bT2", [P, 4 * P], F32, "arena_setup")
    bM2W = sb("bM2W", [P, 4 * P], F32, "arena_setup")
    bCT = sb("bCT", [N, D], F32, "arena_setup")
    rfa = sb("rfa", [1, D], F32, "arena_setup")
    rgam = sb("rgam", [1, D], F32, "arena_setup")
    rbet = sb("rbet", [1, D], F32, "arena_setup")
    tmpc = sb("tmpc", [P, P], F32, "arena_setup")
    tmpd = sb("tmpd", [N, P], F32, "arena_setup")
    wbuf = sb("wbuf", [P, 2 * 2048], F32, "arena_setup")
    wqb = sb("wqb", [P, 2 * 2048], F8, "arena_setup")
    wdump = sb("wdump", [P, 2 * 2048], BF16, "arena_setup")
    h1 = sb("h1", [N, D], F32, "arena_ssm")
    hC1 = sb("hC1", [N, D], BF16, "arena_ssm")
    y2t = sb("y2t", [P, D], BF16, "arena_ssm")

    xq1t = sb("xq1t", [P, D], F16, "arena_ssm")
    xh = sb("xh", [P, 2 * D], BF16, "arena_ssm")
    intraT = sb("intraT", [P, D], F16, "arena_ssm")
    tmpa = sb("tmpa", [P, D], BF16, "arena_ssm")
    _tmpb_off = _off["arena_ssm"][0] - 2048
    tmpb = TV(_arena[0].bitcast(F16).ap()[:, _tmpb_off // 2:_tmpb_off // 2 + D])
    xhT = sb("xhT", [P, DCH * 512], BF16, "arena_ssm")
    accb = sb("accb", [P, DCH * 512], F16, "arena_ssm")
    y2p = sb("y2p", [P, 4 * D], BF16, "arena_ssm")
    G2C = sb("G2C", [P, 4 * D], BF16, "arena_ssm")
    hsl = sb("hsl", [P, HCH * ST], BF16, "arena_ffn")
    ubuf = sb("ubuf", [P, 2 * 512], BF16, "arena_ffn")
    vbuf = sb("vbuf", [P, 2 * 512], BF16, "arena_ffn")
    cbuf = sb("cbuf", [P, 2 * 512], BF16, "arena_ffn")
    acc2 = sb("acc2", [P, HCH * WIN], F16, "arena_ffn")
    maxt = sb("maxt", [P, WIN], F16, "arena_ffn")
    mint = sb("mint", [P, WIN], F16, "arena_ffn")
    amT = maxt
    s2col = sb("s2col", [P, 4], F32, "arena_ffn")
    s2cb = sb("s2cb", [P, 4], F32, "arena_ffn")
    s2rep = sb("s2rep", [P, WIN], BF16, "arena_ffn")
    ffw = sb("ffw", [P, 2 * WIN], BF16, "arena_ffn")
    srow = sb("srow", [1, WIN], F32, "arena_ffn")
    ffT = sb("ffT", [P, 4 * D], BF16, "arena_ffn")

    psA = nc.alloc_psum_tensor("psA", [P, 1024], F32)
    psB = nc.alloc_psum_tensor("psB", [P, 1024], F32)
    psM1a = nc.alloc_psum_tensor("psM1a", [P, 512], F32)
    psM1b = nc.alloc_psum_tensor("psM1b", [P, 512], F32)
    psM2a = nc.alloc_psum_tensor("psM2a", [P, 512], F32)
    psM2b = nc.alloc_psum_tensor("psM2b", [P, 512], F32)

    pg = {"A": ("v", 0), "B": ("v", 0), "M1a": ("s", 0), "M1b": ("s", 0),
          "M2a": ("s", 0), "M2b": ("s", 0)}
    xh_g = [[], []]
    intraT_g = [("v", 0), ("v", 0)]
    G2C_g = [("t", 0), ("t", 0)]
    y2p_g = [("v", 0)] * 8
    xq1t_g = ("sq", 0)
    amb_g = ("sq", 0)
    ffw_g = [("v", 0), ("v", 0)]
    ffT_g = ("v", 0)
    outt_g = [("ou", 0), ("ou", 0)]
    w1r_g = [("t", 0), ("t", 0), ("t", 0)]
    w2r_g = [("t", 0), ("t", 0), ("t", 0)]
    conv_g = [("v", 0)] * HCH
    acc2_g = ("t", 0)
    xstore_g = [("ou", 0)] * CH
    xstage_g = [("v", 0), ("v", 0)]
    tmpa_g = ("v", 0)
    tmpb_g = ("sq", 0)
    y2s = [0] * 8
    s2t_g = ("t", 0)
    ubuf_g = [("t", 0), ("t", 0)]
    pg_evac = [("s", 0)]

    def W(*pairs):
        return list(pairs)

    def vev(fn, waits=(), lag=0):
        sch.ev("v", fn, waits, inc="v", lag=lag)
        return sch.now("v")

    def sev(fn, waits=(), lag=0):
        sch.ev("s", fn, waits, inc="s", lag=lag)
        return sch.now("s")

    def tev(fn, waits=()):
        sch.ev("t", fn, waits, inc="t")
        return sch.now("t")

    def dma(dst, src, slow=False):
        if slow:
            return lambda e: e.dma_start(out=dst, in_=src,
                                         allow_slow_non_contiguous=True)
        return lambda e: e.dma_start(out=dst, in_=src)

    for dst, src in [
        (bJ127[:], cst["cJ127"]), (bT1[:], cst["cT1"]), (bK16[:], cst["cK16"]),
        (bI[:], cst["cI"]), (bONE[:], cst["cONE"]),
        (bAlog[:], alog.rearrange("(p o) -> p o", o=1)),
        (bd0[:], deltad[0:1].rearrange("(p o) -> p o", o=1)),
        (bCT[:], CTd),
        (rfa[:], fad.rearrange("(o d) -> o d", o=1)),
        (rgam[:], gammad.rearrange("(o d) -> o d", o=1)),
        (rbet[:], betad.rearrange("(o d) -> o d", o=1)),

        (cD[:], Dd.rearrange("(c p) -> p c", p=P)),
        (cfa[:], fad.rearrange("(c p) -> p c", p=P)),
        (wcv.ap().rearrange("p (c k) -> p c k", k=4)[:, :, 0:3],
         convwd.rearrange("(c p) k -> p c k", p=P)),
        (wcv.ap().rearrange("p (c k) -> p c k", k=4)[:, :, 3:4],
         convbd.rearrange("(c p) -> p c", p=P).rearrange("p c -> p c 1") if False else
         convbd.rearrange("(c p o) -> p c o", p=P, o=1)),
    ]:
        sch.ev("sy", dma(dst, src, slow=True), inc="s_ld", n=16)
    for g in range(4):
        for bdst, key in [(bJ2, "cJ2"), (bM2, "cM2"), (bT2, "cT2"), (bM2W, "cM2W")]:
            sch.ev("sy", dma(bdst[:, g * P:(g + 1) * P], cst[key][g * P:(g + 1) * P, :]),
                   inc="s_ld", n=16)
    sch.ev("g", dma(rrew[:], rewardd.rearrange("(o d) -> o d", o=1)), inc="s_rr", n=16)
    LD0 = sch.now("s_ld")

    vev(lambda e: e.tensor_copy(bIb[:], bI[:]), W(("s_ld", LD0)))
    vev(lambda e: e.memset(ceps[:], EPS))
    vev(lambda e: e.memset(c128[:], 128.0))
    sev(lambda e: e.activation(colm[0:N, 0:1], bAlog[:], AT.Exp), W(("s_ld", LD0)))
    tev(lambda e: e.matmul(psA[0:N, 0:1], bONE[0:1, 0:N], bd0[:], start=True, stop=True),
        W(("s_ld", LD0)))
    vev(lambda e: e.tensor_copy(colm[0:N, 1:2], psA[0:N, 0:1]), W(("t", sch.now("t"))))
    vev(lambda e: e.tensor_scalar(dA[:], colm[0:N, 0:1], colm[0:N, 1:2], -1.0,
                                  ALU.mult, ALU.mult), W(("s", sch.now("s"))))
    VDA = sch.now("v")
    pg["A"] = ("v", VDA)
    sev(lambda e: e.activation(rS[:], dA[:], AT.Exp, scale=c128[0:N, 0:1]), W(("v", VDA)))
    vev(lambda e: e.tensor_copy(dA2[0:N, :], dA[:]))
    vev(lambda e: e.tensor_copy(dA2[N:P, :], dA[:]))
    VDA2 = sch.now("v")
    tev(lambda e: e.transpose(psB[0:1, 0:N], dA[:], bI[0:N, 0:N]), W(("v", VDA)))
    vev(lambda e: e.tensor_copy(srow[0:1, 0:N], psB[0:1, 0:N]), W(("t", sch.now("t"))))
    vev(lambda e: e.tensor_copy(srow[0:1, N:2 * N], srow[0:1, 0:N]))
    VROW = sch.now("v")
    pg["B"] = ("v", VROW)
    tev(lambda e: e.matmul(psA[:, 0:N], bONE[0:1, :], srow[0:1, 0:N], start=True, stop=True),
        W(("v", VROW), pg["A"]))
    tev(lambda e: e.matmul(psB[:, 0:2 * N], bONE[0:1, :], srow[0:1, 0:2 * N],
                           start=True, stop=True), W(pg["B"]))
    TGRID = sch.now("t")
    vev(lambda e: e.tensor_mul(tmpc[:, 0:N], bJ127[:], psA[:, 0:N]), W(("t", TGRID)))
    sev(lambda e: e.activation(V1[:], tmpc[:, 0:N], AT.Exp), W(("v", sch.now("v"))))
    for g in range(4):
        vev(lambda e, g=g: e.tensor_mul(tmpc[:], bJ2[:, g * P:(g + 1) * P], psB[:, 0:P]),
            W(("t", TGRID), ("s", sch.now("s"))))
        vev(lambda e, g=g: e.tensor_add(tmpc[:], tmpc[:], bM2[:, g * P:(g + 1) * P]))
        sev(lambda e, g=g: e.activation(V2[:, g * P:(g + 1) * P], tmpc[:], AT.Exp),
            W(("v", sch.now("v"))))
    for g in range(4):
        vev(lambda e, g=g: e.tensor_scalar(tmpc[:], bT2[:, g * P:(g + 1) * P], dA2[:, 0:1],
                                           None, ALU.mult),
            W(("v", VDA2), ("s", sch.now("s"))))
        vev(lambda e, g=g: e.tensor_add(tmpc[:], tmpc[:], bM2W[:, g * P:(g + 1) * P]))
        sev(lambda e, g=g: e.activation(W2s[:, g * P:(g + 1) * P], tmpc[:], AT.Exp),
            W(("v", sch.now("v"))))
    vev(lambda e: e.tensor_scalar(tmpd[:], bT1[:], dA[:, 0:1], None, ALU.mult),
        W(("s", sch.now("s"))))
    sev(lambda e: e.activation(W1Ts[:], tmpd[:], AT.Exp), W(("v", sch.now("v"))))
    vev(lambda e: e.tensor_scalar(tmpd[:, 0:M], bK16[:], dA[:, 0:1], None, ALU.mult),
        W(("s", sch.now("s"))))
    sev(lambda e: e.activation(P2m[:], tmpd[0:N, 0:M], AT.Exp), W(("v", sch.now("v"))))
    for h in range(2):
        tev(lambda e, h=h: e.matmul(psA[0:N, h * 512:(h + 1) * 512], bONE[0:1, 0:N],
                                    rfa[:, h * 512:(h + 1) * 512], start=True, stop=True),
            W(("v", sch.now("v"))) if h == 0 else W())
    vev(lambda e: e.tensor_mul(CtT[:], bCT[:], psA[0:N, :]), W(("t", sch.now("t"))))
    vev(lambda e: e.tensor_copy(Ct2[0:N, :], CtT[:]))
    vev(lambda e: e.tensor_copy(Ct2[N:P, :], CtT[:]))
    VCT = sch.now("v")
    pg["A"] = ("v", VCT)
    for h in range(2):
        tev(lambda e, h=h: e.matmul(psB[:, h * 512:(h + 1) * 512], bONE[0:1, :],
                                    rgam[:, h * 512:(h + 1) * 512], start=True, stop=True),
            W(("v", VCT)) if h == 0 else W())
    vev(lambda e: e.tensor_copy(gRep[:], psB[:]), W(("t", sch.now("t"))))
    for h in range(2):
        tev(lambda e, h=h: e.matmul(psB[:, h * 512:(h + 1) * 512], bONE[0:1, :],
                                    rbet[:, h * 512:(h + 1) * 512], start=True, stop=True),
            W(("v", sch.now("v"))) if h == 0 else W())
    vev(lambda e: e.tensor_copy(bRep[:], psB[:]), W(("t", sch.now("t"))))
    pg["B"] = ("v", sch.now("v"))
    vev(lambda e: e.tensor_mul(cDfa[:], cD[:], cfa[:]), W(("s_ld", LD0)))
    for dc in range(DCH):
        tev(lambda e, dc=dc: e.matmul(psA[:, dc * M:(dc + 1) * M], CtT[:, dc * P:(dc + 1) * P],
                                      P2m[:], start=True, stop=True),
            W(("s", sch.now("s")), pg["A"]) if dc == 0 else W())
    vev(lambda e: e.tensor_copy(KtT[:], psA[:, 0:DCH * M]), W(("t", sch.now("t"))))
    vev(lambda e: e.tensor_tensor(
        KtT.ap().rearrange("p (c k) -> p c k", k=M)[:, :, 0:1],
        KtT.ap().rearrange("p (c k) -> p c k", k=M)[:, :, 0:1],
        cDfa.ap().rearrange("p (c o) -> p c o", o=1), ALU.add))
    pg["A"] = ("v", sch.now("v"))

    w1s = w1T.rearrange("(a p) h -> a p h", p=P)
    w2s = w2T.rearrange("(a p) h -> a p h", p=P)
    tiles = [(0, a, cb) for a in range(DCH) for cb in range(2)] + \
            [(1, a, 0) for a in range(HCH)]
    tw = {0: 2048, 1: 1024}
    svals = []
    for ti, (wi, a, cb) in enumerate(tiles):
        slot = ti % 2
        wd = tw[wi]
        src = w1s[a, :, cb * 2048:(cb + 1) * 2048] if wi == 0 else w2s[a, :, :]
        guard = svals[ti - 2] if ti >= 2 else 0
        sch.ev("sy", dma(wbuf[:, slot * 2048:slot * 2048 + wd], src),
               W(("s", guard)), inc="s_pl%d" % slot, n=16)
        mi = ti
        sdone = sev(lambda e, slot=slot, wd=wd, mi=mi: e.activation(
            wdump[:, 0:wd], wbuf[:, slot * 2048:slot * 2048 + wd], AT.Abs,
            accum_out=mcols[:, mi:mi + 1]), W(("s_pl%d" % slot, sch.now("s_pl%d" % slot))))
        svals.append(sdone)
    SP1 = sch.now("s")
    for wi, (c0, c1) in [(0, (0, 16)), (1, (16, 48))]:
        vev(lambda e, wi=wi, c0=c0, c1=c1: e.reduce_sum(
            msum[:, wi:wi + 1], mcols[:, c0:c1], AX.X), W(("s", SP1)))
        tev(lambda e, wi=wi: e.matmul(psA[0:1, wi:wi + 1], msum[:, wi:wi + 1],
                                      bONE[:, 0:1], start=True, stop=True),
            W(("v", sch.now("v")), pg["A"]) )
        vev(lambda e, wi=wi: e.tensor_scalar(colm[0:1, 4 + wi:5 + wi], psA[0:1, wi:wi + 1],
                                             1.0 / (HC * D), 1e-5, ALU.mult, ALU.max),
            W(("t", sch.now("t"))))
        tev(lambda e, wi=wi: e.matmul(psA[0:P, 8 + wi:9 + wi] if False else psA[:, 8 + wi:9 + wi],
                                      bONE[0:1, :], colm[0:1, 4 + wi:5 + wi],
                                      start=True, stop=True), W(("v", sch.now("v"))))
        vev(lambda e, wi=wi: e.tensor_copy(mreps[:, 4 + wi:5 + wi], psA[:, 8 + wi:9 + wi]),
            W(("t", sch.now("t"))))
        vev(lambda e, wi=wi: e.reciprocal(mreps[:, wi:wi + 1], mreps[:, 4 + wi:5 + wi]))
        vev(lambda e, wi=wi: e.tensor_scalar(mreps[:, 2 + wi:3 + wi], mreps[:, 4 + wi:5 + wi],
                                             1.0 / 127.0, None, ALU.mult))
    VP1 = sch.now("v")
    pg["A"] = ("v", VP1)

    vev(lambda e: e.memset(mreps[:, 6:7], 192.0), W(("v", VP1)))
    vev(lambda e: e.memset(mreps[:, 7:8], -192.0))
    VMS = sch.now("v")
    vq_last = [VMS, VMS]
    sact1_last = [0, 0]
    for ti, (wi, a, cb) in enumerate(tiles):
        slot = ti % 2
        wd = tw[wi]
        src = w1s[a, :, cb * 2048:(cb + 1) * 2048] if wi == 0 else w2s[a, :, :]
        guard_s = sact1_last[slot]
        sch.ev("sy", dma(wbuf[:, slot * 2048:slot * 2048 + wd], src),
               W(("v", VMS), ("s", guard_s)), inc="s_pl%d" % slot, n=16)
        SYD = sch.now("s_pl%d" % slot)
        sev(lambda e, slot=slot, wd=wd, wi=wi: e.activation(
            wdump[:, slot * 2048:slot * 2048 + wd],
            wbuf[:, slot * 2048:slot * 2048 + wd], AT.Identity,
            bias=mreps[:, 6:7], scale=mreps[:, wi:wi + 1]),
            W(("s_pl%d" % slot, SYD), ("v", vq_last[slot])), lag=1)
        sact1_last[slot] = sch.now("s")
        SACT2 = sev(lambda e, slot=slot, wd=wd: e.activation(
            wdump[:, slot * 2048:slot * 2048 + wd],
            wdump[:, slot * 2048:slot * 2048 + wd], AT.Identity,
            bias=mreps[:, 7:8]))
        vq = vev(lambda e, slot=slot, wd=wd: e.tensor_scalar(
            wqb[:, slot * 2048:slot * 2048 + wd],
            wdump[:, slot * 2048:slot * 2048 + wd],
            -1.0, 1.0, ALU.max, ALU.min),
            W(("s", SACT2), ("s_ps%d" % slot, sch.now("s_ps%d" % slot))))
        vq_last[slot] = vq
        if wi == 0:
            dst = w1q8[16 * cb:16 * cb + 16, a].rearrange("a p c -> p a c")
        else:
            dst = w2q8[:, a].rearrange("b p c -> p b c")
        srcq = wqb.ap()[:, slot * 2048:slot * 2048 + wd].rearrange(
            "p (a c) -> p a c", c=P)
        sch.ev("g", dma(dst, srcq), W(("v", vq)), inc="s_ps%d" % slot, n=16)
    GPREP = sch.now("s_ps0") + sch.now("s_ps1")
    VPREP = sch.now("v")
    if DEBUG and False:
        sch.ev("g", dma(dbg["kt"][:, :], KtT[:]), W(("v", VPREP)), inc="gq", n=16)
        sch.ev("g", dma(dbg["mr"][:, :], mreps[:]), W(("v", VPREP)), inc="gq", n=16)

    store_cnt = [0]

    def dbgev(key, dst_slice, src_ap, waits):
        if DEBUG:
            sch.ev("g", dma(dst_slice, src_ap), waits, inc="gq", n=16)
            return ("gq", sch.now("gq"))
        return ("gq", 0)

    NX1S = ["s_x%d" % i for i in range(NX1)]
    NQTS = ["s_tq%d" % i for i in range(NQT)]
    x1_g = [[("v", 0)]] * NX1
    xq1_g = [("t", 0)] * NQT
    xq1_done = {}
    xhT_g = [("v", 0)]
    h1_g = [("t", 0)]
    PREPW = [("s_ps0", sch.now("s_ps0")), ("s_ps1", sch.now("s_ps1"))]
    first_ring = [True]
    tails_cnt = [0]
    probe_hits = [0]

    def stage_A(s, ci):
        nonlocal tmpa_g
        t0 = ci * P
        slot = ci % 2
        qc = ci % 4
        gi = s * CH + ci
        xs = gi % NX1
        prev = gi - NX1
        xw = [("s_o%d" % (prev % 2), 16 * (prev // 2 + 1))] if prev >= 0 else []
        sch.ev("g", dma(x1[:, xs * D:(xs + 1) * D], u[s, t0:t0 + P, :]),
               W(*xw), inc=NX1S[xs], n=16)
        SYX = sch.now(NX1S[xs])
        vev(lambda e, xs=xs: e.bn_stats(stat6[:, 0:6], x1[:, xs * D:xs * D + 512]),
            W((NX1S[xs], SYX)))
        vev(lambda e, xs=xs: e.bn_stats(stat6[:, 6:12],
                                        x1[:, xs * D + 512:(xs + 1) * D]), lag=1)
        vev(lambda e: e.bn_aggr(stat2[:], stat6[:]))
        sev(lambda e: e.activation(colm[:, 1:2], stat2[:, 1:2], AT.Sqrt,
                                   bias=ceps[:, 0:1]), W(("v", sch.now("v"))), lag=1)
        vev(lambda e: e.reciprocal(colm[:, 1:2], colm[:, 1:2]), W(("s", sch.now("s"))))
        vxh = vev(lambda e, xs=xs, slot=slot: e.tensor_scalar(
            xh[:, slot * D:(slot + 1) * D], x1[:, xs * D:(xs + 1) * D],
            stat2[:, 0:1], colm[:, 1:2], ALU.subtract, ALU.mult), xh_g[slot])
        sch.ev("s", lambda e, slot=slot: e.dma_start_transpose(
            tmpa.ap().rearrange("p (c t) -> p c t", c=DCH),
            xh.ap()[:, slot * D:(slot + 1) * D].rearrange("p (c t) -> p c t", c=DCH)),
            W(("v", vxh), tmpa_g, tmpb_g), inc="s_txh", n=16)
        SQXH = sch.now("s_txh")
        vts = vev(lambda e, qc=qc: e.tensor_copy(
            xhT.ap().rearrange("p (dc t) -> p dc t", dc=DCH)[:, :, qc * P:(qc + 1) * P],
            tmpa.ap().rearrange("p (dc t) -> p dc t", dc=DCH)),
            W(("s_txh", SQXH), *xhT_g))
        tmpa_g = ("v", vts)
        for g in range(4):
            ps, nm = (psA, "A") if g % 2 == 0 else (psB, "B")
            tev(lambda e, g=g, ps=ps, slot=slot: e.matmul(
                ps[:, 0:512], V2[:, g * P:(g + 1) * P],
                xh[:, slot * D:slot * D + 512], start=True, stop=True),
                W(("v", vxh), pg[nm]))
            TG2 = tev(lambda e, g=g, ps=ps, slot=slot: e.matmul(
                ps[:, 512:1024], V2[:, g * P:(g + 1) * P],
                xh[:, slot * D + 512:(slot + 1) * D], start=True, stop=True))
            SG = sev(lambda e, g=g, ps=ps: e.copy(G2C[:, g * D:(g + 1) * D], ps[:]),
                     W(("t", TG2), G2C_g[0] if g == 0 else ("v", 0)), lag=3)
            pg[nm] = ("s", SG)
        SG2 = sch.now("s")
        for g in range(4):
            vev(lambda e, g=g: e.tensor_mul(
                G2C[:, g * D:(g + 1) * D], G2C[:, g * D:(g + 1) * D], Ct2[:]),
                W(("s", SG2)) if g == 0 else W(), lag=3)
        VG2C = sch.now("v")
        for hh in range(2):
            tev(lambda e, hh=hh, slot=slot: e.matmul(
                psA[0:N, hh * 512:(hh + 1) * 512], V1[:],
                xh[:, slot * D + hh * 512:slot * D + (hh + 1) * 512],
                start=True, stop=True), W(pg["A"]) if hh == 0 else W())
        TG1 = sch.now("t")
        xh_g[slot] = [("t", TG1), ("s_txh", SQXH)]
        if ci == 0:
            vev(lambda e: e.memset(h1[:], 0.0), W(h1_g[0]))
        vev(lambda e: e.tensor_mul(hC1[:], h1[:], CtT[:]), W(("t", TG1)),
            lag=0 if ci == 0 else 2)
        vev(lambda e: e.scalar_tensor_tensor(h1[:], h1[:], rS[:, 0:1], psA[0:N, :],
                                             ALU.mult, ALU.add))
        pg["A"] = ("v", sch.now("v"))
        VH = sch.now("v")
        for hh in range(2):
            tev(lambda e, hh=hh: e.matmul(
                psB[:, hh * 512:(hh + 1) * 512], W1Ts[:],
                hC1[:, hh * 512:(hh + 1) * 512], start=True, stop=False),
                W(("v", VH), pg["B"]) if hh == 0 else W())
        for g in range(4):
            for hh in range(2):
                tev(lambda e, g=g, hh=hh: e.matmul(
                    psB[:, hh * 512:(hh + 1) * 512], W2s[:, g * P:(g + 1) * P],
                    G2C[:, g * D + hh * 512:g * D + (hh + 1) * 512],
                    start=False, stop=(g == 3)),
                    W(("v", VG2C)) if (g == 0 and hh == 0) else W())
        TY = sch.now("t")
        G2C_g[0] = ("t", TY)
        h1_g[0] = ("t", TY)
        yg = y2p_g[qc] if isinstance(y2p_g[qc], list) else [y2p_g[qc]]
        SY2 = sev(lambda e, qc=qc: e.copy(y2p[:, qc * D:(qc + 1) * D], psB[:]),
                  W(("t", TY), *yg), lag=1)
        y2s[qc] = SY2
        pg["B"] = ("s", SY2)

    def taps(qtr_base_done):
        def tapap(buf, dc, k, hi):
            a = buf.ap()[:, dc * 512:(dc + 1) * 512].rearrange("p (g i) -> p g i", i=M)
            return a[:, :, k:hi]
        for dc in range(DCH):
            vev(lambda e, dc=dc: e.tensor_scalar(
                tapap(accb, dc, 0, M), tapap(xhT, dc, 0, M),
                KtT[:, dc * M:dc * M + 1], None, ALU.mult),
                W(intraT_g[0], intraT_g[1]) if dc == 0 else W(), lag=7)
        for k in range(1, M):
            for dc in range(DCH):
                vev(lambda e, k=k, dc=dc: e.scalar_tensor_tensor(
                    tapap(accb, dc, k, M), tapap(xhT, dc, 0, M - k),
                    KtT[:, dc * M + k:dc * M + k + 1],
                    tapap(accb, dc, k, M), ALU.mult, ALU.add), lag=7)
        VT = sch.now("v")
        xhT_g[0] = ("v", VT)
        return VT

    def stage_C(s, ci, VTAPS):
        nonlocal tmpb_g, xq1t_g
        qc = ci % 4
        isl = 0
        gi = s * CH + ci
        xs = gi % NX1
        qs = gi % NQT
        vgb = vev(lambda e, qc=qc: e.tensor_copy(
            tmpb.ap().rearrange("p (dc t) -> p dc t", dc=DCH),
            accb.ap().rearrange("p (dc t) -> p dc t", dc=DCH)[:, :, qc * P:(qc + 1) * P]),
            W(("v", VTAPS), tmpb_g, tmpa_g), lag=1)
        itg = [intraT_g[isl]]
        sch.ev("s", lambda e, isl=isl: e.dma_start_transpose(
            intraT.ap().rearrange("p (c t) -> p c t", c=DCH),
            tmpb.ap().rearrange("p (c t) -> p c t", c=DCH)),
            W(("v", vgb), *itg), inc="s_ti%d" % isl, n=16)
        SQI = sch.now("s_ti%d" % isl)
        tmpb_g = ("s_ti%d" % isl, SQI)
        vy = vev(lambda e, qc=qc: e.tensor_add(
            y2t[:], y2p[:, qc * D:(qc + 1) * D], intraT[:]),
            W(("s_ti%d" % isl, SQI), ("s", y2s[qc]),
              *(xq1t_g if isinstance(xq1t_g, list) else [xq1t_g])), lag=2)
        y2p_g[qc] = ("v", vy)
        intraT_g[isl] = ("v", vy)
        if ci == 0:
            vev(lambda e: e.tensor_add(y2t[0:1, :], y2t[0:1, :], rrew[:]),
                W(("s_rr", 16)))
        vev(lambda e: e.bn_stats(stat6[:, 0:6], y2t[:, 0:512]))
        vev(lambda e: e.bn_stats(stat6[:, 6:12], y2t[:, 512:1024]), lag=1)
        vev(lambda e: e.bn_aggr(stat2[:], stat6[:]))
        sev(lambda e: e.activation(colm[:, 1:2], stat2[:, 1:2], AT.Sqrt,
                                   bias=ceps[:, 0:1]), W(("v", sch.now("v"))), lag=1)
        vev(lambda e: e.reciprocal(colm[:, 1:2], colm[:, 1:2]), W(("s", sch.now("s"))))
        vev(lambda e: e.tensor_scalar(y2t[:], y2t[:], stat2[:, 0:1], colm[:, 1:2],
                                      ALU.subtract, ALU.mult))
        vev(lambda e: e.tensor_mul(y2t[:], y2t[:], gRep[:]))
        vev(lambda e: e.tensor_add(y2t[:], y2t[:], bRep[:]))
        vev(lambda e, xs=xs: e.tensor_add(x1[:, xs * D:(xs + 1) * D],
                                          x1[:, xs * D:(xs + 1) * D], y2t[:]))
        if DEBUG:
            sch.ev("g", dma(dbg["x1"][gi * P:(gi + 1) * P, :],
                            x1[:, xs * D:(xs + 1) * D]),
                   W(("v", sch.now("v"))), inc="gq", n=16)
        vev(lambda e, xs=xs: e.bn_stats(stat6[:, 0:6], x1[:, xs * D:xs * D + 512]))
        vev(lambda e, xs=xs: e.bn_stats(stat6[:, 6:12],
                                        x1[:, xs * D + 512:(xs + 1) * D]), lag=1)
        vev(lambda e: e.bn_aggr(stat2[:], stat6[:]))
        sev(lambda e: e.activation(colm[:, 1:2], stat2[:, 1:2], AT.Sqrt,
                                   bias=ceps[:, 0:1]), W(("v", sch.now("v"))), lag=1)
        vev(lambda e: e.reciprocal(colm[:, 1:2], colm[:, 1:2]), W(("s", sch.now("s"))))
        vev(lambda e, xs=xs: e.tensor_scalar(y2t[:], x1[:, xs * D:(xs + 1) * D],
                                             stat2[:, 0:1], colm[:, 1:2],
                                             ALU.subtract, ALU.mult))
        vev(lambda e: e.reduce_max(colm[:, 2:3], y2t[:], AX.X,
                                   apply_absolute_value=True))
        vev(lambda e: e.tensor_scalar(colm[:, 2:3], colm[:, 2:3], 1e-5, None, ALU.max))
        vev(lambda e: e.reciprocal(colm[:, 3:4], colm[:, 2:3]))
        vev(lambda e: e.tensor_scalar(colm[:, 3:4], colm[:, 3:4], 127.0, None, ALU.mult))
        vev(lambda e: e.tensor_scalar(colm[:, 4:5], colm[:, 2:3], mreps[:, 2:3],
                                      None, ALU.mult))
        xqg = xq1t_g if isinstance(xq1t_g, list) else [xq1t_g]
        vev(lambda e: e.tensor_scalar(xq1t[:], y2t[:], colm[:, 3:4], 1536.0,
                                      ALU.mult, ALU.add), W(*xqg))
        vev(lambda e: e.tensor_scalar(xq1t[:], xq1t[:], 1536.0, None, ALU.subtract))
        vxq = vev(lambda e: e.tensor_scalar(y2t[:], xq1t[:], colm[:, 4:5], None,
                                            ALU.mult))
        prevw = (gi - NQT) // 4
        qwg = [("s_mmw", prevw + 1)] if gi >= NQT else []
        sch.ev("s", lambda e, qs=qs: e.dma_start_transpose(
            xq1T.ap()[:, qs * D:(qs + 1) * D].rearrange("p (c t) -> p c t", c=DCH),
            y2t.ap().rearrange("p (c t) -> p c t", c=DCH)),
            W(("v", vxq), *qwg), inc=NQTS[qs], n=16)
        xq1_done[gi] = (NQTS[qs], sch.now(NQTS[qs]))
        xq1t_g = [(NQTS[qs], sch.now(NQTS[qs]))]

    def ssm_half(s, h):
        for q in range(2):
            for c4 in range(4):
                gi = s * CH + h * 8 + q * 4 + c4
                if gi >= NX1:
                    yield ("tails", gi - NX1 + 1)
                stage_A(s, h * 8 + q * 4 + c4)
                yield
            VT = taps(None)
            yield
            for c4 in range(4):
                stage_C(s, h * 8 + q * 4 + c4, VT)
                yield
            yield ("qdone", s * 4 + h * 2 + q + 1)

    def ffn_seq(s):
        nonlocal acc2_g, amb_g, ffT_g, s2t_g

        def conv_products(cch, SPS):
            b0 = cch * ST
            sl = cch % 2
            w_c = W(("s", SPS), acc2_g if cch == 0 else ("v", 0), ubuf_g[sl])
            vev(lambda e, b0=b0, sl=sl, cch=cch: e.tensor_scalar(
                ubuf[:, sl * 512:sl * 512 + 512], hsl[:, b0:b0 + 512],
                wcv[:, cch * 4 + 0:cch * 4 + 1], None, ALU.mult), w_c)
            vev(lambda e, b0=b0, sl=sl, cch=cch: e.tensor_scalar(
                vbuf[:, sl * 512:sl * 512 + 512], hsl[:, b0 + 2:b0 + 514],
                wcv[:, cch * 4 + 2:cch * 4 + 3], None, ALU.mult), lag=1)
            cg = vev(lambda e, b0=b0, sl=sl, cch=cch: e.tensor_scalar(
                cbuf[:, sl * 512:sl * 512 + 512], hsl[:, b0 + 1:b0 + 513],
                wcv[:, cch * 4 + 1:cch * 4 + 2], wcv[:, cch * 4 + 3:cch * 4 + 4],
                ALU.mult, ALU.add), lag=2)
            conv_g[cch] = ("v", cg)
            return cg

        def conv_mms(cch, ps, evS, cg):
            a0 = cch * WIN
            sl = cch % 2
            tev(lambda e, ps=ps, sl=sl: e.matmul(
                ps[:], bIb[:], cbuf[:, sl * 512:sl * 512 + 512],
                start=True, stop=False), W(("v", cg), ("s", evS)))
            tev(lambda e, ps=ps, sl=sl: e.matmul(
                ps[:], bIb[:], ubuf[:, sl * 512:sl * 512 + 512],
                start=False, stop=False))
            TCV = tev(lambda e, ps=ps, sl=sl: e.matmul(
                ps[:], bIb[:], vbuf[:, sl * 512:sl * 512 + 512],
                start=False, stop=True))
            ubuf_g[sl] = ("t", TCV)
            SSI = sev(lambda e, ps=ps, a0=a0: e.activation(
                acc2[:, a0:a0 + WIN], ps[:], AT.Silu), W(("t", TCV)), lag=1)
            sch.mark("SSI s=%d cch=%d" % (s, cch))
            if DEBUG and DBGPROBE is not None and (s, cch) == DBGPROBE[:2]:
                if DBGPROBE[2] == probe_hits[0]:
                    sch.ev("sy", dma(dbg["probe"][:, :], acc2[:, a0:a0 + WIN]),
                           W(("s", SSI)), inc="dbq", n=16)
                probe_hits[0] += 1
            return SSI

        def mm1_cch(win, cch):
            sch.mark("MM1 s=%d win=%d cch=%d" % (s, win, cch))
            rh = cch % 3
            extra = PREPW if first_ring[0] else []
            first_ring[0] = False
            sch.ev("sy", dma(
                w1r.ap()[:, rh * 8 * P:(rh + 1) * 8 * P].rearrange(
                    "p (b c) -> p b c", b=DCH),
                w1q8[cch].rearrange("b p c -> p b c")),
                W(w1r_g[rh], *extra), inc="s_w1%d" % rh, n=16)
            SYW = sch.now("s_w1%d" % rh)
            ps, nm = (psM1a, "M1a") if cch % 2 == 0 else (psM1b, "M1b")
            base = ((s * CH + win * 4) % NQT)
            qw = [xq1_done[s * CH + win * 4 + j] for j in range(4)]
            for dc in range(DCH):
                mvap = xq1T.ap().rearrange("p (qs dc t) -> p qs dc t", qs=NQT,
                                           dc=DCH)[:, base:base + 4, dc, :]
                tev(lambda e, ps=ps, rh=rh, dc=dc, mvap=mvap: e.matmul(
                    ps[:], w1r[:, (rh * 8 + dc) * P:(rh * 8 + dc + 1) * P],
                    mvap, start=(dc == 0), stop=(dc == DCH - 1)),
                    W(("s_w1%d" % rh, SYW), *qw, pg[nm]) if dc == 0 else W())
            TMM = sch.now("t")
            w1r_g[rh] = ("t", TMM)
            if cch == HCH - 1:
                sch.ev("t", lambda e: e.nop(), W(), inc="s_mmw", n=1)
            b0 = cch * ST
            if win > 0:
                SPS = sev(lambda e, ps=ps, b0=b0: e.copy(hsl[:, b0 + 513:b0 + 514],
                                                         ps[:, 0:1]), W(("t", TMM)),
                          lag=1)
                cg = conv_products(cch, SPS)
                sev(lambda e, b0=b0: e.copy(hsl[:, b0:b0 + 1],
                                            hsl[:, b0 + 512:b0 + 513]),
                    W(("v", cg)))
                SE = sev(lambda e, ps=ps, b0=b0: e.copy(hsl[:, b0 + 1:b0 + 513],
                                                        ps[:]), W(("v", cg)))
                pg[nm] = ("s", conv_mms(cch, ps, SE, cg))
            else:
                cgl = conv_g[cch] if isinstance(conv_g[cch], list) else [conv_g[cch]]
                SE = sev(lambda e, ps=ps, b0=b0: e.copy(hsl[:, b0 + 1:b0 + 513],
                                                        ps[:]), W(("t", TMM), *cgl))
                pg[nm] = ("s", SE)

        def finish_window(win):
            nonlocal amb_g, ffT_g, acc2_g, s2t_g
            t0 = win * WIN
            SS = sch.now("s")
            sch.mark("FW s=%d win=%d SS=%d" % (s, win, SS))
            if DEBUG and (s, win) == DBGWIN:
                sch.ev("sy", dma(dbg["acc2"][:, :], acc2[:]), W(("s", SS)),
                       inc="dbq", n=16)
                sch.ev("sy", dma(dbg["hsl"][:, :], hsl[:]), W(("s", SS)),
                       inc="dbq", n=16)
            vev(lambda e: e.tensor_tensor(maxt[:], acc2[:, 0:WIN], acc2[:, WIN:2 * WIN],
                                          ALU.max), W(("s", SS), amb_g))
            vev(lambda e: e.tensor_tensor(mint[:], acc2[:, 0:WIN], acc2[:, WIN:2 * WIN],
                                          ALU.min), lag=1)
            for cch in range(2, HCH):
                vev(lambda e, cch=cch: e.tensor_tensor(
                    maxt[:], maxt[:], acc2[:, cch * WIN:(cch + 1) * WIN], ALU.max),
                    lag=1)
                vev(lambda e, cch=cch: e.tensor_tensor(
                    mint[:], mint[:], acc2[:, cch * WIN:(cch + 1) * WIN], ALU.min),
                    lag=1)
            yield
            vev(lambda e: e.tensor_scalar(mint[:], mint[:], -1.0, None, ALU.mult))
            vam = vev(lambda e: e.tensor_tensor(mint[:], maxt[:], mint[:], ALU.max),
                      W(*(amb_g if isinstance(amb_g, list) else [amb_g])))
            sch.ev("s", lambda e: e.dma_start_transpose(
                amT.ap().rearrange("p (b c) -> p b c", b=4),
                mint.ap().rearrange("p (b c) -> p b c", b=4)),
                W(("v", vam)), inc="s_tam", n=16)
            SQAM = sch.now("s_tam")
            amb_g = ("s_tam", SQAM)
            for tt in range(4):
                vev(lambda e, tt=tt: e.reduce_max(s2col[:, tt:tt + 1],
                                                  amT[:, tt * P:(tt + 1) * P], AX.X),
                    W(("s_tam", SQAM), s2t_g) if tt == 0 else W(),
                    lag=0 if tt == 0 else 1)
                vev(lambda e, tt=tt: e.tensor_scalar(s2col[:, tt:tt + 1],
                                                     s2col[:, tt:tt + 1], 1e-5, None,
                                                     ALU.max))
                vev(lambda e, tt=tt: e.tensor_scalar(s2cb[:, tt:tt + 1],
                                                     s2col[:, tt:tt + 1], mreps[:, 3:4],
                                                     None, ALU.mult))
                vev(lambda e, tt=tt: e.reciprocal(s2col[:, tt:tt + 1],
                                                  s2col[:, tt:tt + 1]))
                vev(lambda e, tt=tt: e.tensor_scalar(s2col[:, tt:tt + 1],
                                                     s2col[:, tt:tt + 1], 127.0, None,
                                                     ALU.mult))
            VS2 = sch.now("v")
            if DEBUG and (s, win) == DBGWIN:
                sch.ev("sy", dma(dbg["scol"][:, 0:4], s2col[:]), W(("v", VS2)),
                       inc="dbq", n=16)
                sch.ev("sy", dma(dbg["scol"][:, 4:8], s2cb[:]), W(("v", VS2)),
                       inc="dbq", n=16)
            for tt in range(4):
                tev(lambda e, tt=tt: e.transpose(psM2a[0:1, tt * P:(tt + 1) * P],
                                                 s2col[:, tt:tt + 1], bI[:]),
                    W(("v", VS2), pg["M2a"]) if tt == 0 else W())
            s2t_g = ("t", sch.now("t"))
            vev(lambda e: e.tensor_copy(srow[:], psM2a[0:1, 0:WIN]),
                W(("t", sch.now("t"))))
            pg["M2a"] = ("v", sch.now("v"))
            tev(lambda e: e.matmul(psM2b[:], bONE[0:1, :], srow[:], start=True,
                                   stop=True), W(("v", sch.now("v")), pg["M2b"]))
            vev(lambda e: e.tensor_copy(s2rep[:], psM2b[:]), W(("t", sch.now("t"))))
            pg["M2b"] = ("v", sch.now("v"))
            VREP = sch.now("v")
            bcast = bass.AP(tensor=s2rep.ap().tensor, offset=s2rep.ap().offset,
                            ap=[list(s2rep.ap().ap[0]), [0, HCH], [1, WIN]])
            vev(lambda e, bcast=bcast: e.tensor_tensor(
                acc2.ap().rearrange("p (c t) -> p c t", c=HCH),
                acc2.ap().rearrange("p (c t) -> p c t", c=HCH), bcast, ALU.mult),
                W(("v", VREP)))
            VQ2 = vev(lambda e: e.tensor_scalar(acc2[:], acc2[:], MAGIC, MAGIC,
                                                ALU.add, ALU.subtract))
            yield
            def w2load(hh):
                rh = hh % 3
                sch.ev("sy", dma(
                    w2r.ap()[:, rh * 16 * P:(rh + 1) * 16 * P].rearrange(
                        "p (a c) -> p a c", a=16),
                    w2q8[hh // 2, (hh % 2) * 16:(hh % 2) * 16 + 16].rearrange(
                        "a p c -> p a c")),
                    W(w2r_g[rh], *PREPW), inc="s_w2%d" % rh, n=16)
            w2load(0)
            w2load(1)
            w2load(2)
            for dc in range(DCH):
                rhA, rhB = (2 * dc) % 3, (2 * dc + 1) % 3
                GQA = sch.now("s_w2%d" % rhA)
                GQB = sch.now("s_w2%d" % rhB)
                ps, nm = (psM2a, "M2a") if dc % 2 == 0 else (psM2b, "M2b")
                for cch in range(HCH):
                    rh = rhA if cch < 16 else rhB
                    ti = cch % 16
                    tev(lambda e, ps=ps, rh=rh, ti=ti, cch=cch: e.matmul(
                        ps[:], w2r[:, (rh * 16 + ti) * P:(rh * 16 + ti + 1) * P],
                        acc2[:, cch * WIN:(cch + 1) * WIN],
                        start=(cch == 0), stop=(cch == HCH - 1)),
                        W(("s_w2%d" % rhA, GQA), ("s_w2%d" % rhB, GQB),
                          ("v", VQ2), pg[nm]) if cch == 0 else W())
                TM2 = sch.now("t")
                w2r_g[rhA] = ("t", TM2)
                w2r_g[rhB] = ("t", TM2)
                if 2 * dc + 3 < 2 * DCH:
                    w2load(2 * dc + 3)
                if 2 * dc + 4 < 2 * DCH:
                    w2load(2 * dc + 4)
                fsl = dc % 2
                SF = sev(lambda e, ps=ps, fsl=fsl: e.copy(
                    ffw[:, fsl * WIN:(fsl + 1) * WIN], ps[:]),
                    W(("t", TM2), ffw_g[fsl]))
                pg[nm] = ("s", SF)
                sch.ev("s", lambda e, dc=dc, fsl=fsl: e.dma_start_transpose(
                    ffT.ap()[:, dc * WIN:(dc + 1) * WIN].rearrange("p (b c) -> p b c",
                                                                   b=4),
                    ffw.ap()[:, fsl * WIN:(fsl + 1) * WIN].rearrange("p (b c) -> p b c",
                                                                     b=4)),
                    W(("s", SF), ffT_g if dc == 0 else ("v", 0)),
                    inc="s_tf%d" % fsl, n=16)
                ffw_g[fsl] = ("s_tf%d" % fsl, sch.now("s_tf%d" % fsl))
                if dc % 4 == 3:
                    yield
            SQF = None
            QFW = [("s_tf0", sch.now("s_tf0")), ("s_tf1", sch.now("s_tf1"))]
            if DEBUG and (s, win) == DBGWIN:
                sch.ev("sy", dma(dbg["ffw"][:, :], ffT[:]), W(*QFW), inc="dbq", n=16)
            for tt in range(4):
                gi = s * CH + win * 4 + tt
                xs = gi % NX1
                osl = (win * 4 + tt) % 2
                ftap = ffT.ap().rearrange("p (dc tt t) -> p dc tt t", dc=DCH, tt=4)[
                    :, :, tt, :]
                vo = vev(lambda e, tt=tt, xs=xs, osl=osl, ftap=ftap: e.scalar_tensor_tensor(
                    outt[:, osl * D:(osl + 1) * D], ftap,
                    s2cb[:, tt:tt + 1], x1[:, xs * D:(xs + 1) * D], ALU.mult, ALU.add),
                    W(*QFW, outt_g[osl]), lag=1)
                sch.ev("s", lambda e, s=s, t0=t0, tt=tt, osl=osl: e.dma_start(
                    out=out[s, t0 + tt * P:t0 + (tt + 1) * P, :],
                    in_=outt[:, osl * D:(osl + 1) * D]), W(("v", vo)),
                    inc="s_o%d" % osl, n=16)
                outt_g[osl] = ("s_o%d" % osl, sch.now("s_o%d" % osl))
                x1_g[xs] = [("s_o%d" % osl, sch.now("s_o%d" % osl))]
                tails_cnt[0] += 1
            ffT_g = ("v", sch.now("v"))
            acc2_g = ("t", sch.now("t"))

        for w in range(NW):
            yield ("q", s * 4 + w)
            if w == 0 and s == 0:
                vev(lambda e: e.memset(
                    hsl.ap().rearrange("p (c k) -> p c k", k=ST)[:, :, 0:1], 0.0),
                    W(*PREPW))
            elif w == 0:
                vev(lambda e: e.memset(
                    hsl.ap().rearrange("p (c k) -> p c k", k=ST)[:, :, 0:1], 0.0),
                    W(*(conv_g[HCH - 1] if isinstance(conv_g[HCH - 1], list)
                        else [conv_g[HCH - 1]])))
            for cch in range(HCH):
                mm1_cch(w, cch)
                if cch % 4 == 3:
                    yield
            if w >= 1:
                for _ in finish_window(w - 1):
                    yield
        vev(lambda e: e.memset(
            hsl.ap().rearrange("p (c k) -> p c k", k=ST)[:, :, 513:514], 0.0),
            W(("s", sch.now("s"))))
        for cch in range(HCH):
            ps, nm = (psM1a, "M1a") if cch % 2 == 0 else (psM1b, "M1b")
            cg = conv_products(cch, 0)
            pg[nm] = ("s", conv_mms(cch, ps, pg[nm][1] if pg[nm][0] == "s" else 0, cg))
            if cch % 8 == 7:
                yield
        for _ in finish_window(NW - 1):
            yield

    ssm_gens = [(s * 2 + h, ssm_half(s, h)) for s in range(BC) for h in range(2)]
    fgens = [ffn_seq(s) for s in range(BC)]
    fneed = []
    for g in fgens:
        r = next(g)
        fneed.append(r[1] if isinstance(r, tuple) else None)
    quarters_done = 0
    fcur = 0
    sidx = 0
    sreq = [None]

    def ffn_tick():
        nonlocal fcur
        if fcur >= BC:
            return False
        if fneed[fcur] is not None and fneed[fcur] >= quarters_done:
            return False
        try:
            r = next(fgens[fcur])
            fneed[fcur] = r[1] if isinstance(r, tuple) else None
        except StopIteration:
            fcur += 1
        return True

    def ssm_tick():
        nonlocal sidx, quarters_done
        if sidx >= len(ssm_gens):
            return False
        if sreq[0] is not None and tails_cnt[0] < sreq[0]:
            return False
        hid, g = ssm_gens[sidx]
        try:
            r = next(g)
            if isinstance(r, tuple):
                if r[0] == "qdone":
                    quarters_done = max(quarters_done, r[1])
                    sreq[0] = None
                else:
                    sreq[0] = r[1]
            else:
                sreq[0] = None
        except StopIteration:
            sidx += 1
            sreq[0] = None
        return True

    while sidx < len(ssm_gens) or fcur < BC:
        p1 = ssm_tick()
        p2 = ffn_tick()
        p4 = ssm_tick()
        p3 = ffn_tick()
        if not (p1 or p2 or p3 or p4):
            raise RuntimeError("pipeline deadlock: sidx=%d sreq=%s fcur=%d fneed=%s "
                               "quarters=%d tails=%d" % (sidx, sreq[0], fcur,
                                                         fneed, quarters_done,
                                                         tails_cnt[0]))

    sch.ev("g", lambda e: e.engine_nop(),
           W(("s_o0", sch.now("s_o0")), ("s_o1", sch.now("s_o1"))))

    sem_names = sorted(set(list(sch.cnt.keys()) +
                           ["v", "s", "t"] +
                           [w[0] for (_e, ws, _f, _i, _n, _l) in sch.events for w in ws]))
    import contextlib
    with contextlib.ExitStack() as stack:
        sems = {nm: stack.enter_context(nc.semaphore(nm)) for nm in sem_names}
        blk = stack.enter_context(nc.Block())

        @blk.sync
        def _(e):
            sch.emit("sy", e, sems)

        @blk.vector
        def _(e):
            sch.emit("v", e, sems, delay=(SERIAL == 11))

        @blk.scalar
        def _(e):
            sch.emit("s", e, sems, delay=(SERIAL == 11))

        @blk.tensor
        def _(e):
            sch.emit("t", e, sems)

        @blk.gpsimd
        def _(e):
            sch.emit("g", e, sems)

    globals()['_last_sch'] = sch
    return nc


def kernel(L0_attn_u, L0_attn_A_log, L0_attn_B, L0_attn_C, L0_attn_D,
           L0_attn_delta, L0_attn_gamma, L0_attn_beta, L0_attn_freq_align,
           L0_attn_reward, L0_ffn_w1, L0_ffn_conv_w, L0_ffn_conv_b, L0_ffn_w2):
    hc = host_consts()
    nc = build_program()
    f32 = lambda a: np.ascontiguousarray(np.asarray(a, dtype=np.float32))
    base = {
        "w1T": f32(np.asarray(L0_ffn_w1).T),
        "w2T": f32(np.asarray(L0_ffn_w2).T),
        "CT": f32(np.asarray(L0_attn_C).T),
        "alog": f32(L0_attn_A_log), "delta": f32(L0_attn_delta),
        "Dvec": f32(L0_attn_D), "fa": f32(L0_attn_freq_align),
        "gamma": f32(L0_attn_gamma), "beta": f32(L0_attn_beta),
        "reward": f32(L0_attn_reward),
        "convw": f32(np.asarray(L0_ffn_conv_w).reshape(HC, 3)),
        "convb": f32(L0_ffn_conv_b),
    }
    base.update(hc)
    uu = f32(L0_attn_u)
    in_maps = []
    for c in range(8):
        m = dict(base)
        m["u"] = np.ascontiguousarray(uu[2 * c:2 * c + 2])
        in_maps.append(m)
    res = run_bass_kernel_spmd(nc, in_maps, core_ids=list(range(8)))
    return np.concatenate([r["out"] for r in res.results], axis=0)


# version 23
# speedup vs baseline: 1.0666x; 1.0067x over previous
import numpy as np
import concourse.bass as bass
import concourse.mybir as mybir
from concourse.bass_utils import run_bass_kernel_spmd

F32 = mybir.dt.float32
BF16 = mybir.dt.bfloat16
F16 = mybir.dt.float16
F8 = mybir.dt.float8e4
AT = mybir.ActivationFunctionType
ALU = mybir.AluOpType
AX = mybir.AxisListType

MAGIC = 12582912.0
NEGBIG = -1.0e30
EPS = 1e-5

BC, L, D, DCH, N, M, CH = 2, 2048, 1024, 8, 64, 16, 16
HC, HCH, WIN, NW, P = 4096, 32, 512, 4, 128
ST = 514

_CONST_CACHE = {}
DEBUG = False
SERIAL = False
DBGWIN = (0, 1)
DBGPROBE = None
BLEVEL = 0
SLO, SHI = 0, 10 ** 9
SELFWAIT = True
RACECHECK = False


def host_consts():
    if _CONST_CACHE:
        return _CONST_CACHE
    c = {}
    j = np.arange(P)
    c["cJ127"] = np.broadcast_to((127.0 - j)[:, None], (P, N)).astype(np.float32).copy()
    c["cT1"] = np.broadcast_to((j + 1.0)[None, :], (N, P)).astype(np.float32).copy()
    c["cK16"] = np.broadcast_to(np.arange(M, dtype=np.float64)[None, :], (N, M)).astype(np.float32).copy()
    J2 = np.zeros((4, P, P), np.float32)
    M2 = np.full((4, P, P), NEGBIG, np.float32)
    for g in range(4):
        for jj in range(P):
            q, jl = jj // M, jj % M
            for e in range(2):
                if q == 2 * g + e:
                    J2[g, jj, e * N:(e + 1) * N] = 15.0 - jl
                    M2[g, jj, e * N:(e + 1) * N] = 0.0
    c["cJ2"], c["cM2"] = J2.reshape(4 * P, P).copy(), M2.reshape(4 * P, P).copy()
    T2 = np.zeros((4, P, P), np.float32)
    M2W = np.full((4, P, P), NEGBIG, np.float32)
    for pp in range(4):
        for e in range(2):
            qp = 2 * pp + e
            lo = M * (qp + 1)
            for t in range(lo, P):
                T2[pp, e * N:(e + 1) * N, t] = t - lo + 1.0
                M2W[pp, e * N:(e + 1) * N, t] = 0.0
    c["cT2"], c["cM2W"] = T2.reshape(4 * P, P).copy(), M2W.reshape(4 * P, P).copy()
    c["cI"] = np.eye(P, dtype=np.float32)
    c["cONE"] = np.ones((P, P), np.float32)
    _CONST_CACHE.update(c)
    return c


class Sched:

    def __init__(self):
        self.events = []
        self.cnt = {}
        self.pending = {}
        self.marks = []

    def mark(self, name):
        self.marks.append((name, len(self.events), dict(self.cnt)))

    def barrier(self, engines=("sy", "v", "s", "t", "g")):
        snap = [(s, v) for s, v in self.cnt.items() if v > 0]
        for eng in engines:
            self.pending[eng] = list(snap)

    def now(self, sem):
        return self.cnt.get(sem, 0)

    def ev(self, eng, fn, waits=(), inc=None, n=1, lag=0):
        if SERIAL == 1:
            w = [(s, v) for s, v in self.cnt.items() if v > 0]
        elif SERIAL == 2 and eng in ("v", "s", "t"):
            w = [(s, v) for s, v in self.cnt.items()
                 if v > 0 and s in ("v", "s", "t")]
            w += [(s, v) for (s, v) in waits
                  if s is not None and v > 0 and s not in ("v", "s", "t")]
        elif SERIAL == 7 and eng == "v":
            vi = self.cnt.get("v", 0)
            if SLO <= vi < SHI:
                w = [(s, v) for s, v in self.cnt.items() if v > 0 and s == "s"]
                w += [(s, v) for (s, v) in waits if s is not None and v > 0]
            else:
                w = [(s, v) for (s, v) in waits if s is not None and v > 0]
        elif SERIAL == 8 and eng == "s":
            si = self.cnt.get("s", 0)
            if SLO <= si < SHI:
                w = [(s, v) for s, v in self.cnt.items() if v > 0 and s == "v"]
                w += [(s, v) for (s, v) in waits if s is not None and v > 0]
            else:
                w = [(s, v) for (s, v) in waits if s is not None and v > 0]
        elif SERIAL == 9:
            if eng == "v":
                w = [(s, v) for s, v in self.cnt.items() if v > 0 and s == "s"]
                w += [(s, v) for (s, v) in waits if s is not None and v > 0]
            elif eng == "s":
                si = self.cnt.get("s", 0)
                w = list(waits)
                if SLO <= si < SHI:
                    w = [(s, v) for s, v in self.cnt.items() if v > 0 and s == "v"] + w
                w = [(s, v) for (s, v) in w if s is not None and v > 0]
            else:
                w = [(s, v) for (s, v) in waits if s is not None and v > 0]
        elif SERIAL == 10:
            if eng == "s":
                w = [(s, v) for s, v in self.cnt.items() if v > 0 and s == "v"]
                w += [(s, v) for (s, v) in waits if s is not None and v > 0]
            elif eng == "v":
                vi = self.cnt.get("v", 0)
                w = list(waits)
                if SLO <= vi < SHI:
                    w = [(s, v) for s, v in self.cnt.items() if v > 0 and s == "s"] + w
                w = [(s, v) for (s, v) in w if s is not None and v > 0]
            else:
                w = [(s, v) for (s, v) in waits if s is not None and v > 0]
        elif SERIAL in (4, 5, 6):
            pair = {4: ("v", "s"), 5: ("v", "t"), 6: ("s", "t")}[SERIAL]
            if eng in pair:
                w = [(s, v) for s, v in self.cnt.items() if v > 0 and s in pair]
                w += [(s, v) for (s, v) in waits
                      if s is not None and v > 0 and s not in pair]
            else:
                w = [(s, v) for (s, v) in waits if s is not None and v > 0]
        elif SERIAL == 3:
            w = [(s, v) for s, v in self.cnt.items()
                 if v > 0 and s not in ("v", "s", "t")]
            w += [(s, v) for (s, v) in waits if s is not None and v > 0]
        else:
            w = [(s, v) for (s, v) in waits if s is not None and v > 0]
        if eng in self.pending:
            w = self.pending.pop(eng) + w
        self.events.append((eng, w, fn, inc, n, lag))
        if inc is not None:
            self.cnt[inc] = self.cnt.get(inc, 0) + n

    def emit(self, eng_name, handle, sems, delay=False):
        own = 0
        for eng, waits, fn, inc, n, lag in self.events:
            if eng != eng_name:
                continue
            if SELFWAIT and eng_name in ("v", "s") and own - lag > 0:
                handle.wait_ge(sems[eng_name], own - lag)
            for s, v in waits:
                if s == eng_name and v <= own - lag:
                    continue
                handle.wait_ge(sems[s], v)
            inst = fn(handle)
            if inc == eng_name:
                own += n
            if inc is not None:
                inst.then_inc(sems[inc], n)


def build_program():
    nc = bass.Bass("TRN2", target_bir_lowering=False,
                   detect_race_conditions=RACECHECK)
    hc = host_consts()

    u = nc.dram_tensor("u", [BC, L, D], F32, kind="ExternalInput").ap()
    out = nc.dram_tensor("out", [BC, L, D], F32, kind="ExternalOutput").ap()
    w1T = nc.dram_tensor("w1T", [D, HC], F32, kind="ExternalInput").ap()
    w2T = nc.dram_tensor("w2T", [HC, D], F32, kind="ExternalInput").ap()
    CTd = nc.dram_tensor("CT", [N, D], F32, kind="ExternalInput").ap()
    alog = nc.dram_tensor("alog", [N], F32, kind="ExternalInput").ap()
    deltad = nc.dram_tensor("delta", [D], F32, kind="ExternalInput").ap()
    Dd = nc.dram_tensor("Dvec", [D], F32, kind="ExternalInput").ap()
    fad = nc.dram_tensor("fa", [D], F32, kind="ExternalInput").ap()
    gammad = nc.dram_tensor("gamma", [D], F32, kind="ExternalInput").ap()
    betad = nc.dram_tensor("beta", [D], F32, kind="ExternalInput").ap()
    rewardd = nc.dram_tensor("reward", [D], F32, kind="ExternalInput").ap()
    convwd = nc.dram_tensor("convw", [HC, 3], F32, kind="ExternalInput").ap()
    convbd = nc.dram_tensor("convb", [HC], F32, kind="ExternalInput").ap()
    cst = {k: nc.dram_tensor(k, list(v.shape), F32, kind="ExternalInput").ap()
           for k, v in hc.items()}
    dbg = {}
    if DEBUG:
        dbg["xh"] = nc.dram_tensor("dbg_xh", [L, D], BF16, kind="ExternalOutput").ap()
        dbg["y2p"] = nc.dram_tensor("dbg_y2p", [L, D], BF16, kind="ExternalOutput").ap()
        dbg["intra"] = nc.dram_tensor("dbg_intra", [L, D], F16, kind="ExternalOutput").ap()
        dbg["x1"] = nc.dram_tensor("dbg_x1", [BC * L, D], F16, kind="ExternalOutput").ap()
        dbg["xq1"] = nc.dram_tensor("dbg_xq1", [L, D], F16, kind="ExternalOutput").ap()
        dbg["hsl"] = nc.dram_tensor("dbg_hsl", [P, HCH * ST], BF16, kind="ExternalOutput").ap()
        dbg["acc2"] = nc.dram_tensor("dbg_acc2", [P, HCH * WIN], F16, kind="ExternalOutput").ap()
        dbg["scol"] = nc.dram_tensor("dbg_scol", [P, 8], F32, kind="ExternalOutput").ap()
        dbg["ffw"] = nc.dram_tensor("dbg_ffw", [P, DCH * WIN], BF16, kind="ExternalOutput").ap()
        dbg["kt"] = nc.dram_tensor("dbg_kt", [P, DCH * M], F32, kind="ExternalOutput").ap()
        dbg["mr"] = nc.dram_tensor("dbg_mr", [P, 8], F32, kind="ExternalOutput").ap()
        dbg["probe"] = nc.dram_tensor("dbg_probe", [P, WIN], F16, kind="ExternalOutput").ap()
    w1q8 = nc.dram_tensor("w1q8", [HCH, DCH, P, P], F8, kind="Internal").ap()
    w2q8 = nc.dram_tensor("w2q8", [DCH, HCH, P, P], F8, kind="Internal").ap()

    sch = Sched()
    keep = []
    SSM_BYTES = 51200
    FFN_BYTES = 87296
    ARENA_BYTES = SSM_BYTES + FFN_BYTES
    _off = {"arena_ssm": [0], "arena_ffn": [SSM_BYTES], "arena_setup": [SSM_BYTES]}
    _REG_END = {"arena_ssm": SSM_BYTES, "arena_ffn": ARENA_BYTES,
                "arena_setup": ARENA_BYTES}
    _arena = [None]

    def _align(x, a=64):
        return (x + a - 1) // a * a

    class TV:

        def __init__(self, ap):
            self._ap = ap

        def ap(self):
            return self._ap

        def __getitem__(self, sl):
            return self._ap[sl]

    def sb(name, shape, dtype, region="perm"):
        nbytes = int(np.prod(shape[1:])) * mybir.dt.size(dtype)
        if region == "perm":
            h = nc.alloc_sbuf_tensor(name, shape, dtype)
            keep.append(h)
            return h
        o = _off[region][0]
        _off[region][0] = _align(o + nbytes)
        assert _off[region][0] <= _REG_END.get(region, ARENA_BYTES), (
            name, region, _off[region][0])
        es = mybir.dt.size(dtype)
        assert o % es == 0
        view = _arena[0].bitcast(dtype).ap()[:, o // es: o // es + int(np.prod(shape[1:]))]
        if len(shape) == 2 and shape[0] < 128:
            view = view[0:shape[0], :]
        return TV(view)

    def start_arena():
        _arena[0] = nc.alloc_sbuf_tensor("arena", [P, ARENA_BYTES], mybir.dt.uint8)
        keep.append(_arena[0])

    bI = sb("bI", [P, P], F32)
    bONE = sb("bONE", [P, P], F32)
    bAlog = sb("bAlog", [N, 1], F32)
    bd0 = sb("bd0", [1, 1], F32)
    rrew = sb("rrew", [1, D], BF16)
    cD = sb("cD", [P, DCH], F32)
    cfa = sb("cfa", [P, DCH], F32)
    cDfa = sb("cDfa", [P, DCH], F32)
    wcv = sb("wcv", [P, HCH * 4], F32)
    dA = sb("dA", [N, 1], F32)
    dA2 = sb("dA2", [P, 1], F32)
    rS = sb("rS", [N, 1], F32)
    V1 = sb("V1", [P, N], BF16)
    W1Ts = sb("W1Ts", [N, P], BF16)
    V2 = sb("V2", [P, 4 * P], BF16)
    W2s = sb("W2s", [P, 4 * P], BF16)
    P2m = sb("P2m", [N, M], BF16)
    CtT = sb("CtT", [N, D], BF16)
    Ct2 = sb("Ct2", [P, D], BF16)
    KtT = sb("KtT", [P, DCH * M], F32)
    gRep = sb("gRep", [P, D], BF16)
    bRep = sb("bRep", [P, D], BF16)
    mcols = sb("mcols", [P, 48], F32)
    msum = sb("msum", [P, 4], F32)
    mreps = sb("mreps", [P, 8], F32)
    stat6 = sb("stat6", [P, 12], F32)
    stat2 = sb("stat2", [P, 2], F32)
    colm = sb("colm", [P, 8], F32)
    ceps = sb("ceps", [P, 1], F32)
    c128 = sb("c128", [P, 1], F32)

    NX1, NQT = 12, 8
    x1 = sb("x1", [P, NX1 * D], F16)
    xq1T = sb("xq1T", [P, NQT * D], BF16)
    w1r = sb("w1r", [P, 24 * P], F8)
    w2r = sb("w2r", [P, 48 * P], F8)
    bIb = sb("bIb", [P, P], BF16)
    outt = sb("outt", [P, 2 * D], F32)
    start_arena()
    bJ127 = sb("bJ127", [P, N], F32, "arena_setup")
    bT1 = sb("bT1", [N, P], F32, "arena_setup")
    bK16 = sb("bK16", [N, M], F32, "arena_setup")
    bJ2 = sb("bJ2", [P, 4 * P], F32, "arena_setup")
    bM2 = sb("bM2", [P, 4 * P], F32, "arena_setup")
    bT2 = sb("bT2", [P, 4 * P], F32, "arena_setup")
    bM2W = sb("bM2W", [P, 4 * P], F32, "arena_setup")
    bCT = sb("bCT", [N, D], F32, "arena_setup")
    rfa = sb("rfa", [1, D], F32, "arena_setup")
    rgam = sb("rgam", [1, D], F32, "arena_setup")
    rbet = sb("rbet", [1, D], F32, "arena_setup")
    tmpc = sb("tmpc", [P, P], F32, "arena_setup")
    tmpd = sb("tmpd", [N, P], F32, "arena_setup")
    wbuf = sb("wbuf", [P, 2 * 2048], F32, "arena_setup")
    wqb = sb("wqb", [P, 2 * 2048], F8, "arena_setup")
    wdump = sb("wdump", [P, 2 * 2048], BF16, "arena_setup")
    h1 = sb("h1", [N, D], F32, "arena_ssm")
    hC1 = sb("hC1", [N, D], BF16, "arena_ssm")
    y2t = sb("y2t", [P, D], BF16, "arena_ssm")

    xq1t = sb("xq1t", [P, D], F16, "arena_ssm")
    xh = sb("xh", [P, 2 * D], BF16, "arena_ssm")
    intraT = sb("intraT", [P, D], F16, "arena_ssm")
    tmpa = sb("tmpa", [P, D], BF16, "arena_ssm")
    _tmpb_off = _off["arena_ssm"][0] - 2048
    tmpb = TV(_arena[0].bitcast(F16).ap()[:, _tmpb_off // 2:_tmpb_off // 2 + D])
    xhT = sb("xhT", [P, DCH * 512], BF16, "arena_ssm")
    accb = sb("accb", [P, DCH * 512], F16, "arena_ssm")
    y2p = sb("y2p", [P, 4 * D], BF16, "arena_ssm")
    G2C = sb("G2C", [P, 4 * D], BF16, "arena_ssm")
    hsl = sb("hsl", [P, HCH * ST], BF16, "arena_ffn")
    ubuf = sb("ubuf", [P, 2 * 512], BF16, "arena_ffn")
    vbuf = sb("vbuf", [P, 2 * 512], BF16, "arena_ffn")
    cbuf = sb("cbuf", [P, 2 * 512], BF16, "arena_ffn")
    acc2 = sb("acc2", [P, HCH * WIN], F16, "arena_ffn")
    maxt = sb("maxt", [P, WIN], F16, "arena_ffn")
    mint = sb("mint", [P, WIN], F16, "arena_ffn")
    amT = maxt
    s2col = sb("s2col", [P, 4], F32, "arena_ffn")
    s2cb = sb("s2cb", [P, 4], F32, "arena_ffn")
    s2rep = sb("s2rep", [P, WIN], BF16, "arena_ffn")
    ffw = sb("ffw", [P, 2 * WIN], BF16, "arena_ffn")
    srow = sb("srow", [1, WIN], F32, "arena_ffn")
    ffT = sb("ffT", [P, 4 * D], BF16, "arena_ffn")

    psA = nc.alloc_psum_tensor("psA", [P, 1024], F32)
    psB = nc.alloc_psum_tensor("psB", [P, 1024], F32)
    psM1a = nc.alloc_psum_tensor("psM1a", [P, 512], F32)
    psM1b = nc.alloc_psum_tensor("psM1b", [P, 512], F32)
    psM2a = nc.alloc_psum_tensor("psM2a", [P, 512], F32)
    psM2b = nc.alloc_psum_tensor("psM2b", [P, 512], F32)

    pg = {"A": ("v", 0), "B": ("v", 0), "M1a": ("s", 0), "M1b": ("s", 0),
          "M2a": ("s", 0), "M2b": ("s", 0)}
    xh_g = [[], []]
    intraT_g = [("v", 0), ("v", 0)]
    G2C_g = [("t", 0), ("t", 0)]
    y2p_g = [("v", 0)] * 8
    xq1t_g = ("sq", 0)
    amb_g = ("sq", 0)
    ffw_g = [("v", 0), ("v", 0)]
    ffT_g = ("v", 0)
    outt_g = [("ou", 0), ("ou", 0)]
    w1r_g = [("t", 0), ("t", 0), ("t", 0)]
    w2r_g = [("t", 0), ("t", 0), ("t", 0)]
    conv_g = [("v", 0)] * HCH
    acc2_g = ("t", 0)
    xstore_g = [("ou", 0)] * CH
    xstage_g = [("v", 0), ("v", 0)]
    tmpa_g = ("v", 0)
    tmpb_g = ("sq", 0)
    y2s = [0] * 8
    s2t_g = ("t", 0)
    ubuf_g = [("t", 0), ("t", 0)]
    pg_evac = [("s", 0)]

    def W(*pairs):
        return list(pairs)

    def vev(fn, waits=(), lag=0):
        sch.ev("v", fn, waits, inc="v", lag=lag)
        return sch.now("v")

    def sev(fn, waits=(), lag=0):
        sch.ev("s", fn, waits, inc="s", lag=lag)
        return sch.now("s")

    def tev(fn, waits=()):
        sch.ev("t", fn, waits, inc="t")
        return sch.now("t")

    def dma(dst, src, slow=False):
        if slow:
            return lambda e: e.dma_start(out=dst, in_=src,
                                         allow_slow_non_contiguous=True)
        return lambda e: e.dma_start(out=dst, in_=src)

    for dst, src in [
        (bJ127[:], cst["cJ127"]), (bT1[:], cst["cT1"]), (bK16[:], cst["cK16"]),
        (bI[:], cst["cI"]), (bONE[:], cst["cONE"]),
        (bAlog[:], alog.rearrange("(p o) -> p o", o=1)),
        (bd0[:], deltad[0:1].rearrange("(p o) -> p o", o=1)),
        (bCT[:], CTd),
        (rfa[:], fad.rearrange("(o d) -> o d", o=1)),
        (rgam[:], gammad.rearrange("(o d) -> o d", o=1)),
        (rbet[:], betad.rearrange("(o d) -> o d", o=1)),

        (cD[:], Dd.rearrange("(c p) -> p c", p=P)),
        (cfa[:], fad.rearrange("(c p) -> p c", p=P)),
        (wcv.ap().rearrange("p (c k) -> p c k", k=4)[:, :, 0:3],
         convwd.rearrange("(c p) k -> p c k", p=P)),
        (wcv.ap().rearrange("p (c k) -> p c k", k=4)[:, :, 3:4],
         convbd.rearrange("(c p) -> p c", p=P).rearrange("p c -> p c 1") if False else
         convbd.rearrange("(c p o) -> p c o", p=P, o=1)),
    ]:
        sch.ev("sy", dma(dst, src, slow=True), inc="s_ld", n=16)
    for g in range(4):
        for bdst, key in [(bJ2, "cJ2"), (bM2, "cM2"), (bT2, "cT2"), (bM2W, "cM2W")]:
            sch.ev("sy", dma(bdst[:, g * P:(g + 1) * P], cst[key][g * P:(g + 1) * P, :]),
                   inc="s_ld", n=16)
    sch.ev("g", dma(rrew[:], rewardd.rearrange("(o d) -> o d", o=1)), inc="s_rr", n=16)
    LD0 = sch.now("s_ld")

    vev(lambda e: e.tensor_copy(bIb[:], bI[:]), W(("s_ld", LD0)))
    vev(lambda e: e.memset(ceps[:], EPS))
    vev(lambda e: e.memset(c128[:], 128.0))
    sev(lambda e: e.activation(colm[0:N, 0:1], bAlog[:], AT.Exp), W(("s_ld", LD0)))
    tev(lambda e: e.matmul(psA[0:N, 0:1], bONE[0:1, 0:N], bd0[:], start=True, stop=True),
        W(("s_ld", LD0)))
    vev(lambda e: e.tensor_copy(colm[0:N, 1:2], psA[0:N, 0:1]), W(("t", sch.now("t"))))
    vev(lambda e: e.tensor_scalar(dA[:], colm[0:N, 0:1], colm[0:N, 1:2], -1.0,
                                  ALU.mult, ALU.mult), W(("s", sch.now("s"))))
    VDA = sch.now("v")
    pg["A"] = ("v", VDA)
    sev(lambda e: e.activation(rS[:], dA[:], AT.Exp, scale=c128[0:N, 0:1]), W(("v", VDA)))
    vev(lambda e: e.tensor_copy(dA2[0:N, :], dA[:]))
    vev(lambda e: e.tensor_copy(dA2[N:P, :], dA[:]))
    VDA2 = sch.now("v")
    tev(lambda e: e.transpose(psB[0:1, 0:N], dA[:], bI[0:N, 0:N]), W(("v", VDA)))
    vev(lambda e: e.tensor_copy(srow[0:1, 0:N], psB[0:1, 0:N]), W(("t", sch.now("t"))))
    vev(lambda e: e.tensor_copy(srow[0:1, N:2 * N], srow[0:1, 0:N]))
    VROW = sch.now("v")
    pg["B"] = ("v", VROW)
    tev(lambda e: e.matmul(psA[:, 0:N], bONE[0:1, :], srow[0:1, 0:N], start=True, stop=True),
        W(("v", VROW), pg["A"]))
    tev(lambda e: e.matmul(psB[:, 0:2 * N], bONE[0:1, :], srow[0:1, 0:2 * N],
                           start=True, stop=True), W(pg["B"]))
    TGRID = sch.now("t")
    vev(lambda e: e.tensor_mul(tmpc[:, 0:N], bJ127[:], psA[:, 0:N]), W(("t", TGRID)))
    sev(lambda e: e.activation(V1[:], tmpc[:, 0:N], AT.Exp), W(("v", sch.now("v"))))
    for g in range(4):
        vev(lambda e, g=g: e.tensor_mul(tmpc[:], bJ2[:, g * P:(g + 1) * P], psB[:, 0:P]),
            W(("t", TGRID), ("s", sch.now("s"))))
        vev(lambda e, g=g: e.tensor_add(tmpc[:], tmpc[:], bM2[:, g * P:(g + 1) * P]))
        sev(lambda e, g=g: e.activation(V2[:, g * P:(g + 1) * P], tmpc[:], AT.Exp),
            W(("v", sch.now("v"))))
    for g in range(4):
        vev(lambda e, g=g: e.tensor_scalar(tmpc[:], bT2[:, g * P:(g + 1) * P], dA2[:, 0:1],
                                           None, ALU.mult),
            W(("v", VDA2), ("s", sch.now("s"))))
        vev(lambda e, g=g: e.tensor_add(tmpc[:], tmpc[:], bM2W[:, g * P:(g + 1) * P]))
        sev(lambda e, g=g: e.activation(W2s[:, g * P:(g + 1) * P], tmpc[:], AT.Exp),
            W(("v", sch.now("v"))))
    vev(lambda e: e.tensor_scalar(tmpd[:], bT1[:], dA[:, 0:1], None, ALU.mult),
        W(("s", sch.now("s"))))
    sev(lambda e: e.activation(W1Ts[:], tmpd[:], AT.Exp), W(("v", sch.now("v"))))
    vev(lambda e: e.tensor_scalar(tmpd[:, 0:M], bK16[:], dA[:, 0:1], None, ALU.mult),
        W(("s", sch.now("s"))))
    sev(lambda e: e.activation(P2m[:], tmpd[0:N, 0:M], AT.Exp), W(("v", sch.now("v"))))
    for h in range(2):
        tev(lambda e, h=h: e.matmul(psA[0:N, h * 512:(h + 1) * 512], bONE[0:1, 0:N],
                                    rfa[:, h * 512:(h + 1) * 512], start=True, stop=True),
            W(("v", sch.now("v"))) if h == 0 else W())
    vev(lambda e: e.tensor_mul(CtT[:], bCT[:], psA[0:N, :]), W(("t", sch.now("t"))))
    vev(lambda e: e.tensor_copy(Ct2[0:N, :], CtT[:]))
    vev(lambda e: e.tensor_copy(Ct2[N:P, :], CtT[:]))
    VCT = sch.now("v")
    pg["A"] = ("v", VCT)
    for h in range(2):
        tev(lambda e, h=h: e.matmul(psB[:, h * 512:(h + 1) * 512], bONE[0:1, :],
                                    rgam[:, h * 512:(h + 1) * 512], start=True, stop=True),
            W(("v", VCT)) if h == 0 else W())
    vev(lambda e: e.tensor_copy(gRep[:], psB[:]), W(("t", sch.now("t"))))
    for h in range(2):
        tev(lambda e, h=h: e.matmul(psB[:, h * 512:(h + 1) * 512], bONE[0:1, :],
                                    rbet[:, h * 512:(h + 1) * 512], start=True, stop=True),
            W(("v", sch.now("v"))) if h == 0 else W())
    vev(lambda e: e.tensor_copy(bRep[:], psB[:]), W(("t", sch.now("t"))))
    pg["B"] = ("v", sch.now("v"))
    vev(lambda e: e.tensor_mul(cDfa[:], cD[:], cfa[:]), W(("s_ld", LD0)))
    for dc in range(DCH):
        tev(lambda e, dc=dc: e.matmul(psA[:, dc * M:(dc + 1) * M], CtT[:, dc * P:(dc + 1) * P],
                                      P2m[:], start=True, stop=True),
            W(("s", sch.now("s")), pg["A"]) if dc == 0 else W())
    vev(lambda e: e.tensor_copy(KtT[:], psA[:, 0:DCH * M]), W(("t", sch.now("t"))))
    vev(lambda e: e.tensor_tensor(
        KtT.ap().rearrange("p (c k) -> p c k", k=M)[:, :, 0:1],
        KtT.ap().rearrange("p (c k) -> p c k", k=M)[:, :, 0:1],
        cDfa.ap().rearrange("p (c o) -> p c o", o=1), ALU.add))
    pg["A"] = ("v", sch.now("v"))

    w1s = w1T.rearrange("(a p) h -> a p h", p=P)
    w2s = w2T.rearrange("(a p) h -> a p h", p=P)
    tiles = [(0, a, cb) for a in range(DCH) for cb in range(2)] + \
            [(1, a, 0) for a in range(HCH)]
    tw = {0: 2048, 1: 1024}
    svals = []
    for ti, (wi, a, cb) in enumerate(tiles):
        slot = ti % 2
        wd = tw[wi]
        src = w1s[a, :, cb * 2048:(cb + 1) * 2048] if wi == 0 else w2s[a, :, :]
        guard = svals[ti - 2] if ti >= 2 else 0
        sch.ev("sy", dma(wbuf[:, slot * 2048:slot * 2048 + wd], src),
               W(("s", guard)), inc="s_pl%d" % slot, n=16)
        mi = ti
        sdone = sev(lambda e, slot=slot, wd=wd, mi=mi: e.activation(
            wdump[:, 0:wd], wbuf[:, slot * 2048:slot * 2048 + wd], AT.Abs,
            accum_out=mcols[:, mi:mi + 1]), W(("s_pl%d" % slot, sch.now("s_pl%d" % slot))))
        svals.append(sdone)
    SP1 = sch.now("s")
    for wi, (c0, c1) in [(0, (0, 16)), (1, (16, 48))]:
        vev(lambda e, wi=wi, c0=c0, c1=c1: e.reduce_sum(
            msum[:, wi:wi + 1], mcols[:, c0:c1], AX.X), W(("s", SP1)))
        tev(lambda e, wi=wi: e.matmul(psA[0:1, wi:wi + 1], msum[:, wi:wi + 1],
                                      bONE[:, 0:1], start=True, stop=True),
            W(("v", sch.now("v")), pg["A"]) )
        vev(lambda e, wi=wi: e.tensor_scalar(colm[0:1, 4 + wi:5 + wi], psA[0:1, wi:wi + 1],
                                             1.0 / (HC * D), 1e-5, ALU.mult, ALU.max),
            W(("t", sch.now("t"))))
        tev(lambda e, wi=wi: e.matmul(psA[0:P, 8 + wi:9 + wi] if False else psA[:, 8 + wi:9 + wi],
                                      bONE[0:1, :], colm[0:1, 4 + wi:5 + wi],
                                      start=True, stop=True), W(("v", sch.now("v"))))
        vev(lambda e, wi=wi: e.tensor_copy(mreps[:, 4 + wi:5 + wi], psA[:, 8 + wi:9 + wi]),
            W(("t", sch.now("t"))))
        vev(lambda e, wi=wi: e.reciprocal(mreps[:, wi:wi + 1], mreps[:, 4 + wi:5 + wi]))
        vev(lambda e, wi=wi: e.tensor_scalar(mreps[:, 2 + wi:3 + wi], mreps[:, 4 + wi:5 + wi],
                                             1.0 / 127.0, None, ALU.mult))
    VP1 = sch.now("v")
    pg["A"] = ("v", VP1)

    vev(lambda e: e.memset(mreps[:, 6:7], 192.0), W(("v", VP1)))
    vev(lambda e: e.memset(mreps[:, 7:8], -192.0))
    VMS = sch.now("v")
    vq_last = [VMS, VMS]
    sact1_last = [0, 0]
    for ti, (wi, a, cb) in enumerate(tiles):
        slot = ti % 2
        wd = tw[wi]
        src = w1s[a, :, cb * 2048:(cb + 1) * 2048] if wi == 0 else w2s[a, :, :]
        guard_s = sact1_last[slot]
        sch.ev("sy", dma(wbuf[:, slot * 2048:slot * 2048 + wd], src),
               W(("v", VMS), ("s", guard_s)), inc="s_pl%d" % slot, n=16)
        SYD = sch.now("s_pl%d" % slot)
        sev(lambda e, slot=slot, wd=wd, wi=wi: e.activation(
            wdump[:, slot * 2048:slot * 2048 + wd],
            wbuf[:, slot * 2048:slot * 2048 + wd], AT.Identity,
            bias=mreps[:, 6:7], scale=mreps[:, wi:wi + 1]),
            W(("s_pl%d" % slot, SYD), ("v", vq_last[slot])), lag=1)
        sact1_last[slot] = sch.now("s")
        SACT2 = sev(lambda e, slot=slot, wd=wd: e.activation(
            wdump[:, slot * 2048:slot * 2048 + wd],
            wdump[:, slot * 2048:slot * 2048 + wd], AT.Identity,
            bias=mreps[:, 7:8]))
        vq = vev(lambda e, slot=slot, wd=wd: e.tensor_scalar(
            wqb[:, slot * 2048:slot * 2048 + wd],
            wdump[:, slot * 2048:slot * 2048 + wd],
            -1.0, 1.0, ALU.max, ALU.min),
            W(("s", SACT2), ("s_ps%d" % slot, sch.now("s_ps%d" % slot))))
        vq_last[slot] = vq
        if wi == 0:
            dst = w1q8[16 * cb:16 * cb + 16, a].rearrange("a p c -> p a c")
        else:
            dst = w2q8[:, a].rearrange("b p c -> p b c")
        srcq = wqb.ap()[:, slot * 2048:slot * 2048 + wd].rearrange(
            "p (a c) -> p a c", c=P)
        sch.ev("g", dma(dst, srcq), W(("v", vq)), inc="s_ps%d" % slot, n=16)
    GPREP = sch.now("s_ps0") + sch.now("s_ps1")
    VPREP = sch.now("v")
    if DEBUG and False:
        sch.ev("g", dma(dbg["kt"][:, :], KtT[:]), W(("v", VPREP)), inc="gq", n=16)
        sch.ev("g", dma(dbg["mr"][:, :], mreps[:]), W(("v", VPREP)), inc="gq", n=16)

    store_cnt = [0]

    def dbgev(key, dst_slice, src_ap, waits):
        if DEBUG:
            sch.ev("g", dma(dst_slice, src_ap), waits, inc="gq", n=16)
            return ("gq", sch.now("gq"))
        return ("gq", 0)

    NX1S = ["s_x%d" % i for i in range(NX1)]
    NQTS = ["s_tq%d" % i for i in range(NQT)]
    x1_g = [[("v", 0)]] * NX1
    xq1_g = [("t", 0)] * NQT
    xq1_done = {}
    xhT_g = [("v", 0)]
    h1_g = [("t", 0)]
    PREPW = [("s_ps0", sch.now("s_ps0")), ("s_ps1", sch.now("s_ps1"))]
    first_ring = [True]
    tails_cnt = [0]
    probe_hits = [0]

    def stage_A(s, ci):
        nonlocal tmpa_g
        t0 = ci * P
        slot = ci % 2
        qc = ci % 4
        gi = s * CH + ci
        xs = gi % NX1
        prev = gi - NX1
        xw = [("s_o%d" % (prev % 2), 16 * (prev // 2 + 1))] if prev >= 0 else []
        sch.ev("g", dma(x1[:, xs * D:(xs + 1) * D], u[s, t0:t0 + P, :]),
               W(*xw), inc=NX1S[xs], n=16)
        SYX = sch.now(NX1S[xs])
        vev(lambda e, xs=xs: e.bn_stats(stat6[:, 0:6], x1[:, xs * D:xs * D + 512]),
            W((NX1S[xs], SYX)))
        vev(lambda e, xs=xs: e.bn_stats(stat6[:, 6:12],
                                        x1[:, xs * D + 512:(xs + 1) * D]), lag=1)
        vev(lambda e: e.bn_aggr(stat2[:], stat6[:]))
        sev(lambda e: e.activation(colm[:, 1:2], stat2[:, 1:2], AT.Sqrt,
                                   bias=ceps[:, 0:1]), W(("v", sch.now("v"))), lag=1)
        vev(lambda e: e.reciprocal(colm[:, 1:2], colm[:, 1:2]), W(("s", sch.now("s"))))
        vxh = vev(lambda e, xs=xs, slot=slot: e.tensor_scalar(
            xh[:, slot * D:(slot + 1) * D], x1[:, xs * D:(xs + 1) * D],
            stat2[:, 0:1], colm[:, 1:2], ALU.subtract, ALU.mult), xh_g[slot])
        sch.ev("s", lambda e, slot=slot: e.dma_start_transpose(
            tmpa.ap().rearrange("p (c t) -> p c t", c=DCH),
            xh.ap()[:, slot * D:(slot + 1) * D].rearrange("p (c t) -> p c t", c=DCH)),
            W(("v", vxh), tmpa_g, tmpb_g), inc="s_txh", n=16, lag=2)
        SQXH = sch.now("s_txh")
        vts = vev(lambda e, qc=qc: e.tensor_copy(
            xhT.ap().rearrange("p (dc t) -> p dc t", dc=DCH)[:, :, qc * P:(qc + 1) * P],
            tmpa.ap().rearrange("p (dc t) -> p dc t", dc=DCH)),
            W(("s_txh", SQXH), *xhT_g))
        tmpa_g = ("v", vts)
        for g in range(4):
            ps, nm = (psA, "A") if g % 2 == 0 else (psB, "B")
            tev(lambda e, g=g, ps=ps, slot=slot: e.matmul(
                ps[:, 0:512], V2[:, g * P:(g + 1) * P],
                xh[:, slot * D:slot * D + 512], start=True, stop=True),
                W(("v", vxh), pg[nm]))
            TG2 = tev(lambda e, g=g, ps=ps, slot=slot: e.matmul(
                ps[:, 512:1024], V2[:, g * P:(g + 1) * P],
                xh[:, slot * D + 512:(slot + 1) * D], start=True, stop=True))
            SG = sev(lambda e, g=g, ps=ps: e.copy(G2C[:, g * D:(g + 1) * D], ps[:]),
                     W(("t", TG2), G2C_g[0] if g == 0 else ("v", 0)), lag=3)
            pg[nm] = ("s", SG)
        SG2 = sch.now("s")
        for g in range(4):
            vev(lambda e, g=g: e.tensor_mul(
                G2C[:, g * D:(g + 1) * D], G2C[:, g * D:(g + 1) * D], Ct2[:]),
                W(("s", SG2)) if g == 0 else W(), lag=3)
        VG2C = sch.now("v")
        for hh in range(2):
            tev(lambda e, hh=hh, slot=slot: e.matmul(
                psA[0:N, hh * 512:(hh + 1) * 512], V1[:],
                xh[:, slot * D + hh * 512:slot * D + (hh + 1) * 512],
                start=True, stop=True), W(pg["A"]) if hh == 0 else W())
        TG1 = sch.now("t")
        xh_g[slot] = [("t", TG1), ("s_txh", SQXH)]
        if ci == 0:
            vev(lambda e: e.memset(h1[:], 0.0), W(h1_g[0]))
        vev(lambda e: e.tensor_mul(hC1[:], h1[:], CtT[:]), W(("t", TG1)),
            lag=0 if ci == 0 else 2)
        vev(lambda e: e.scalar_tensor_tensor(h1[:], h1[:], rS[:, 0:1], psA[0:N, :],
                                             ALU.mult, ALU.add))
        pg["A"] = ("v", sch.now("v"))
        VH = sch.now("v")
        for hh in range(2):
            tev(lambda e, hh=hh: e.matmul(
                psB[:, hh * 512:(hh + 1) * 512], W1Ts[:],
                hC1[:, hh * 512:(hh + 1) * 512], start=True, stop=False),
                W(("v", VH), pg["B"]) if hh == 0 else W())
        for g in range(4):
            for hh in range(2):
                tev(lambda e, g=g, hh=hh: e.matmul(
                    psB[:, hh * 512:(hh + 1) * 512], W2s[:, g * P:(g + 1) * P],
                    G2C[:, g * D + hh * 512:g * D + (hh + 1) * 512],
                    start=False, stop=(g == 3)),
                    W(("v", VG2C)) if (g == 0 and hh == 0) else W())
        TY = sch.now("t")
        G2C_g[0] = ("t", TY)
        h1_g[0] = ("t", TY)
        yg = y2p_g[qc] if isinstance(y2p_g[qc], list) else [y2p_g[qc]]
        SY2 = sev(lambda e, qc=qc: e.copy(y2p[:, qc * D:(qc + 1) * D], psB[:]),
                  W(("t", TY), *yg), lag=1)
        y2s[qc] = SY2
        pg["B"] = ("s", SY2)

    def taps(qtr_base_done):
        def tapap(buf, dc, k, hi):
            a = buf.ap()[:, dc * 512:(dc + 1) * 512].rearrange("p (g i) -> p g i", i=M)
            return a[:, :, k:hi]
        for dc in range(DCH):
            vev(lambda e, dc=dc: e.tensor_scalar(
                tapap(accb, dc, 0, M), tapap(xhT, dc, 0, M),
                KtT[:, dc * M:dc * M + 1], None, ALU.mult),
                W(intraT_g[0], intraT_g[1]) if dc == 0 else W(), lag=7)
        for k in range(1, M):
            for dc in range(DCH):
                vev(lambda e, k=k, dc=dc: e.scalar_tensor_tensor(
                    tapap(accb, dc, k, M), tapap(xhT, dc, 0, M - k),
                    KtT[:, dc * M + k:dc * M + k + 1],
                    tapap(accb, dc, k, M), ALU.mult, ALU.add), lag=7)
        VT = sch.now("v")
        xhT_g[0] = ("v", VT)
        return VT

    def stage_C(s, ci, VTAPS):
        nonlocal tmpb_g, xq1t_g
        qc = ci % 4
        isl = 0
        gi = s * CH + ci
        xs = gi % NX1
        qs = gi % NQT
        vgb = vev(lambda e, qc=qc: e.tensor_copy(
            tmpb.ap().rearrange("p (dc t) -> p dc t", dc=DCH),
            accb.ap().rearrange("p (dc t) -> p dc t", dc=DCH)[:, :, qc * P:(qc + 1) * P]),
            W(("v", VTAPS), tmpb_g, tmpa_g), lag=1)
        itg = [intraT_g[isl]]
        sch.ev("s", lambda e, isl=isl: e.dma_start_transpose(
            intraT.ap().rearrange("p (c t) -> p c t", c=DCH),
            tmpb.ap().rearrange("p (c t) -> p c t", c=DCH)),
            W(("v", vgb), *itg), inc="s_ti%d" % isl, n=16, lag=2)
        SQI = sch.now("s_ti%d" % isl)
        tmpb_g = ("s_ti%d" % isl, SQI)
        vy = vev(lambda e, qc=qc: e.tensor_add(
            y2t[:], y2p[:, qc * D:(qc + 1) * D], intraT[:]),
            W(("s_ti%d" % isl, SQI), ("s", y2s[qc]),
              *(xq1t_g if isinstance(xq1t_g, list) else [xq1t_g])), lag=2)
        y2p_g[qc] = ("v", vy)
        intraT_g[isl] = ("v", vy)
        if ci == 0:
            vev(lambda e: e.tensor_add(y2t[0:1, :], y2t[0:1, :], rrew[:]),
                W(("s_rr", 16)))
        vev(lambda e: e.bn_stats(stat6[:, 0:6], y2t[:, 0:512]))
        vev(lambda e: e.bn_stats(stat6[:, 6:12], y2t[:, 512:1024]), lag=1)
        vev(lambda e: e.bn_aggr(stat2[:], stat6[:]))
        sev(lambda e: e.activation(colm[:, 1:2], stat2[:, 1:2], AT.Sqrt,
                                   bias=ceps[:, 0:1]), W(("v", sch.now("v"))), lag=1)
        vev(lambda e: e.reciprocal(colm[:, 1:2], colm[:, 1:2]), W(("s", sch.now("s"))))
        vev(lambda e: e.tensor_scalar(y2t[:], y2t[:], stat2[:, 0:1], colm[:, 1:2],
                                      ALU.subtract, ALU.mult))
        vev(lambda e: e.tensor_mul(y2t[:], y2t[:], gRep[:]))
        vev(lambda e: e.tensor_add(y2t[:], y2t[:], bRep[:]))
        vev(lambda e, xs=xs: e.tensor_add(x1[:, xs * D:(xs + 1) * D],
                                          x1[:, xs * D:(xs + 1) * D], y2t[:]))
        if DEBUG:
            sch.ev("g", dma(dbg["x1"][gi * P:(gi + 1) * P, :],
                            x1[:, xs * D:(xs + 1) * D]),
                   W(("v", sch.now("v"))), inc="gq", n=16)
        vev(lambda e, xs=xs: e.bn_stats(stat6[:, 0:6], x1[:, xs * D:xs * D + 512]))
        vev(lambda e, xs=xs: e.bn_stats(stat6[:, 6:12],
                                        x1[:, xs * D + 512:(xs + 1) * D]), lag=1)
        vev(lambda e: e.bn_aggr(stat2[:], stat6[:]))
        sev(lambda e: e.activation(colm[:, 1:2], stat2[:, 1:2], AT.Sqrt,
                                   bias=ceps[:, 0:1]), W(("v", sch.now("v"))), lag=1)
        vev(lambda e: e.reciprocal(colm[:, 1:2], colm[:, 1:2]), W(("s", sch.now("s"))))
        vev(lambda e, xs=xs: e.tensor_scalar(y2t[:], x1[:, xs * D:(xs + 1) * D],
                                             stat2[:, 0:1], colm[:, 1:2],
                                             ALU.subtract, ALU.mult))
        vev(lambda e: e.reduce_max(colm[:, 2:3], y2t[:], AX.X,
                                   apply_absolute_value=True))
        vev(lambda e: e.tensor_scalar(colm[:, 2:3], colm[:, 2:3], 1e-5, None, ALU.max))
        vev(lambda e: e.reciprocal(colm[:, 3:4], colm[:, 2:3]))
        vev(lambda e: e.tensor_scalar(colm[:, 3:4], colm[:, 3:4], 127.0, None, ALU.mult))
        vev(lambda e: e.tensor_scalar(colm[:, 4:5], colm[:, 2:3], mreps[:, 2:3],
                                      None, ALU.mult))
        xqg = xq1t_g if isinstance(xq1t_g, list) else [xq1t_g]
        vev(lambda e: e.tensor_scalar(xq1t[:], y2t[:], colm[:, 3:4], 1536.0,
                                      ALU.mult, ALU.add), W(*xqg))
        vev(lambda e: e.tensor_scalar(xq1t[:], xq1t[:], 1536.0, None, ALU.subtract))
        vxq = vev(lambda e: e.tensor_scalar(y2t[:], xq1t[:], colm[:, 4:5], None,
                                            ALU.mult))
        prevw = (gi - NQT) // 4
        qwg = [("s_mmw", prevw + 1)] if gi >= NQT else []
        sch.ev("s", lambda e, qs=qs: e.dma_start_transpose(
            xq1T.ap()[:, qs * D:(qs + 1) * D].rearrange("p (c t) -> p c t", c=DCH),
            y2t.ap().rearrange("p (c t) -> p c t", c=DCH)),
            W(("v", vxq), *qwg), inc=NQTS[qs], n=16, lag=2)
        xq1_done[gi] = (NQTS[qs], sch.now(NQTS[qs]))
        xq1t_g = [(NQTS[qs], sch.now(NQTS[qs]))]

    def ssm_half(s, h):
        for q in range(2):
            for c4 in range(4):
                gi = s * CH + h * 8 + q * 4 + c4
                if gi >= NX1:
                    yield ("tails", gi - NX1 + 1)
                stage_A(s, h * 8 + q * 4 + c4)
                yield
            VT = taps(None)
            yield
            for c4 in range(4):
                stage_C(s, h * 8 + q * 4 + c4, VT)
                yield
            yield ("qdone", s * 4 + h * 2 + q + 1)

    def ffn_seq(s):
        nonlocal acc2_g, amb_g, ffT_g, s2t_g

        def conv_products(cch, SPS):
            b0 = cch * ST
            sl = cch % 2
            w_c = W(("s", SPS), acc2_g if cch == 0 else ("v", 0), ubuf_g[sl])
            vev(lambda e, b0=b0, sl=sl, cch=cch: e.tensor_scalar(
                ubuf[:, sl * 512:sl * 512 + 512], hsl[:, b0:b0 + 512],
                wcv[:, cch * 4 + 0:cch * 4 + 1], None, ALU.mult), w_c)
            vev(lambda e, b0=b0, sl=sl, cch=cch: e.tensor_scalar(
                vbuf[:, sl * 512:sl * 512 + 512], hsl[:, b0 + 2:b0 + 514],
                wcv[:, cch * 4 + 2:cch * 4 + 3], None, ALU.mult), lag=1)
            cg = vev(lambda e, b0=b0, sl=sl, cch=cch: e.tensor_scalar(
                cbuf[:, sl * 512:sl * 512 + 512], hsl[:, b0 + 1:b0 + 513],
                wcv[:, cch * 4 + 1:cch * 4 + 2], wcv[:, cch * 4 + 3:cch * 4 + 4],
                ALU.mult, ALU.add), lag=2)
            conv_g[cch] = ("v", cg)
            return cg

        def conv_mms(cch, ps, evS, cg):
            a0 = cch * WIN
            sl = cch % 2
            tev(lambda e, ps=ps, sl=sl: e.matmul(
                ps[:], bIb[:], cbuf[:, sl * 512:sl * 512 + 512],
                start=True, stop=False), W(("v", cg), ("s", evS)))
            tev(lambda e, ps=ps, sl=sl: e.matmul(
                ps[:], bIb[:], ubuf[:, sl * 512:sl * 512 + 512],
                start=False, stop=False))
            TCV = tev(lambda e, ps=ps, sl=sl: e.matmul(
                ps[:], bIb[:], vbuf[:, sl * 512:sl * 512 + 512],
                start=False, stop=True))
            ubuf_g[sl] = ("t", TCV)
            SSI = sev(lambda e, ps=ps, a0=a0: e.activation(
                acc2[:, a0:a0 + WIN], ps[:], AT.Silu), W(("t", TCV)), lag=1)
            sch.mark("SSI s=%d cch=%d" % (s, cch))
            if DEBUG and DBGPROBE is not None and (s, cch) == DBGPROBE[:2]:
                if DBGPROBE[2] == probe_hits[0]:
                    sch.ev("sy", dma(dbg["probe"][:, :], acc2[:, a0:a0 + WIN]),
                           W(("s", SSI)), inc="dbq", n=16)
                probe_hits[0] += 1
            return SSI

        def mm1_cch(win, cch):
            sch.mark("MM1 s=%d win=%d cch=%d" % (s, win, cch))
            rh = cch % 3
            extra = PREPW if first_ring[0] else []
            first_ring[0] = False
            sch.ev("sy", dma(
                w1r.ap()[:, rh * 8 * P:(rh + 1) * 8 * P].rearrange(
                    "p (b c) -> p b c", b=DCH),
                w1q8[cch].rearrange("b p c -> p b c")),
                W(w1r_g[rh], *extra), inc="s_w1%d" % rh, n=16)
            SYW = sch.now("s_w1%d" % rh)
            ps, nm = (psM1a, "M1a") if cch % 2 == 0 else (psM1b, "M1b")
            base = ((s * CH + win * 4) % NQT)
            qw = [xq1_done[s * CH + win * 4 + j] for j in range(4)]
            for dc in range(DCH):
                mvap = xq1T.ap().rearrange("p (qs dc t) -> p qs dc t", qs=NQT,
                                           dc=DCH)[:, base:base + 4, dc, :]
                tev(lambda e, ps=ps, rh=rh, dc=dc, mvap=mvap: e.matmul(
                    ps[:], w1r[:, (rh * 8 + dc) * P:(rh * 8 + dc + 1) * P],
                    mvap, start=(dc == 0), stop=(dc == DCH - 1)),
                    W(("s_w1%d" % rh, SYW), *qw, pg[nm]) if dc == 0 else W())
            TMM = sch.now("t")
            w1r_g[rh] = ("t", TMM)
            if cch == HCH - 1:
                sch.ev("t", lambda e: e.nop(), W(), inc="s_mmw", n=1)
            b0 = cch * ST
            if win > 0:
                SPS = sev(lambda e, ps=ps, b0=b0: e.copy(hsl[:, b0 + 513:b0 + 514],
                                                         ps[:, 0:1]), W(("t", TMM)),
                          lag=1)
                cg = conv_products(cch, SPS)
                sev(lambda e, b0=b0: e.copy(hsl[:, b0:b0 + 1],
                                            hsl[:, b0 + 512:b0 + 513]),
                    W(("v", cg)))
                SE = sev(lambda e, ps=ps, b0=b0: e.copy(hsl[:, b0 + 1:b0 + 513],
                                                        ps[:]), W(("v", cg)))
                pg[nm] = ("s", conv_mms(cch, ps, SE, cg))
            else:
                cgl = conv_g[cch] if isinstance(conv_g[cch], list) else [conv_g[cch]]
                SE = sev(lambda e, ps=ps, b0=b0: e.copy(hsl[:, b0 + 1:b0 + 513],
                                                        ps[:]), W(("t", TMM), *cgl))
                pg[nm] = ("s", SE)

        def finish_window(win):
            nonlocal amb_g, ffT_g, acc2_g, s2t_g
            t0 = win * WIN
            SS = sch.now("s")
            sch.mark("FW s=%d win=%d SS=%d" % (s, win, SS))
            if DEBUG and (s, win) == DBGWIN:
                sch.ev("sy", dma(dbg["acc2"][:, :], acc2[:]), W(("s", SS)),
                       inc="dbq", n=16)
                sch.ev("sy", dma(dbg["hsl"][:, :], hsl[:]), W(("s", SS)),
                       inc="dbq", n=16)
            vev(lambda e: e.tensor_tensor(maxt[:], acc2[:, 0:WIN], acc2[:, WIN:2 * WIN],
                                          ALU.max), W(("s", SS), amb_g))
            vev(lambda e: e.tensor_tensor(mint[:], acc2[:, 0:WIN], acc2[:, WIN:2 * WIN],
                                          ALU.min), lag=1)
            for cch in range(2, HCH):
                vev(lambda e, cch=cch: e.tensor_tensor(
                    maxt[:], maxt[:], acc2[:, cch * WIN:(cch + 1) * WIN], ALU.max),
                    lag=1)
                vev(lambda e, cch=cch: e.tensor_tensor(
                    mint[:], mint[:], acc2[:, cch * WIN:(cch + 1) * WIN], ALU.min),
                    lag=1)
            yield
            vev(lambda e: e.tensor_scalar(mint[:], mint[:], -1.0, None, ALU.mult))
            vam = vev(lambda e: e.tensor_tensor(mint[:], maxt[:], mint[:], ALU.max),
                      W(*(amb_g if isinstance(amb_g, list) else [amb_g])))
            sch.ev("s", lambda e: e.dma_start_transpose(
                amT.ap().rearrange("p (b c) -> p b c", b=4),
                mint.ap().rearrange("p (b c) -> p b c", b=4)),
                W(("v", vam)), inc="s_tam", n=16, lag=2)
            SQAM = sch.now("s_tam")
            amb_g = ("s_tam", SQAM)
            for tt in range(4):
                vev(lambda e, tt=tt: e.reduce_max(s2col[:, tt:tt + 1],
                                                  amT[:, tt * P:(tt + 1) * P], AX.X),
                    W(("s_tam", SQAM), s2t_g) if tt == 0 else W(),
                    lag=0 if tt == 0 else 1)
                vev(lambda e, tt=tt: e.tensor_scalar(s2col[:, tt:tt + 1],
                                                     s2col[:, tt:tt + 1], 1e-5, None,
                                                     ALU.max))
                vev(lambda e, tt=tt: e.tensor_scalar(s2cb[:, tt:tt + 1],
                                                     s2col[:, tt:tt + 1], mreps[:, 3:4],
                                                     None, ALU.mult))
                vev(lambda e, tt=tt: e.reciprocal(s2col[:, tt:tt + 1],
                                                  s2col[:, tt:tt + 1]))
                vev(lambda e, tt=tt: e.tensor_scalar(s2col[:, tt:tt + 1],
                                                     s2col[:, tt:tt + 1], 127.0, None,
                                                     ALU.mult))
            VS2 = sch.now("v")
            if DEBUG and (s, win) == DBGWIN:
                sch.ev("sy", dma(dbg["scol"][:, 0:4], s2col[:]), W(("v", VS2)),
                       inc="dbq", n=16)
                sch.ev("sy", dma(dbg["scol"][:, 4:8], s2cb[:]), W(("v", VS2)),
                       inc="dbq", n=16)
            for tt in range(4):
                tev(lambda e, tt=tt: e.transpose(psM2a[0:1, tt * P:(tt + 1) * P],
                                                 s2col[:, tt:tt + 1], bI[:]),
                    W(("v", VS2), pg["M2a"]) if tt == 0 else W())
            s2t_g = ("t", sch.now("t"))
            vev(lambda e: e.tensor_copy(srow[:], psM2a[0:1, 0:WIN]),
                W(("t", sch.now("t"))))
            pg["M2a"] = ("v", sch.now("v"))
            tev(lambda e: e.matmul(psM2b[:], bONE[0:1, :], srow[:], start=True,
                                   stop=True), W(("v", sch.now("v")), pg["M2b"]))
            vev(lambda e: e.tensor_copy(s2rep[:], psM2b[:]), W(("t", sch.now("t"))))
            pg["M2b"] = ("v", sch.now("v"))
            VREP = sch.now("v")
            bcast = bass.AP(tensor=s2rep.ap().tensor, offset=s2rep.ap().offset,
                            ap=[list(s2rep.ap().ap[0]), [0, HCH], [1, WIN]])
            vev(lambda e, bcast=bcast: e.tensor_tensor(
                acc2.ap().rearrange("p (c t) -> p c t", c=HCH),
                acc2.ap().rearrange("p (c t) -> p c t", c=HCH), bcast, ALU.mult),
                W(("v", VREP)))
            VQ2 = vev(lambda e: e.tensor_scalar(acc2[:], acc2[:], MAGIC, MAGIC,
                                                ALU.add, ALU.subtract))
            yield
            def w2load(hh):
                rh = hh % 3
                sch.ev("sy", dma(
                    w2r.ap()[:, rh * 16 * P:(rh + 1) * 16 * P].rearrange(
                        "p (a c) -> p a c", a=16),
                    w2q8[hh // 2, (hh % 2) * 16:(hh % 2) * 16 + 16].rearrange(
                        "a p c -> p a c")),
                    W(w2r_g[rh], *PREPW), inc="s_w2%d" % rh, n=16)
            w2load(0)
            w2load(1)
            w2load(2)
            for dc in range(DCH):
                rhA, rhB = (2 * dc) % 3, (2 * dc + 1) % 3
                GQA = sch.now("s_w2%d" % rhA)
                GQB = sch.now("s_w2%d" % rhB)
                ps, nm = (psM2a, "M2a") if dc % 2 == 0 else (psM2b, "M2b")
                for cch in range(HCH):
                    rh = rhA if cch < 16 else rhB
                    ti = cch % 16
                    tev(lambda e, ps=ps, rh=rh, ti=ti, cch=cch: e.matmul(
                        ps[:], w2r[:, (rh * 16 + ti) * P:(rh * 16 + ti + 1) * P],
                        acc2[:, cch * WIN:(cch + 1) * WIN],
                        start=(cch == 0), stop=(cch == HCH - 1)),
                        W(("s_w2%d" % rhA, GQA), ("s_w2%d" % rhB, GQB),
                          ("v", VQ2), pg[nm]) if cch == 0 else W())
                TM2 = sch.now("t")
                w2r_g[rhA] = ("t", TM2)
                w2r_g[rhB] = ("t", TM2)
                if 2 * dc + 3 < 2 * DCH:
                    w2load(2 * dc + 3)
                if 2 * dc + 4 < 2 * DCH:
                    w2load(2 * dc + 4)
                fsl = dc % 2
                SF = sev(lambda e, ps=ps, fsl=fsl: e.copy(
                    ffw[:, fsl * WIN:(fsl + 1) * WIN], ps[:]),
                    W(("t", TM2), ffw_g[fsl]))
                pg[nm] = ("s", SF)
                sch.ev("s", lambda e, dc=dc, fsl=fsl: e.dma_start_transpose(
                    ffT.ap()[:, dc * WIN:(dc + 1) * WIN].rearrange("p (b c) -> p b c",
                                                                   b=4),
                    ffw.ap()[:, fsl * WIN:(fsl + 1) * WIN].rearrange("p (b c) -> p b c",
                                                                     b=4)),
                    W(("s", SF), ffT_g if dc == 0 else ("v", 0)),
                    inc="s_tf%d" % fsl, n=16, lag=2)
                ffw_g[fsl] = ("s_tf%d" % fsl, sch.now("s_tf%d" % fsl))
                if dc % 4 == 3:
                    yield
            SQF = None
            QFW = [("s_tf0", sch.now("s_tf0")), ("s_tf1", sch.now("s_tf1"))]
            if DEBUG and (s, win) == DBGWIN:
                sch.ev("sy", dma(dbg["ffw"][:, :], ffT[:]), W(*QFW), inc="dbq", n=16)
            for tt in range(4):
                gi = s * CH + win * 4 + tt
                xs = gi % NX1
                osl = (win * 4 + tt) % 2
                ftap = ffT.ap().rearrange("p (dc tt t) -> p dc tt t", dc=DCH, tt=4)[
                    :, :, tt, :]
                vo = vev(lambda e, tt=tt, xs=xs, osl=osl, ftap=ftap: e.scalar_tensor_tensor(
                    outt[:, osl * D:(osl + 1) * D], ftap,
                    s2cb[:, tt:tt + 1], x1[:, xs * D:(xs + 1) * D], ALU.mult, ALU.add),
                    W(*QFW, outt_g[osl]), lag=1)
                sch.ev("s", lambda e, s=s, t0=t0, tt=tt, osl=osl: e.dma_start(
                    out=out[s, t0 + tt * P:t0 + (tt + 1) * P, :],
                    in_=outt[:, osl * D:(osl + 1) * D]), W(("v", vo)),
                    inc="s_o%d" % osl, n=16, lag=2)
                outt_g[osl] = ("s_o%d" % osl, sch.now("s_o%d" % osl))
                x1_g[xs] = [("s_o%d" % osl, sch.now("s_o%d" % osl))]
                tails_cnt[0] += 1
            ffT_g = ("v", sch.now("v"))
            acc2_g = ("t", sch.now("t"))

        for w in range(NW):
            yield ("q", s * 4 + w)
            if w == 0 and s == 0:
                vev(lambda e: e.memset(
                    hsl.ap().rearrange("p (c k) -> p c k", k=ST)[:, :, 0:1], 0.0),
                    W(*PREPW))
            elif w == 0:
                vev(lambda e: e.memset(
                    hsl.ap().rearrange("p (c k) -> p c k", k=ST)[:, :, 0:1], 0.0),
                    W(*(conv_g[HCH - 1] if isinstance(conv_g[HCH - 1], list)
                        else [conv_g[HCH - 1]])))
            for cch in range(HCH):
                mm1_cch(w, cch)
                if cch % 4 == 3:
                    yield
            if w >= 1:
                for _ in finish_window(w - 1):
                    yield
        vev(lambda e: e.memset(
            hsl.ap().rearrange("p (c k) -> p c k", k=ST)[:, :, 513:514], 0.0),
            W(("s", sch.now("s"))))
        for cch in range(HCH):
            ps, nm = (psM1a, "M1a") if cch % 2 == 0 else (psM1b, "M1b")
            cg = conv_products(cch, 0)
            pg[nm] = ("s", conv_mms(cch, ps, pg[nm][1] if pg[nm][0] == "s" else 0, cg))
            if cch % 8 == 7:
                yield
        for _ in finish_window(NW - 1):
            yield

    ssm_gens = [(s * 2 + h, ssm_half(s, h)) for s in range(BC) for h in range(2)]
    fgens = [ffn_seq(s) for s in range(BC)]
    fneed = []
    for g in fgens:
        r = next(g)
        fneed.append(r[1] if isinstance(r, tuple) else None)
    quarters_done = 0
    fcur = 0
    sidx = 0
    sreq = [None]

    def ffn_tick():
        nonlocal fcur
        if fcur >= BC:
            return False
        if fneed[fcur] is not None and fneed[fcur] >= quarters_done:
            return False
        try:
            r = next(fgens[fcur])
            fneed[fcur] = r[1] if isinstance(r, tuple) else None
        except StopIteration:
            fcur += 1
        return True

    def ssm_tick():
        nonlocal sidx, quarters_done
        if sidx >= len(ssm_gens):
            return False
        if sreq[0] is not None and tails_cnt[0] < sreq[0]:
            return False
        hid, g = ssm_gens[sidx]
        try:
            r = next(g)
            if isinstance(r, tuple):
                if r[0] == "qdone":
                    quarters_done = max(quarters_done, r[1])
                    sreq[0] = None
                else:
                    sreq[0] = r[1]
            else:
                sreq[0] = None
        except StopIteration:
            sidx += 1
            sreq[0] = None
        return True

    while sidx < len(ssm_gens) or fcur < BC:
        p1 = ssm_tick()
        p2 = ffn_tick()
        p4 = ssm_tick()
        p3 = ffn_tick()
        if not (p1 or p2 or p3 or p4):
            raise RuntimeError("pipeline deadlock: sidx=%d sreq=%s fcur=%d fneed=%s "
                               "quarters=%d tails=%d" % (sidx, sreq[0], fcur,
                                                         fneed, quarters_done,
                                                         tails_cnt[0]))

    sch.ev("g", lambda e: e.engine_nop(),
           W(("s_o0", sch.now("s_o0")), ("s_o1", sch.now("s_o1"))))

    sem_names = sorted(set(list(sch.cnt.keys()) +
                           ["v", "s", "t"] +
                           [w[0] for (_e, ws, _f, _i, _n, _l) in sch.events for w in ws]))
    import contextlib
    with contextlib.ExitStack() as stack:
        sems = {nm: stack.enter_context(nc.semaphore(nm)) for nm in sem_names}
        blk = stack.enter_context(nc.Block())

        @blk.sync
        def _(e):
            sch.emit("sy", e, sems)

        @blk.vector
        def _(e):
            sch.emit("v", e, sems, delay=(SERIAL == 11))

        @blk.scalar
        def _(e):
            sch.emit("s", e, sems, delay=(SERIAL == 11))

        @blk.tensor
        def _(e):
            sch.emit("t", e, sems)

        @blk.gpsimd
        def _(e):
            sch.emit("g", e, sems)

    globals()['_last_sch'] = sch
    return nc


def kernel(L0_attn_u, L0_attn_A_log, L0_attn_B, L0_attn_C, L0_attn_D,
           L0_attn_delta, L0_attn_gamma, L0_attn_beta, L0_attn_freq_align,
           L0_attn_reward, L0_ffn_w1, L0_ffn_conv_w, L0_ffn_conv_b, L0_ffn_w2):
    hc = host_consts()
    nc = build_program()
    f32 = lambda a: np.ascontiguousarray(np.asarray(a, dtype=np.float32))
    base = {
        "w1T": f32(np.asarray(L0_ffn_w1).T),
        "w2T": f32(np.asarray(L0_ffn_w2).T),
        "CT": f32(np.asarray(L0_attn_C).T),
        "alog": f32(L0_attn_A_log), "delta": f32(L0_attn_delta),
        "Dvec": f32(L0_attn_D), "fa": f32(L0_attn_freq_align),
        "gamma": f32(L0_attn_gamma), "beta": f32(L0_attn_beta),
        "reward": f32(L0_attn_reward),
        "convw": f32(np.asarray(L0_ffn_conv_w).reshape(HC, 3)),
        "convb": f32(L0_ffn_conv_b),
    }
    base.update(hc)
    uu = f32(L0_attn_u)
    in_maps = []
    for c in range(8):
        m = dict(base)
        m["u"] = np.ascontiguousarray(uu[2 * c:2 * c + 2])
        in_maps.append(m)
    res = run_bass_kernel_spmd(nc, in_maps, core_ids=list(range(8)))
    return np.concatenate([r["out"] for r in res.results], axis=0)
